# version 11
# baseline (speedup 1.0000x reference)
import contextlib

import numpy as np

import concourse.bass as bass
import concourse.mybir as mybir
from concourse.bass_utils import run_bass_kernel_spmd

F32 = mybir.dt.float32
BF16 = mybir.dt.bfloat16
I32 = mybir.dt.int32
AF = mybir.ActivationFunctionType
ALU = mybir.AluOpType
AXX = mybir.AxisListType.X

NCORE = 8
BSEQ = 2
L = 2048
NT = BSEQ * L
D = 1024
DI = 2048
N0 = 16
N1 = 64
R0 = 64
F1 = 4096
TC = 512
NCH = L // TC
DT = D // 128
DIT = DI // 128
F1T = F1 // 128

ENGS = ("pe", "act", "dve", "pool", "sp")
ENG2SEC = {"pe": "tensor", "act": "scalar", "dve": "vector", "pool": "gpsimd",
           "sp": "sync"}
ESEM = {e: "s_" + e for e in ENGS}


class Buf:
    __slots__ = ("ap", "writers", "readers", "name")

    def __init__(self, ap, name=""):
        self.ap = ap
        self.writers = []
        self.readers = []
        self.name = name


class Op:
    __slots__ = ("eng", "fn", "deps", "seq", "queue", "fence", "is_fence")


QENG = {"d_sp": "sp", "d_pool": "pool", "d_act": "act"}


class Plan:

    def __init__(self, nc):
        self.nc = nc
        self.ops = {e: [] for e in ENGS}
        self.ntok = {e: 0 for e in ENGS}
        self.qgroup = {q: [] for q in QENG}
        self.nfence = {q: 0 for q in QENG}
        self.qlastfence = {q: None for q in QENG}
        self.qmap = {"sp": "sp", "pool": "pool"}
        self.nop_sem = None
        self.guards = {}
        self.cur_guard = ()

    def _resolve(self, d):
        if isinstance(d, tuple) or not getattr(d, "queue", None):
            return d
        if d.fence is None:
            self.fence(d.queue)
        return d.fence

    def snapshot(self):
        return [ops[-1] for ops in self.ops.values() if ops]

    def begin_stage(self, side, use_psum=True):
        g = list(self.guards.get(side, ()))
        if use_psum:
            g += list(self.guards.get("psum", ()))
        self.cur_guard = tuple(g)
        self._stage_psum = use_psum

    def end_stage(self, side):
        snap = self.snapshot()
        self.guards[side] = snap
        if self._stage_psum:
            self.guards["psum"] = snap
        self.cur_guard = ()

    def end_stage_tails(self, side, tails, use_psum):
        self.guards[side] = list(tails.values())
        if use_psum:
            self.guards["psum"] = list(self.guards.get("psum", ())) \
                if False else list(tails.values())
        self.cur_guard = ()

    def op(self, eng, fn, deps=(), reads=(), writes=(), rw=(), queue=None,
           is_fence=False):
        dd = []
        for d in deps:
            if d is not None:
                dd.append(d)
        if not is_fence:
            dd.extend(self.cur_guard)
        for b in reads:
            dd.extend(b.writers)
        for b in rw:
            dd.extend(b.writers)
            dd.extend(b.readers)
        for b in writes:
            dd.extend(b.writers)
            dd.extend(b.readers)
        if not is_fence:
            dd = [self._resolve(d) for d in dd]
        o = Op()
        o.eng = eng
        o.fn = fn
        o.deps = dd
        o.queue = queue
        o.fence = None
        o.is_fence = is_fence
        if queue is None and not is_fence:
            self.ntok[eng] += 1
            o.seq = self.ntok[eng]
        else:
            o.seq = None
        self.ops[eng].append(o)
        if queue is not None:
            self.qgroup[queue].append(o)
        for b in reads:
            b.readers.append(o)
        for b in rw:
            b.writers = [o]
            b.readers = []
        for b in writes:
            b.writers = [o]
            b.readers = []
        return o

    def fence(self, queue):
        grp = self.qgroup[queue]
        if not grp:
            return self.qlastfence[queue]
        self.qgroup[queue] = []
        f = self.op(QENG[queue], ("FENCE", queue), is_fence=True)
        self.nfence[queue] += 1
        f.seq = ("f_" + queue[2:], self.nfence[queue])
        for d in grp:
            d.fence = f
        self.qlastfence[queue] = f
        return f

    def dma(self, eng, dst, src, reads=(), writes=(), rw=(), deps=(), **kw):
        if "accum_op" in kw:
            eng = "pool"
        else:
            eng = self.qmap.get(eng, eng)
        q = "d_" + eng
        dap = dst.ap if isinstance(dst, Buf) else dst
        sap = src.ap if isinstance(src, Buf) else src
        rd = list(reads) + ([src] if isinstance(src, Buf) else [])
        wr = list(writes) + ([dst] if isinstance(dst, Buf) and not rw else [])
        return self.op(eng, lambda h: h.dma_start(out=dap, in_=sap, **kw),
                       deps=deps, reads=rd, writes=wr, rw=rw, queue=q)

    def barrier(self):
        for q in QENG:
            self.fence(q)
        tails = [ops[-1] for ops in self.ops.values() if ops]
        bar = []
        for e in ENGS:
            bar.append(self.op(e, None, deps=tails))
        tails2 = list(bar)
        for e in ENGS:
            self.op(e, None, deps=tails2)

    def token(self, o):
        if isinstance(o, tuple):
            return o
        assert o.seq is not None, "dma token must resolve through its fence"
        if isinstance(o.seq, tuple):
            return o.seq
        return (ESEM[o.eng], o.seq)

    def emit(self, ctx):
        qtotal = {q: [0] for q in QENG}
        for eng in ENGS:
            ops = self.ops[eng]
            if not ops:
                continue

            def body(h, eng=eng, ops=ops):
                hw = ctx.waited[eng]
                own = ESEM[eng]
                for o in ops:
                    need = {}
                    for d in o.deps:
                        s, v = self.token(d)
                        if need.get(s, 0) < v:
                            need[s] = v
                    for s in sorted(need):
                        v = need[s]
                        if hw.get(s, 0) >= v:
                            continue
                        h.wait_ge(ctx.sem[s], v)
                        hw[s] = v
                    if isinstance(o.fn, tuple):
                        q = o.fn[1]
                        h.wait_ge(ctx.sem[q], qtotal[q][0])
                        h.sem_inc(ctx.sem[o.seq[0]], 1)
                    elif o.fn is None:
                        h.sem_inc(ctx.sem[own], 1)
                    else:
                        inst = o.fn(h)
                        if o.queue is not None:
                            inst.then_inc(ctx.sem[o.queue], 16)
                            qtotal[o.queue][0] += 16
                        else:
                            inst.then_inc(ctx.sem[own], 1)
            getattr(ctx.block, ENG2SEC[eng])(body)


class Ctx:
    def __init__(self, nc, block):
        self.nc = nc
        self.block = block
        self.sem = {}
        for name in ([ESEM[e] for e in ENGS] + ["d_sp", "d_pool", "d_act",
                     "f_sp", "f_pool", "f_act"]):
            self.sem[name] = nc.semaphore(name).__enter__()
        self.waited = {e: {} for e in ENGS}



class StageRun:

    def __init__(self, plan, side, genfn, nslice, cost):
        self.plan = plan
        self.side = side
        self.guard = tuple(plan.guards.get(side, ()))
        self.gen = genfn()
        self.tails = {}
        self.done = False
        self.nslice = max(nslice, 1)
        self.cps = cost / max(nslice, 1)

    def resume(self):
        p = self.plan
        _CUR_SIDE[0] = self.side
        if self.side == "right":
            p.qmap = {"sp": "act", "pool": "act"}
        else:
            p.qmap = {"sp": "sp", "pool": "pool"}
        p.cur_guard = self.guard
        before = {e: len(p.ops[e]) for e in ENGS}
        try:
            next(self.gen)
        except StopIteration:
            self.done = True
        for e in ENGS:
            if len(p.ops[e]) > before[e]:
                self.tails[e] = p.ops[e][-1]
        p.cur_guard = ()
        p.qmap = {"sp": "sp", "pool": "pool"}
        _CUR_SIDE[0] = "left"

    def finish(self):
        self.plan.guards[self.side] = list(self.tails.values())


def lane_exec(plan, lanes):
    state = [{"q": list(lane), "cur": None, "acc": 0.0} for lane in lanes]
    while True:
        cands = [s for s in state if s["cur"] is not None or s["q"]]
        if not cands:
            break
        s = min(cands, key=lambda s: s["acc"])
        if s["cur"] is None:
            side, nslice, cost, genfn = s["q"].pop(0)
            import os
            if os.environ.get("KPRINT"):
                print("stage", side, genfn, "sbuf_free",
                      plan.nc.sbuf_bytes_remaining)
            s["cur"] = StageRun(plan, side, genfn, nslice, cost)
        r = s["cur"]
        r.resume()
        s["acc"] += r.cps
        if r.done:
            r.finish()
            s["cur"] = None



def build_program(debug=(), upto=10 ** 9):
    nc = bass.Bass("TRN2", target_bir_lowering=False)
    dbg = set(debug)

    def din(name, shape, dtype=F32):
        return nc.dram_tensor(name, shape, dtype, kind="ExternalInput").ap()

    def dscr(name, shape, dtype=F32):
        kind = "ExternalOutput" if name in dbg else "Internal"
        return nc.dram_tensor(name, shape, dtype, kind=kind).ap()

    io = {}
    io["x0"] = din("x0", [D, NT])
    io["ln_g"] = din("ln_g", [1, D])
    io["ln_b"] = din("ln_b", [1, D])
    for d in ("f", "b"):
        io[f"{d}_inT"] = din(f"{d}_inT", [D, 2 * DI])
        io[f"{d}_convw"] = din(f"{d}_convw", [128, DIT * 4])
        io[f"{d}_convb"] = din(f"{d}_convb", [128, DIT])
        io[f"{d}_xwT"] = din(f"{d}_xwT", [DI, R0 + 2 * N0])
        io[f"{d}_dtwT"] = din(f"{d}_dtwT", [R0, DI])
        io[f"{d}_dtb"] = din(f"{d}_dtb", [128, DIT])
        io[f"{d}_alog"] = din(f"{d}_alog", [DI, N0])
        io[f"{d}_dp"] = din(f"{d}_dp", [128, DIT])
        io[f"{d}_outT"] = din(f"{d}_outT", [DI, D])
    io["combT"] = din("combT", [2 * D, D])
    io["combb"] = din("combb", [128, DT])
    io["ffn_k"] = din("ffn_k", [D, D])
    io["ffn_bias"] = din("ffn_bias", [128, DT])
    io["ffn_a"] = din("ffn_a", [D, 16])
    io["ffn_b"] = din("ffn_b", [16, D])
    io["in1T"] = din("in1T", [D, 4288])
    io["conv1w"] = din("conv1w", [128, DIT * 4])
    io["conv1b"] = din("conv1b", [128, DIT])
    io["alog1b"] = din("alog1b", [128, N1])
    io["dp1"] = din("dp1", [128, DIT])
    io["dt1T"] = din("dt1T", [R0, DI])
    io["dt1b"] = din("dt1b", [128, DIT])
    io["out1T"] = din("out1T", [DI, D])
    io["w1f"] = din("w1f", [F1, D])
    io["w1T"] = din("w1T", [D, F1])
    io["b1"] = din("b1", [128, F1T])
    io["w2f"] = din("w2f", [D, F1])
    io["w2T"] = din("w2T", [F1, D])
    io["b2"] = din("b2", [128, DT])
    io["ident"] = din("ident", [128, 128])

    io["xout"] = nc.dram_tensor("xout", [D, NT], F32, kind="ExternalOutput").ap()

    io["xn_s"] = dscr("xn_s", [D, NT])
    for d in ("f", "b"):
        io[f"xi_{d}_s"] = dscr(f"xi_{d}_s", [DI, NT])
        io[f"z_{d}_s"] = dscr(f"z_{d}_s", [DI, NT])
        io[f"xc_{d}_s"] = dscr(f"xc_{d}_s", [DI, NT])
        io[f"dl_{d}_s"] = dscr(f"dl_{d}_s", [DI, NT])
        io[f"y_{d}_s"] = dscr(f"y_{d}_s", [DI, NT])
        io[f"bc_{d}_s"] = dscr(f"bc_{d}_s", [2 * N0, NT])
        io[f"dtr_{d}_s"] = dscr(f"dtr_{d}_s", [R0, NT])
        io[f"o_{d}_s"] = dscr(f"o_{d}_s", [D, NT])
    io["x1_s"] = dscr("x1_s", [D, NT])
    io["x2_s"] = dscr("x2_s", [D, NT])
    io["x3_s"] = dscr("x3_s", [D, NT])
    io["wq0_s"] = dscr("wq0_s", [D, D])
    io["xi1_s"] = dscr("xi1_s", [DI, NT])
    io["z1_s"] = dscr("z1_s", [DI, NT])
    io["xc1_s"] = dscr("xc1_s", [DI, NT])
    io["dl1_s"] = dscr("dl1_s", [DI, NT])
    io["y1_s"] = dscr("y1_s", [DI, NT])
    io["bc1_s"] = dscr("bc1_s", [2 * N1, NT])
    io["dtr1_s"] = dscr("dtr1_s", [R0, NT])
    io["wq1T_s"] = dscr("wq1T_s", [D, F1], BF16)
    io["wq2T_s"] = dscr("wq2T_s", [F1, D], BF16)
    io["s1_s"] = dscr("s1_s", [F1, 1])
    io["s2_s"] = dscr("s2_s", [D, 1])
    io["h_s"] = dscr("h_s", [F1, NT])
    for tag, n in (("1", F1T), ("2", DT)):
        io[f"qd{tag}_sin"] = dscr(f"qd{tag}_sin", [128, 1])
        io[f"qd{tag}_max"] = dscr(f"qd{tag}_max", [128, 1])
        io[f"qd{tag}_se"] = dscr(f"qd{tag}_se", [128, n])
        io[f"qd{tag}_be"] = dscr(f"qd{tag}_be", [128, n])
    io["cc_in"] = nc.dram_tensor("cc_in", [1, 1], F32, kind="Internal").ap()
    io["cc_out"] = nc.dram_tensor("cc_out", [1, 1], F32, kind="Internal").ap()

    DB = {k: Buf(v, k) for k, v in io.items()}

    P = {}

    def psb(name, shape, dtype=F32):
        P[name] = Buf(nc.alloc_sbuf_tensor("p_" + name, shape, dtype).ap(), name)
        return P[name]

    psb("onesc", [1, 128])
    psb("onesr", [1, TC])
    psb("ones1", [128, 1])
    psb("ident", [128, 128])
    psb("A0f", [128, DIT * N0])
    psb("A0b", [128, DIT * N0])
    psb("A1", [128, N1])
    psb("sw1", [128, F1T])
    psb("sw2", [128, DT])
    psb("b1", [128, F1T])
    psb("b2", [128, DT])
    psb("rcp1", [128, 1])
    psb("seff1", [128, F1T])
    psb("beff1", [128, F1T])
    psb("rcp2", [128, 1])
    psb("seff2", [128, DT])
    psb("beff2", [128, DT])
    psb("gmx", [128, 1])
    psb("ghx", [128, 1])
    for j in range(4):
        P[f"pb{j}"] = Buf(nc.alloc_psum_tensor(f"p_pb{j}", [128, TC], F32).ap(),
                          f"pb{j}")
    P["psx"] = Buf(nc.alloc_psum_tensor("p_psx", [1, TC], F32).ap(), "psx")
    P["psq"] = Buf(nc.alloc_psum_tensor("p_psq", [1, TC], F32).ap(), "psq")
    P["par"] = Buf(nc.alloc_psum_tensor("p_par", [128, TC], F32).ap(), "par")
    P["pbr"] = Buf(nc.alloc_psum_tensor("p_pbr", [128, TC], F32).ap(), "pbr")

    with nc.Block() as block:
        ctx = Ctx(nc, block)
        plan = Plan(nc)
        plan.nop_sem = nc.semaphore("nop").__enter__()
        emit_all(plan, nc, io, DB, P, upto=upto)
        plan.emit(ctx)
    return nc


_UNIQ = [0]
_CUR_SIDE = ["left"]


def pbanks(P):
    if _CUR_SIDE[0] == "right":
        return [P["pb2"], P["pb3"]]
    return [P["pb0"], P["pb1"]]


def sb(nc, st, name, shape, dtype=F32):
    _UNIQ[0] += 1
    return Buf(st.enter_context(
        nc.sbuf_tensor(f"{name}_{_UNIQ[0]}", shape, dtype,
                       side=_CUR_SIDE[0])).ap(), name)


def ps(nc, st, name, shape, dtype=F32):
    _UNIQ[0] += 1
    return Buf(st.enter_context(
        nc.psum_tensor(f"{name}_{_UNIQ[0]}", shape, dtype)).ap(), name)



def emit_all(plan, nc, io, DB, P, upto=10 ** 9):
    emit_prep(plan, nc, io, DB, P)
    plan.barrier()
    emit_prep_wq0(plan, nc, io, DB, P)
    plan.barrier()
    emit_prep_quantw(plan, nc, io, DB, P, "w1f", "w1T", F1, D, P["sw1"],
                     DB["wq1T_s"], DB["s1_s"], F1T)
    plan.barrier()
    emit_prep_quantw(plan, nc, io, DB, P, "w2f", "w2T", D, F1, P["sw2"],
                     DB["wq2T_s"], DB["s2_s"], DT)
    plan.barrier()
    for seq in range(BSEQ):
        emit_preln(plan, nc, io, DB, P, seq)
        plan.barrier()

    def chain(seq, d):
        return [
            ("x", 32, 1.0, lambda: emit_inproj0(plan, nc, io, DB, P, seq, d)),
            ("x", DIT, 0.45, lambda: emit_conv(
                plan, nc, io, DB, seq, DB[f"xi_{d}_s"], DB[f"xc_{d}_s"],
                io[f"{d}_convw"], io[f"{d}_convb"], DB[f"{d}_convw"],
                DB[f"{d}_convb"], fwd=(d == "f"))),
            ("x", NCH, 0.3, lambda: emit_xw0(plan, nc, io, DB, P, seq, d)),
            ("x", DIT, 0.5, lambda: emit_dtproj(
                plan, nc, io, DB, P, seq, io[f"{d}_dtwT"], DB[f"{d}_dtwT"],
                io[f"{d}_dtb"], DB[f"{d}_dtb"], DB[f"dtr_{d}_s"],
                DB[f"dl_{d}_s"])),
            ("x", 8, 3.2, lambda: emit_scan0(plan, nc, io, DB, P, seq, d)),
            ("x", DIT, 0.3, lambda: emit_gating(
                plan, nc, io, DB, seq, DB[f"y_{d}_s"], DB[f"z_{d}_s"])),
            ("x", NCH, 0.6, lambda: emit_proj_simple(
                plan, nc, io, DB, P, seq, io[f"{d}_outT"], DB[f"{d}_outT"],
                DB[f"y_{d}_s"], DIT, DT, DB[f"o_{d}_s"], name=f"op0{seq}{d}")),
        ]

    def l1chain(seq):
        return [
            ("x", 34, 1.2, lambda: emit_inproj1(plan, nc, io, DB, P, seq)),
            ("x", DIT, 0.45, lambda: emit_conv(
                plan, nc, io, DB, seq, DB["xi1_s"], DB["xc1_s"],
                io["conv1w"], io["conv1b"], DB["conv1w"], DB["conv1b"],
                fwd=True)),
            ("x", DIT, 0.5, lambda: emit_dtproj(
                plan, nc, io, DB, P, seq, io["dt1T"], DB["dt1T"],
                io["dt1b"], DB["dt1b"], DB["dtr1_s"], DB["dl1_s"])),
            ("x", 8, 6.4, lambda: emit_scan1(plan, nc, io, DB, P, seq)),
            ("x", DIT, 0.3, lambda: emit_gating(
                plan, nc, io, DB, seq, DB["y1_s"], DB["z1_s"])),
            ("x", NCH, 1.0, lambda: emit_outproj1_resln(plan, nc, io, DB, P,
                                                        seq)),
        ]

    def atomgen(fn):
        def g():
            fn()
            return
            yield
        return g

    laneL = (chain(0, "f") + chain(1, "f")
             + [("x", NCH, 0.8, lambda: emit_combine(plan, nc, io, DB, P, 0)),
                ("x", NCH, 0.7, lambda: emit_ffn0(plan, nc, io, DB, P, 0))]
             + l1chain(0))
    laneR = (chain(0, "b") + chain(1, "b")
             + [("x", NCH, 0.8, lambda: emit_combine(plan, nc, io, DB, P, 1)),
                ("x", NCH, 0.7, lambda: emit_ffn0(plan, nc, io, DB, P, 1))]
             + l1chain(1))
    laneL = [("left", n, c, g) for (_, n, c, g) in laneL]
    laneR = [("right", n, c, g) for (_, n, c, g) in laneR]
    lane_exec(plan, [laneL, laneR])
    plan.barrier()

    emit_quantprep(plan, nc, io, DB, P, P["gmx"], P["rcp1"], P["sw1"], P["b1"],
                   P["seff1"], P["beff1"], F1T,
                   (DB["qd1_sin"], DB["qd1_max"], DB["qd1_se"], DB["qd1_be"]))
    plan.barrier()
    for seq in range(BSEQ):
        emit_ffn1a(plan, nc, io, DB, P, seq)
        plan.barrier()
    emit_quantprep(plan, nc, io, DB, P, P["ghx"], P["rcp2"], P["sw2"], P["b2"],
                   P["seff2"], P["beff2"], DT,
                   (DB["qd2_sin"], DB["qd2_max"], DB["qd2_se"], DB["qd2_be"]))
    plan.barrier()
    for seq in range(BSEQ):
        emit_ffn1b(plan, nc, io, DB, P, seq)
        plan.barrier()



def emit_prep(plan, nc, io, DB, P):
    plan.op("dve", lambda h: h.memset(P["onesc"].ap, 1.0), writes=[P["onesc"]])
    plan.op("dve", lambda h: h.memset(P["onesr"].ap, 1.0), writes=[P["onesr"]])
    plan.op("dve", lambda h: h.memset(P["ones1"].ap, 1.0), writes=[P["ones1"]])
    plan.op("dve", lambda h: h.memset(P["gmx"].ap, -1e30), writes=[P["gmx"]])
    plan.op("dve", lambda h: h.memset(P["ghx"].ap, -1e30), writes=[P["ghx"]])
    plan.dma("sp", P["ident"], io["ident"], reads=[DB["ident"]])
    plan.dma("sp", P["b1"], io["b1"], reads=[DB["b1"]])
    plan.dma("sp", P["b2"], io["b2"], reads=[DB["b2"]])
    with contextlib.ExitStack() as st:
        at = sb(nc, st, "prep_at", [128, N0])
        for d in ("f", "b"):
            dst = P["A0f"] if d == "f" else P["A0b"]
            for dt in range(DIT):
                plan.dma("sp", at, io[f"{d}_alog"][dt * 128:(dt + 1) * 128, :],
                         reads=[DB[f"{d}_alog"]])
                plan.op("act",
                        lambda h, dt=dt, dst=dst: h.activation(
                            dst.ap[:, dt * N0:(dt + 1) * N0], at.ap, AF.Exp),
                        reads=[at], rw=[dst])
    plan.dma("sp", P["A1"], io["alog1b"], reads=[DB["alog1b"]])
    plan.op("act", lambda h: h.activation(P["A1"].ap, P["A1"].ap, AF.Exp),
            rw=[P["A1"]])


def round_nearest(plan, buf, ibuf, t1, t2):
    plan.op("dve", lambda h: h.tensor_copy(ibuf.ap, buf.ap),
            reads=[buf], writes=[ibuf])
    plan.op("dve", lambda h: h.tensor_copy(t1.ap, ibuf.ap),
            reads=[ibuf], writes=[t1])
    plan.op("dve", lambda h: h.tensor_tensor(buf.ap, buf.ap, t1.ap,
                                             ALU.subtract),
            reads=[t1], rw=[buf])
    plan.op("dve", lambda h: h.tensor_scalar(t2.ap, buf.ap, 0.5, None,
                                             ALU.is_gt),
            reads=[buf], writes=[t2])
    plan.op("dve", lambda h: h.tensor_scalar(buf.ap, buf.ap, -0.5, None,
                                             ALU.is_lt), rw=[buf])
    plan.op("dve", lambda h: h.tensor_tensor(t1.ap, t1.ap, t2.ap, ALU.add),
            reads=[t2], rw=[t1])
    plan.op("dve", lambda h: h.tensor_tensor(buf.ap, t1.ap, buf.ap,
                                             ALU.subtract),
            reads=[t1], rw=[buf])


def emit_prep_wq0(plan, nc, io, DB, P):
    with contextlib.ExitStack() as st:
        t = [sb(nc, st, f"wq_t{j}", [128, D]) for j in range(2)]
        sg = sb(nc, st, "wq_sg", [128, D])
        ti = sb(nc, st, "wq_ti", [128, D], I32)
        mx = sb(nc, st, "wq_mx", [128, 1])
        mx2 = sb(nc, st, "wq_mx2", [128, 1])
        sc = sb(nc, st, "wq_sc", [128, 1])
        rsc = sb(nc, st, "wq_rsc", [128, 1])
        srow = sb(nc, st, "wq_srow", [1, 1])
        pt = P["psx"]
        pb = P["par"]
        k = io["ffn_k"]
        for dt in range(DT):
            b = t[dt % 2]
            plan.dma("sp", b, k[dt * 128:(dt + 1) * 128, :], reads=[DB["ffn_k"]])
            if dt == 0:
                plan.op("dve", lambda h: h.reduce_max(
                    mx.ap, t[0].ap, axis=AXX, apply_absolute_value=True),
                    reads=[t[0]], writes=[mx])
            else:
                plan.op("dve", lambda h, b=b: h.reduce_max(
                    mx2.ap, b.ap, axis=AXX, apply_absolute_value=True),
                    reads=[b], writes=[mx2])
                plan.op("dve", lambda h: h.tensor_max(mx.ap, mx.ap, mx2.ap),
                        reads=[mx2], rw=[mx])
        plan.op("pe", lambda h: h.transpose(pt.ap[0:1, 0:128], mx.ap,
                                            P["ident"].ap),
                reads=[mx, P["ident"]], writes=[pt])
        plan.op("dve", lambda h: h.reduce_max(srow.ap, pt.ap[0:1, 0:128],
                                              axis=AXX),
                reads=[pt], writes=[srow])
        plan.op("dve", lambda h: h.tensor_scalar(srow.ap, srow.ap, 1.0 / 7.0,
                                                 None, ALU.mult), rw=[srow])
        plan.op("pe", lambda h: h.matmul(pb.ap[:, 0:1], P["onesc"].ap,
                                         srow.ap, start=True, stop=True),
                reads=[srow, P["onesc"]], writes=[pb])
        plan.op("dve", lambda h: h.tensor_copy(sc.ap, pb.ap[:, 0:1]),
                reads=[pb], writes=[sc])
        plan.op("dve", lambda h: h.reciprocal(rsc.ap, sc.ap),
                reads=[sc], writes=[rsc])
        sg2 = sb(nc, st, "wq_sg2", [128, D])
        for dt in range(DT):
            b = t[dt % 2]
            plan.dma("sp", b, k[dt * 128:(dt + 1) * 128, :], reads=[DB["ffn_k"]])
            plan.op("dve", lambda h, b=b: h.tensor_scalar(
                b.ap, b.ap, rsc.ap[:, 0:1], None, ALU.mult),
                reads=[rsc], rw=[b])
            round_nearest(plan, b, ti, sg, sg2)
            plan.op("dve", lambda h, b=b: h.tensor_scalar(
                b.ap, b.ap, -8.0, 7.0, ALU.max, ALU.min), rw=[b])
            plan.op("dve", lambda h, b=b: h.tensor_scalar(
                b.ap, b.ap, sc.ap[:, 0:1], None, ALU.mult),
                reads=[sc], rw=[b])
            plan.dma("pool", io["wq0_s"][dt * 128:(dt + 1) * 128, :], b,
                     writes=[DB["wq0_s"]])


def emit_prep_quantw(plan, nc, io, DB, P, wf_name, wT_name, NF, ND, s_sb,
                     outdb, srowdb, NFT):
    with contextlib.ExitStack() as st:
        t = [sb(nc, st, f"q_t{j}{wf_name}", [128, ND]) for j in range(2)]
        mx = sb(nc, st, f"q_mx{wf_name}", [128, 1])
        wf = io[wf_name]
        for m in range(NFT):
            b = t[m % 2]
            plan.dma("sp", b, wf[m * 128:(m + 1) * 128, :], reads=[DB[wf_name]])
            plan.op("dve", lambda h, b=b: h.reduce_max(
                mx.ap, b.ap, axis=AXX, apply_absolute_value=True),
                reads=[b], writes=[mx])
            plan.op("dve", lambda h, m=m: h.tensor_scalar(
                s_sb.ap[:, m:m + 1], mx.ap, 1e-8, 1.0 / 127.0, ALU.max,
                ALU.mult), reads=[mx], rw=[s_sb])
            plan.op("dve", lambda h, m=m: h.reciprocal(
                mx.ap, s_sb.ap[:, m:m + 1]), reads=[s_sb], writes=[mx])
            plan.dma("pool", srowdb.ap[m * 128:(m + 1) * 128, 0:1],
                     mx.ap, reads=[mx], writes=[srowdb])
        NDT = ND // 128
        CH = NF // TC
        w = [sb(nc, st, f"q_w{j}{wf_name}", [128, TC]) for j in range(2)]
        sr = [sb(nc, st, f"q_sr{j}{wf_name}", [128, TC]) for j in range(2)]
        sg = sb(nc, st, f"q_sg{wf_name}", [128, TC])
        sg2 = sb(nc, st, f"q_sg2{wf_name}", [128, TC])
        ti = sb(nc, st, f"q_ti{wf_name}", [128, TC], I32)
        ob = sb(nc, st, f"q_ob{wf_name}", [128, TC], BF16)
        wT = io[wT_name]
        i_s = 0
        for c in range(CH):
            for kd in range(NDT):
                bw, bs = w[i_s % 2], sr[i_s % 2]
                plan.dma("sp", bw,
                         wT[kd * 128:(kd + 1) * 128, c * TC:(c + 1) * TC],
                         reads=[DB[wT_name]])
                plan.dma("sp", bs,
                         srowdb.ap[c * TC:(c + 1) * TC, 0:1]
                         .transpose([1, 0]).broadcast_to([128, TC]),
                         reads=[srowdb], writes=[bs])
                plan.op("dve", lambda h, bw=bw, bs=bs: h.tensor_tensor(
                    bw.ap, bw.ap, bs.ap, ALU.mult), reads=[bs], rw=[bw])
                round_nearest(plan, bw, ti, sg, sg2)
                plan.op("dve", lambda h, bw=bw: h.tensor_scalar(
                    bw.ap, bw.ap, -128.0, 127.0, ALU.max, ALU.min), rw=[bw])
                plan.op("dve", lambda h, bw=bw: h.tensor_copy(ob.ap, bw.ap),
                        reads=[bw], writes=[ob])
                plan.dma("pool",
                         outdb.ap[kd * 128:(kd + 1) * 128, c * TC:(c + 1) * TC],
                         ob, reads=[ob], writes=[outdb])
                i_s += 1



class LNScratch:
    def __init__(self, plan, nc, st, tag, P=None):
        self.p_sx = P["psx"]
        self.p_sq = P["psq"]
        self.p_ar = P["par"]
        self.p_br = P["pbr"]
        self.sq = sb(nc, st, f"ln_sq{tag}", [128, TC])
        self.rows = [sb(nc, st, f"ln_r{j}{tag}", [1, TC]) for j in range(4)]
        self.mxr = sb(nc, st, f"ln_mx{tag}", [128, 1])


def emit_postln(plan, P, lns, tiles, dst_fn, dstdb, g_ap=None, b_ap=None,
                gdeps=(), maxacc=None):
    r_mean, r_var, r_b, r_rstd = lns.rows
    for dt in range(DT):
        tl = tiles[dt]
        plan.op("pe", lambda h, dt=dt, tl=tl: h.matmul(
            lns.p_sx.ap, P["ones1"].ap, tl.ap, start=(dt == 0),
            stop=(dt == DT - 1)),
            reads=[tl, P["ones1"]], writes=[lns.p_sx] if dt == 0 else (),
            rw=() if dt == 0 else ())
        plan.op("act", lambda h, tl=tl: h.activation(lns.sq.ap, tl.ap,
                                                     AF.Square),
                reads=[tl], writes=[lns.sq])
        plan.op("pe", lambda h, dt=dt: h.matmul(
            lns.p_sq.ap, P["ones1"].ap, lns.sq.ap, start=(dt == 0),
            stop=(dt == DT - 1)),
            reads=[lns.sq, P["ones1"]], writes=[lns.p_sq] if dt == 0 else ())
    last_sx = plan.ops["pe"][-1]
    lns.p_sx.writers = [plan.ops["pe"][-2]]
    lns.p_sq.writers = [last_sx]
    plan.op("dve", lambda h: h.tensor_scalar(r_mean.ap, lns.p_sx.ap, 1.0 / D,
                                             None, ALU.mult),
            reads=[lns.p_sx], writes=[r_mean])
    plan.op("dve", lambda h: h.tensor_scalar(r_var.ap, lns.p_sq.ap, 1.0 / D,
                                             1e-5, ALU.mult, ALU.add),
            reads=[lns.p_sq], writes=[r_var])
    plan.op("dve", lambda h: h.tensor_tensor(r_b.ap, r_mean.ap, r_mean.ap,
                                             ALU.mult),
            reads=[r_mean], writes=[r_b])
    plan.op("dve", lambda h: h.tensor_tensor(r_var.ap, r_var.ap, r_b.ap,
                                             ALU.subtract),
            reads=[r_b], rw=[r_var])
    plan.op("act", lambda h: h.activation(r_rstd.ap, r_var.ap, AF.Sqrt),
            reads=[r_var], writes=[r_rstd])
    plan.op("dve", lambda h: h.reciprocal(r_rstd.ap, r_rstd.ap), rw=[r_rstd])
    for _ in range(2):
        plan.op("dve", lambda h: h.tensor_tensor(r_b.ap, r_rstd.ap, r_rstd.ap,
                                                 ALU.mult),
                reads=[r_rstd], writes=[r_b])
        plan.op("dve", lambda h: h.tensor_tensor(r_b.ap, r_b.ap, r_var.ap,
                                                 ALU.mult),
                reads=[r_var], rw=[r_b])
        plan.op("dve", lambda h: h.tensor_scalar(r_b.ap, r_b.ap, -0.5, 1.5,
                                                 ALU.mult, ALU.add), rw=[r_b])
        plan.op("dve", lambda h: h.tensor_tensor(r_rstd.ap, r_rstd.ap, r_b.ap,
                                                 ALU.mult),
                reads=[r_b], rw=[r_rstd])
    plan.op("dve", lambda h: h.tensor_tensor(r_b.ap, r_mean.ap, r_rstd.ap,
                                             ALU.mult),
            reads=[r_mean, r_rstd], writes=[r_b])
    plan.op("dve", lambda h: h.tensor_scalar(r_b.ap, r_b.ap, -1.0, None,
                                             ALU.mult), rw=[r_b])
    if g_ap is None:
        plan.op("pe", lambda h: h.matmul(lns.p_ar.ap, P["onesc"].ap, r_rstd.ap,
                                         start=True, stop=True),
                reads=[r_rstd, P["onesc"]], writes=[lns.p_ar])
        plan.op("pe", lambda h: h.matmul(lns.p_br.ap, P["onesc"].ap, r_b.ap,
                                         start=True, stop=True),
                reads=[r_b, P["onesc"]], writes=[lns.p_br])
    for dt in range(DT):
        tl = tiles[dt]
        if g_ap is not None:
            plan.op("pe", lambda h, dt=dt: h.matmul(
                lns.p_ar.ap, g_ap[:, dt * 128:(dt + 1) * 128], r_rstd.ap,
                start=True, stop=True),
                deps=list(gdeps), reads=[r_rstd], writes=[lns.p_ar])
            plan.op("pe", lambda h, dt=dt: h.matmul(
                lns.p_br.ap, g_ap[:, dt * 128:(dt + 1) * 128], r_b.ap,
                start=True, stop=False),
                deps=list(gdeps), reads=[r_b], writes=[lns.p_br])
            o = plan.op("pe", lambda h, dt=dt: h.matmul(
                lns.p_br.ap, b_ap[:, dt * 128:(dt + 1) * 128], P["onesr"].ap,
                start=False, stop=True),
                deps=list(gdeps), reads=[P["onesr"]])
            lns.p_br.writers = [o]
            lns.p_br.readers = []
        plan.op("dve", lambda h, tl=tl: h.tensor_tensor(
            tl.ap, tl.ap, lns.p_ar.ap, ALU.mult), reads=[lns.p_ar], rw=[tl])
        plan.op("dve", lambda h, tl=tl: h.tensor_tensor(
            tl.ap, tl.ap, lns.p_br.ap, ALU.add), reads=[lns.p_br], rw=[tl])
        if maxacc is not None:
            plan.op("dve", lambda h, tl=tl: h.reduce_max(lns.mxr.ap, tl.ap,
                                                         axis=AXX),
                    reads=[tl], writes=[lns.mxr])
            plan.op("dve", lambda h: h.tensor_max(maxacc.ap, maxacc.ap,
                                                  lns.mxr.ap),
                    reads=[lns.mxr], rw=[maxacc])
        plan.dma("pool", dst_fn(dt), tl, reads=[tl], writes=[dstdb])


def emit_preln(plan, nc, io, DB, P, seq):
    with contextlib.ExitStack() as st:
        tiles = [sb(nc, st, f"pl_t{j}", [128, TC]) for j in range(DT)]
        gsb = sb(nc, st, "pl_g", [1, D])
        bsb = sb(nc, st, "pl_b", [1, D])
        lns = LNScratch(plan, nc, st, "pl", P)
        g1 = plan.dma("sp", gsb, io["ln_g"], reads=[DB["ln_g"]])
        g2 = plan.dma("sp", bsb, io["ln_b"], reads=[DB["ln_b"]])
        for c in range(NCH):
            col = seq * L + c * TC
            for dt in range(DT):
                plan.dma("sp", tiles[dt],
                         io["x0"][dt * 128:(dt + 1) * 128, col:col + TC],
                         reads=[DB["x0"]])
            emit_postln(plan, P, lns, tiles,
                        lambda dt, col=col: io["xn_s"][dt * 128:(dt + 1) * 128,
                                                       col:col + TC],
                        DB["xn_s"], g_ap=gsb.ap, b_ap=bsb.ap,
                        gdeps=[g1, g2, gsb.writers[0] if gsb.writers else None])



def mm_group(plan, psum, pairs, mw=None):
    n = len(pairs)
    first_deps_bufs = [psum]
    for k, (wb, wa, mb, ma) in enumerate(pairs):
        start = (k == 0)
        stop = (k == n - 1)
        if start:
            o = plan.op("pe", lambda h, wa=wa, ma=ma, start=start, stop=stop:
                        h.matmul(psum.ap if mw is None else psum.ap[0:mw, :],
                                 wa, ma, start=start, stop=stop),
                        reads=[wb, mb], writes=[psum])
        else:
            o = plan.op("pe", lambda h, wa=wa, ma=ma, start=start, stop=stop:
                        h.matmul(psum.ap if mw is None else psum.ap[0:mw, :],
                                 wa, ma, start=start, stop=stop),
                        reads=[wb, mb])
    psum.writers = [o]
    psum.readers = []
    return o


def emit_proj_simple(plan, nc, io, DB, P, seq, wT_ap, wT_db, mov_db, KT, MT,
                     dst_db, name=""):
    with contextlib.ExitStack() as st:
        wb = [[sb(nc, st, f"pj_w{s}_{k}{name}", [128, 128]) for k in range(KT)]
              for s in range(2)]
        mb = [sb(nc, st, f"pj_m{k}{name}", [128, TC]) for k in range(KT)]
        pb = pbanks(P)
        eb = [sb(nc, st, f"pj_e{j}{name}", [128, TC]) for j in range(2)]
        for c in range(NCH):
            yield
            col = seq * L + c * TC
            for k in range(KT):
                plan.dma("sp", mb[k],
                         mov_db.ap[k * 128:(k + 1) * 128, col:col + TC],
                         reads=[mov_db])
            for m in range(MT):
                gi = c * MT + m
                ws = gi % 2
                for k in range(KT):
                    plan.dma("sp", wb[ws][k],
                             wT_ap[k * 128:(k + 1) * 128,
                                   m * 128:(m + 1) * 128],
                             reads=[wT_db])
                bank = gi % 2
                mm_group(plan, pb[bank],
                         [(wb[ws][k], wb[ws][k].ap, mb[k], mb[k].ap)
                          for k in range(KT)])
                e = eb[gi % 2]
                plan.op("act", lambda h, e=e, bank=bank: h.activation(
                    e.ap, pb[bank].ap, AF.Copy), reads=[pb[bank]], writes=[e])
                plan.dma("pool", dst_db.ap[m * 128:(m + 1) * 128, col:col + TC],
                         e, reads=[e], writes=[dst_db])


def emit_inproj0(plan, nc, io, DB, P, seq, d):
    with contextlib.ExitStack() as st:
        xn = [sb(nc, st, f"ip_xn{k}", [128, L]) for k in range(DT)]
        wb = [[sb(nc, st, f"ip_w{s}_{k}", [128, 128]) for k in range(DT)]
              for s in range(2)]
        pb = pbanks(P)
        eb = [sb(nc, st, f"ip_e{j}", [128, TC]) for j in range(3)]
        for k in range(DT):
            plan.dma("sp", xn[k], io["xn_s"][k * 128:(k + 1) * 128,
                                             seq * L:(seq + 1) * L],
                     reads=[DB["xn_s"]])
        gi = 0
        for m in range(2 * DIT):
            yield
            ws = m % 2
            for k in range(DT):
                plan.dma("sp", wb[ws][k],
                         io[f"{d}_inT"][k * 128:(k + 1) * 128,
                                        m * 128:(m + 1) * 128],
                         reads=[DB[f"{d}_inT"]])
            for c in range(NCH):
                col = seq * L + c * TC
                bank = gi % 2
                mm_group(plan, pb[bank],
                         [(wb[ws][k], wb[ws][k].ap, xn[k],
                           xn[k].ap[:, c * TC:(c + 1) * TC]) for k in range(DT)])
                e = eb[gi % 3]
                plan.op("act", lambda h, e=e, bank=bank: h.activation(
                    e.ap, pb[bank].ap, AF.Copy), reads=[pb[bank]], writes=[e])
                dst = DB[f"xi_{d}_s"] if m < DIT else DB[f"z_{d}_s"]
                row = (m if m < DIT else m - DIT) * 128
                plan.dma("pool", dst.ap[row:row + 128, col:col + TC], e,
                         reads=[e], writes=[dst])
                gi += 1


def emit_conv(plan, nc, io, DB, seq, src_db, dst_db, convw_ap, convb_ap,
              convw_db, convb_db, fwd=True):
    with contextlib.ExitStack() as st:
        xb = [sb(nc, st, f"cv_x{j}", [128, L + 3]) for j in range(2)]
        ac = [sb(nc, st, f"cv_a{j}", [128, L]) for j in range(2)]
        wc = sb(nc, st, "cv_w", [128, DIT * 4])
        bc = sb(nc, st, "cv_b", [128, DIT])
        plan.dma("sp", wc, convw_ap, reads=[convw_db])
        plan.dma("sp", bc, convb_ap, reads=[convb_db])
        for kt in range(DIT):
            yield
            b = xb[kt % 2]
            a = ac[kt % 2]
            if fwd:
                plan.dma("sp", b.ap[:, 3:3 + L],
                         src_db.ap[kt * 128:(kt + 1) * 128,
                                   seq * L:(seq + 1) * L],
                         reads=[src_db], writes=[b])
                plan.op("dve", lambda h, b=b: h.memset(b.ap[:, 0:3], 0.0),
                        rw=[b])
            else:
                plan.dma("sp", b.ap[:, 0:L],
                         src_db.ap[kt * 128:(kt + 1) * 128,
                                   seq * L:(seq + 1) * L],
                         reads=[src_db], writes=[b])
                plan.op("dve", lambda h, b=b: h.memset(b.ap[:, L:L + 3], 0.0),
                        rw=[b])
            off0 = 0 if fwd else 3
            plan.op("dve", lambda h, b=b, a=a, kt=kt, off0=off0: h.tensor_scalar(
                a.ap, b.ap[:, off0:off0 + L], wc.ap[:, kt * 4:kt * 4 + 1],
                None, ALU.mult), reads=[b, wc], writes=[a])
            for j in range(1, 4):
                off = j if fwd else 3 - j
                plan.op("dve", lambda h, b=b, a=a, kt=kt, j=j, off=off:
                        h.scalar_tensor_tensor(
                            a.ap, b.ap[:, off:off + L],
                            wc.ap[:, kt * 4 + j:kt * 4 + j + 1], a.ap,
                            ALU.mult, ALU.add), reads=[b, wc], rw=[a])
            plan.op("act", lambda h, a=a, kt=kt: h.activation(
                a.ap, a.ap, AF.Silu, bias=bc.ap[:, kt:kt + 1]),
                reads=[bc], rw=[a])
            plan.dma("pool", dst_db.ap[kt * 128:(kt + 1) * 128,
                                       seq * L:(seq + 1) * L], a,
                     reads=[a], writes=[dst_db])


def emit_xw0(plan, nc, io, DB, P, seq, d):
    with contextlib.ExitStack() as st:
        wb = [sb(nc, st, f"xw_w{k}", [128, 96]) for k in range(DIT)]
        mb = [sb(nc, st, f"xw_m{k}", [128, TC]) for k in range(DIT)]
        pb = pbanks(P)
        eb = [sb(nc, st, f"xw_e{j}", [96, TC]) for j in range(2)]
        for k in range(DIT):
            plan.dma("sp", wb[k], io[f"{d}_xwT"][k * 128:(k + 1) * 128, :],
                     reads=[DB[f"{d}_xwT"]])
        for c in range(NCH):
            yield
            col = seq * L + c * TC
            s2 = c % 2
            for k in range(DIT):
                plan.dma("sp", mb[k],
                         DB[f"xc_{d}_s"].ap[k * 128:(k + 1) * 128, col:col + TC],
                         reads=[DB[f"xc_{d}_s"]])
            mm_group(plan, pb[s2],
                     [(wb[k], wb[k].ap, mb[k], mb[k].ap)
                      for k in range(DIT)], mw=96)
            e = eb[s2]
            plan.op("act", lambda h, e=e, s2=s2: h.activation(
                e.ap, pb[s2].ap[0:96, :], AF.Copy), reads=[pb[s2]], writes=[e])
            plan.dma("pool", DB[f"dtr_{d}_s"].ap[:, col:col + TC],
                     e.ap[0:64, :], reads=[e], writes=[DB[f"dtr_{d}_s"]])
            plan.dma("pool", DB[f"bc_{d}_s"].ap[:, col:col + TC],
                     e.ap[64:96, :], reads=[e], writes=[DB[f"bc_{d}_s"]])


def emit_dtproj(plan, nc, io, DB, P, seq, dtwT_ap, dtwT_db, dtb_ap, dtb_db,
                dtr_db, dl_db):
    with contextlib.ExitStack() as st:
        wb = [sb(nc, st, f"dp_w{j}", [64, 128]) for j in range(2)]
        mbb = sb(nc, st, "dp_m", [64, L])
        pb = pbanks(P)
        sg = [sb(nc, st, f"dp_s{j}", [128, TC]) for j in range(2)]
        nb = sb(nc, st, "dp_nb", [128, DIT])
        bt = sb(nc, st, "dp_bt", [128, DIT])
        plan.dma("sp", bt, dtb_ap, reads=[dtb_db])
        plan.op("dve", lambda h: h.tensor_scalar(nb.ap, bt.ap, -1.0, None,
                                                 ALU.mult),
                reads=[bt], writes=[nb])
        plan.dma("sp", mbb, dtr_db.ap[:, seq * L:(seq + 1) * L],
                 reads=[dtr_db])
        gi = 0
        for m in range(DIT):
            yield
            ws = m % 2
            plan.dma("sp", wb[ws], dtwT_ap[:, m * 128:(m + 1) * 128],
                     reads=[dtwT_db])
            for c in range(NCH):
                col = seq * L + c * TC
                bank = gi % 2
                mm_group(plan, pb[bank],
                         [(wb[ws], wb[ws].ap, mbb,
                           mbb.ap[:, c * TC:(c + 1) * TC])])
                s = sg[gi % 2]
                plan.op("act", lambda h, s=s, bank=bank, m=m: h.activation(
                    s.ap, pb[bank].ap, AF.Sigmoid, scale=-1.0,
                    bias=nb.ap[:, m:m + 1]),
                    reads=[pb[bank], nb], writes=[s])
                plan.op("act", lambda h, s=s: h.activation(s.ap, s.ap, AF.Ln),
                        rw=[s])
                plan.dma("pool", dl_db.ap[m * 128:(m + 1) * 128, col:col + TC],
                         s, reads=[s], writes=[dl_db])
                gi += 1



def emit_scan(plan, nc, io, DB, P, seq, *, N, NG, TCS, dl_db, xc_db, bc_db,
              y_db, dp_ap, dp_db, a_col_fn, reverse, name):
    nh = L // TCS
    with contextlib.ExitStack() as st:
        brep = [sb(nc, st, f"sc_B{j}{name}", [128, TCS]) for j in range(NG)]
        crep = [sb(nc, st, f"sc_C{j}{name}", [128, TCS]) for j in range(NG)]
        dl = [sb(nc, st, f"sc_dl{j}{name}", [128, TCS]) for j in range(2)]
        xc = [sb(nc, st, f"sc_xc{j}{name}", [128, TCS]) for j in range(2)]
        du = sb(nc, st, f"sc_du{name}", [128, TCS])
        ya = sb(nc, st, f"sc_ya{name}", [128, TCS])
        da = [sb(nc, st, f"sc_da{j}{name}", [128, TCS]) for j in range(2)]
        bb = [sb(nc, st, f"sc_b{j}{name}", [128, TCS]) for j in range(2)]
        hh = [sb(nc, st, f"sc_h{j}{name}", [128, TCS]) for j in range(2)]
        hc = [sb(nc, st, f"sc_hc{j}{name}", [128, TCS]) for j in range(2)]
        dpb = sb(nc, st, f"sc_dp{name}", [128, DIT])
        carry = (sb(nc, st, f"sc_cr{name}", [128, DIT * N]) if nh > 1 else None)
        plan.dma("sp", dpb, dp_ap, reads=[dp_db])
        chunks = list(range(nh)) if not reverse else list(range(nh - 1, -1, -1))
        for ci, ch in enumerate(chunks):
            col = seq * L + ch * TCS
            for g in range(N // NG):
                yield
                ns = [g * NG + j for j in range(NG)]
                for j, n in enumerate(ns):
                    plan.dma("sp", brep[j],
                             bc_db.ap[n:n + 1, col:col + TCS]
                             .broadcast_to([128, TCS]),
                             reads=[bc_db], writes=[brep[j]])
                    plan.dma("sp", crep[j],
                             bc_db.ap[N + n:N + n + 1, col:col + TCS]
                             .broadcast_to([128, TCS]),
                             reads=[bc_db], writes=[crep[j]])
                for dt in range(DIT):
                    dlb, xcb = dl[dt % 2], xc[dt % 2]
                    plan.dma("sp", dlb, dl_db.ap[dt * 128:(dt + 1) * 128,
                                                 col:col + TCS],
                             reads=[dl_db])
                    plan.dma("sp", xcb, xc_db.ap[dt * 128:(dt + 1) * 128,
                                                 col:col + TCS],
                             reads=[xc_db])
                    plan.op("dve", lambda h, dlb=dlb, xcb=xcb:
                            h.scalar_tensor_tensor(du.ap, xcb.ap, -1.0, dlb.ap,
                                                   ALU.mult, ALU.mult),
                            reads=[dlb, xcb], writes=[du])
                    if g == 0:
                        plan.op("dve", lambda h, xcb=xcb, dt=dt: h.tensor_scalar(
                            ya.ap, xcb.ap, dpb.ap[:, dt:dt + 1], None, ALU.mult),
                            reads=[xcb, dpb], writes=[ya])
                    def emit_pre(j):
                        aap = a_col_fn(dt, ns[j])
                        p = j % 2
                        dap, bbp = da[p], bb[p]
                        plan.op("act", lambda h, dlb=dlb, aap=aap, dap=dap:
                                h.activation(dap.ap, dlb.ap, AF.Exp, scale=aap),
                                reads=[dlb], writes=[dap])
                        plan.op("dve", lambda h, j=j, bbp=bbp: h.tensor_tensor(
                            bbp.ap, du.ap, brep[j].ap, ALU.mult),
                            reads=[du, brep[j]], writes=[bbp])

                    emit_pre(0)
                    for j, n in enumerate(ns):
                        if j + 1 < len(ns):
                            emit_pre(j + 1)
                        p = j % 2
                        dap, bbp, hhp, hcp = da[p], bb[p], hh[p], hc[p]
                        if nh > 1 and ci > 0:
                            cc = dt * N + n
                            init = carry.ap[:, cc:cc + 1]
                            ini_reads = [carry]
                        else:
                            init = 0.0
                            ini_reads = []
                        if not reverse:
                            plan.op("dve", lambda h, dap=dap, bbp=bbp,
                                    hhp=hhp, init=init:
                                    h.tensor_tensor_scan(hhp.ap, dap.ap,
                                                         bbp.ap, init,
                                                         ALU.mult, ALU.add),
                                    reads=[dap, bbp] + ini_reads,
                                    writes=[hhp])
                        else:
                            plan.op("dve", lambda h, dap=dap, bbp=bbp,
                                    hhp=hhp, init=init:
                                    h.tensor_tensor_scan(hhp.ap[:, ::-1],
                                                         dap.ap[:, ::-1],
                                                         bbp.ap[:, ::-1],
                                                         init, ALU.mult,
                                                         ALU.add),
                                    reads=[dap, bbp] + ini_reads,
                                    writes=[hhp])
                        if nh > 1 and ci < nh - 1:
                            cc = dt * N + n
                            lastc = (TCS - 1) if not reverse else 0
                            plan.op("dve", lambda h, cc=cc, lastc=lastc,
                                    hhp=hhp:
                                    h.tensor_copy(carry.ap[:, cc:cc + 1],
                                                  hhp.ap[:, lastc:lastc + 1]),
                                    reads=[hhp], rw=[carry])
                        plan.op("dve", lambda h, j=j, hhp=hhp, hcp=hcp:
                                h.tensor_tensor(hcp.ap, hhp.ap, crep[j].ap,
                                                ALU.mult),
                                reads=[hhp, crep[j]], writes=[hcp])
                        if g == 0 or j > 0:
                            plan.op("dve", lambda h, hcp=hcp: h.tensor_tensor(
                                ya.ap, ya.ap, hcp.ap, ALU.add),
                                reads=[hcp], rw=[ya])
                        else:
                            plan.op("dve", lambda h, hcp=hcp: h.tensor_copy(
                                ya.ap, hcp.ap), reads=[hcp], writes=[ya])
                    if g == 0:
                        plan.dma("pool", y_db.ap[dt * 128:(dt + 1) * 128,
                                                 col:col + TCS], ya,
                                 reads=[ya], rw=[y_db])
                    else:
                        plan.dma("pool", y_db.ap[dt * 128:(dt + 1) * 128,
                                                 col:col + TCS], ya,
                                 reads=[ya], rw=[y_db], accum_op=ALU.add)


def emit_scan0(plan, nc, io, DB, P, seq, d):
    A = P["A0f"] if d == "f" else P["A0b"]

    def acol(dt, n):
        return A.ap[:, dt * N0 + n:dt * N0 + n + 1]

    yield from emit_scan(plan, nc, io, DB, P, seq, N=N0, NG=4, TCS=L // 2,
                         dl_db=DB[f"dl_{d}_s"], xc_db=DB[f"xc_{d}_s"],
              bc_db=DB[f"bc_{d}_s"], y_db=DB[f"y_{d}_s"],
              dp_ap=io[f"{d}_dp"], dp_db=DB[f"{d}_dp"], a_col_fn=acol,
              reverse=(d == "b"), name=f"s0{d}")


def emit_scan1(plan, nc, io, DB, P, seq):
    def acol(dt, n):
        return P["A1"].ap[:, n:n + 1]

    yield from emit_scan(plan, nc, io, DB, P, seq, N=N1, NG=4, TCS=L // 2,
              dl_db=DB["dl1_s"], xc_db=DB["xc1_s"], bc_db=DB["bc1_s"],
              y_db=DB["y1_s"], dp_ap=io["dp1"], dp_db=DB["dp1"],
              a_col_fn=acol, reverse=False, name="s1")


def emit_gating(plan, nc, io, DB, seq, y_db, z_db):
    with contextlib.ExitStack() as st:
        yb = [sb(nc, st, f"gt_y{j}", [128, L]) for j in range(2)]
        zb = [sb(nc, st, f"gt_z{j}", [128, L]) for j in range(2)]
        for kt in range(DIT):
            yield
            y, z = yb[kt % 2], zb[kt % 2]
            plan.dma("sp", y, y_db.ap[kt * 128:(kt + 1) * 128,
                                      seq * L:(seq + 1) * L], reads=[y_db])
            plan.dma("sp", z, z_db.ap[kt * 128:(kt + 1) * 128,
                                      seq * L:(seq + 1) * L], reads=[z_db])
            plan.op("act", lambda h, z=z: h.activation(z.ap, z.ap, AF.Silu),
                    rw=[z])
            plan.op("dve", lambda h, y=y, z=z: h.tensor_tensor(
                y.ap, y.ap, z.ap, ALU.mult), reads=[z], rw=[y])
            plan.dma("pool", y_db.ap[kt * 128:(kt + 1) * 128,
                                     seq * L:(seq + 1) * L], y,
                     reads=[y], rw=[y_db])



def emit_combine(plan, nc, io, DB, P, seq):
    with contextlib.ExitStack() as st:
        wb = [[sb(nc, st, f"cb_w{s}_{k}", [128, 128]) for k in range(2 * DT)]
              for s in range(2)]
        mb = [sb(nc, st, f"cb_m{k}", [128, TC]) for k in range(2 * DT)]
        pb = pbanks(P)
        res = [sb(nc, st, f"cb_r{j}", [128, TC]) for j in range(DT)]
        out = [sb(nc, st, f"cb_o{j}", [128, TC]) for j in range(DT)]
        cbb = sb(nc, st, "cb_bias", [128, DT])
        lns = LNScratch(plan, nc, st, "cb", P)
        plan.dma("sp", cbb, io["combb"], reads=[DB["combb"]])
        for c in range(NCH):
            yield
            col = seq * L + c * TC
            ms = c % 2
            for k in range(2 * DT):
                srcdb = DB["o_f_s"] if k < DT else DB["o_b_s"]
                row = (k if k < DT else k - DT) * 128
                plan.dma("sp", mb[k], srcdb.ap[row:row + 128, col:col + TC],
                         reads=[srcdb])
            for dt in range(DT):
                plan.dma("sp", res[dt],
                         io["x0"][dt * 128:(dt + 1) * 128, col:col + TC],
                         reads=[DB["x0"]])
            for m in range(DT):
                gi = c * DT + m
                ws = gi % 2
                for k in range(2 * DT):
                    plan.dma("sp", wb[ws][k],
                             io["combT"][k * 128:(k + 1) * 128,
                                         m * 128:(m + 1) * 128],
                             reads=[DB["combT"]])
                bank = gi % 2
                mm_group(plan, pb[bank],
                         [(wb[ws][k], wb[ws][k].ap, mb[k], mb[k].ap)
                          for k in range(2 * DT)])
                plan.op("dve", lambda h, m=m, bank=bank: h.scalar_tensor_tensor(
                    out[m].ap, res[m].ap, 2.0, pb[bank].ap, ALU.mult, ALU.add),
                    reads=[pb[bank], res[m]], writes=[out[m]])
                plan.op("dve", lambda h, m=m: h.tensor_scalar(
                    out[m].ap, out[m].ap, cbb.ap[:, m:m + 1], None, ALU.add),
                    reads=[cbb], rw=[out[m]])
            emit_postln(plan, P, lns, out,
                        lambda dt, col=col: io["x1_s"][dt * 128:(dt + 1) * 128,
                                                       col:col + TC],
                        DB["x1_s"])


def emit_ffn0(plan, nc, io, DB, P, seq):
    with contextlib.ExitStack() as st:
        wq = [sb(nc, st, f"f0_wq{k}", [128, D]) for k in range(DT)]
        ab = [sb(nc, st, f"f0_a{k}", [128, 16]) for k in range(DT)]
        bb = sb(nc, st, "f0_b", [16, D])
        bias = sb(nc, st, "f0_bias", [128, DT])
        x1 = [sb(nc, st, f"f0_x{k}", [128, TC]) for k in range(DT)]
        pu = P["par"]
        ub = sb(nc, st, "f0_u", [16, TC])
        pb = pbanks(P)
        out = [sb(nc, st, f"f0_o{j}", [128, TC]) for j in range(DT)]
        lns = LNScratch(plan, nc, st, "f0", P)
        for k in range(DT):
            plan.dma("sp", wq[k], DB["wq0_s"].ap[k * 128:(k + 1) * 128, :],
                     reads=[DB["wq0_s"]])
            plan.dma("sp", ab[k], io["ffn_a"][k * 128:(k + 1) * 128, :],
                     reads=[DB["ffn_a"]])
        plan.dma("sp", bb, io["ffn_b"], reads=[DB["ffn_b"]])
        plan.dma("sp", bias, io["ffn_bias"], reads=[DB["ffn_bias"]])
        for c in range(NCH):
            yield
            col = seq * L + c * TC
            for k in range(DT):
                plan.dma("sp", x1[k],
                         DB["x1_s"].ap[k * 128:(k + 1) * 128, col:col + TC],
                         reads=[DB["x1_s"]])
            mm_group(plan, pu, [(ab[k], ab[k].ap, x1[k], x1[k].ap)
                                for k in range(DT)], mw=16)
            plan.op("act", lambda h: h.activation(ub.ap, pu.ap[0:16, :],
                                                  AF.Copy),
                    reads=[pu], writes=[ub])
            for m in range(DT):
                bank = m % 2
                pairs = [(wq[k], wq[k].ap[:, m * 128:(m + 1) * 128], x1[k],
                          x1[k].ap) for k in range(DT)]
                pairs.append((bb, bb.ap[:, m * 128:(m + 1) * 128], ub, ub.ap))
                mm_group(plan, pb[bank], pairs)
                plan.op("dve", lambda h, m=m, bank=bank: h.scalar_tensor_tensor(
                    out[m].ap, pb[bank].ap, bias.ap[:, m:m + 1], x1[m].ap,
                    ALU.add, ALU.add),
                    reads=[pb[bank], bias, x1[m]], writes=[out[m]])
            emit_postln(plan, P, lns, out,
                        lambda dt, col=col: io["x2_s"][dt * 128:(dt + 1) * 128,
                                                       col:col + TC],
                        DB["x2_s"])


def emit_inproj1(plan, nc, io, DB, P, seq):
    with contextlib.ExitStack() as st:
        x2 = [sb(nc, st, f"i1_x{k}", [128, L]) for k in range(DT)]
        wb = [[sb(nc, st, f"i1_w{s}_{k}", [128, 128]) for k in range(DT)]
              for s in range(2)]
        pb = pbanks(P)
        eb = [sb(nc, st, f"i1_e{j}", [128, TC]) for j in range(3)]
        for k in range(DT):
            plan.dma("sp", x2[k], DB["x2_s"].ap[k * 128:(k + 1) * 128,
                                                seq * L:(seq + 1) * L],
                     reads=[DB["x2_s"]])
        mdefs = [(m * 128, 128) for m in range(33)] + [(4224, 64)]
        gi = 0
        for mi, (moff, mw) in enumerate(mdefs):
            yield
            ws = mi % 2
            for k in range(DT):
                plan.dma("sp", wb[ws][k].ap[:, 0:mw],
                         io["in1T"][k * 128:(k + 1) * 128, moff:moff + mw],
                         reads=[DB["in1T"]], writes=[wb[ws][k]])
            for c in range(NCH):
                col = seq * L + c * TC
                bank = gi % 2
                mm_group(plan, pb[bank],
                         [(wb[ws][k], wb[ws][k].ap[:, 0:mw], x2[k],
                           x2[k].ap[:, c * TC:(c + 1) * TC])
                          for k in range(DT)], mw=mw)
                e = eb[gi % 3]
                plan.op("act", lambda h, e=e, bank=bank, mw=mw: h.activation(
                    e.ap[0:mw, :], pb[bank].ap[0:mw, :], AF.Copy),
                    reads=[pb[bank]], writes=[e])
                if moff < DI:
                    dst, row = DB["xi1_s"], moff
                elif moff < 2 * DI:
                    dst, row = DB["z1_s"], moff - DI
                elif moff == 4096:
                    dst, row = DB["bc1_s"], 0
                else:
                    dst, row = DB["dtr1_s"], 0
                plan.dma("pool", dst.ap[row:row + mw, col:col + TC],
                         e.ap[0:mw, :], reads=[e], writes=[dst])
                gi += 1


def emit_outproj1_resln(plan, nc, io, DB, P, seq):
    with contextlib.ExitStack() as st:
        wb = [[sb(nc, st, f"o1_w{s}_{k}", [128, 128]) for k in range(DIT)]
              for s in range(2)]
        mb = [sb(nc, st, f"o1_m{k}", [128, TC]) for k in range(DIT)]
        pb = pbanks(P)
        out = [sb(nc, st, f"o1_o{j}", [128, TC]) for j in range(DT)]
        res = [sb(nc, st, f"o1_r{j}", [128, TC]) for j in range(DT)]
        lns = LNScratch(plan, nc, st, "o1", P)
        for c in range(NCH):
            yield
            col = seq * L + c * TC
            for k in range(DIT):
                plan.dma("sp", mb[k],
                         DB["y1_s"].ap[k * 128:(k + 1) * 128, col:col + TC],
                         reads=[DB["y1_s"]])
            for m in range(DT):
                gi = c * DT + m
                ws = gi % 2
                for k in range(DIT):
                    plan.dma("sp", wb[ws][k],
                             io["out1T"][k * 128:(k + 1) * 128,
                                         m * 128:(m + 1) * 128],
                             reads=[DB["out1T"]])
                bank = gi % 2
                mm_group(plan, pb[bank],
                         [(wb[ws][k], wb[ws][k].ap, mb[k], mb[k].ap)
                          for k in range(DIT)])
                plan.op("act", lambda h, m=m, bank=bank: h.activation(
                    out[m].ap, pb[bank].ap, AF.Copy),
                    reads=[pb[bank]], writes=[out[m]])
            for dt in range(DT):
                plan.dma("sp", res[dt],
                         DB["x2_s"].ap[dt * 128:(dt + 1) * 128, col:col + TC],
                         reads=[DB["x2_s"]])
                plan.op("dve", lambda h, dt=dt: h.tensor_tensor(
                    out[dt].ap, out[dt].ap, res[dt].ap, ALU.add),
                    reads=[res[dt]], rw=[out[dt]])
            emit_postln(plan, P, lns, out,
                        lambda dt, col=col: io["x3_s"][dt * 128:(dt + 1) * 128,
                                                       col:col + TC],
                        DB["x3_s"], maxacc=P["gmx"])


def emit_quantprep(plan, nc, io, DB, P, macc, rcp, sw, bvec, seff, beff, NFT,
                   dbgdb=None):
    with contextlib.ExitStack() as st:
        pt = P["psx"]
        pbb = P["par"]
        srow = sb(nc, st, "qp_srow", [1, 1])
        sbc = sb(nc, st, "qp_sbc", [128, 1])
        sgn = sb(nc, st, "qp_sg", [128, NFT])
        tii = sb(nc, st, "qp_ti", [128, NFT], I32)
        plan.op("pe", lambda h: h.transpose(pt.ap[0:1, 0:128], macc.ap,
                                            P["ident"].ap),
                reads=[macc, P["ident"]], writes=[pt])
        plan.op("dve", lambda h: h.reduce_max(srow.ap, pt.ap[0:1, 0:128],
                                              axis=AXX),
                reads=[pt], writes=[srow])
        plan.dma("pool", io["cc_in"], srow, reads=[srow], writes=[DB["cc_in"]])
        plan.op("pool", lambda h: h.collective_compute(
            "AllReduce", ALU.max,
            replica_groups=[[0, 1, 2, 3, 4, 5, 6, 7]],
            ins=[io["cc_in"]], outs=[io["cc_out"]]),
            reads=[DB["cc_in"]], writes=[DB["cc_out"]])
        plan.dma("pool", srow, io["cc_out"], reads=[DB["cc_out"]],
                 writes=[srow])
        plan.op("dve", lambda h: h.tensor_scalar(srow.ap, srow.ap, 1e-8,
                                                 1.0 / 255.0, ALU.max,
                                                 ALU.mult), rw=[srow])
        plan.op("pe", lambda h: h.matmul(pbb.ap[:, 0:1], P["onesc"].ap,
                                         srow.ap, start=True, stop=True),
                reads=[srow, P["onesc"]], writes=[pbb])
        plan.op("dve", lambda h: h.tensor_copy(sbc.ap, pbb.ap[:, 0:1]),
                reads=[pbb], writes=[sbc])
        plan.op("dve", lambda h: h.reciprocal(rcp.ap, sbc.ap),
                reads=[sbc], writes=[rcp])
        plan.op("dve", lambda h: h.tensor_scalar(seff.ap, sw.ap, sbc.ap[:, 0:1],
                                                 None, ALU.mult),
                reads=[sw, sbc], writes=[seff])
        rcq = sb(nc, st, "qp_rcq", [128, NFT])
        plan.op("dve", lambda h: h.reciprocal(rcq.ap, seff.ap),
                reads=[seff], writes=[rcq])
        plan.op("dve", lambda h: h.tensor_tensor(beff.ap, bvec.ap, rcq.ap,
                                                 ALU.mult),
                reads=[bvec, rcq], writes=[beff])
        sgn2 = sb(nc, st, "qp_sg2", [128, NFT])
        round_nearest(plan, beff, tii, sgn, sgn2)
        plan.op("dve", lambda h: h.tensor_tensor(beff.ap, beff.ap, seff.ap,
                                                 ALU.mult),
                reads=[seff], rw=[beff])
        if dbgdb is not None:
            dsin, dmax, dse, dbe = dbgdb
            plan.dma("pool", dsin.ap, sbc, reads=[sbc], writes=[dsin])
            plan.dma("pool", dmax.ap, macc, reads=[macc], writes=[dmax])
            plan.dma("pool", dse.ap, seff, reads=[seff], writes=[dse])
            plan.dma("pool", dbe.ap, beff, reads=[beff], writes=[dbe])


def quantize_tiles(plan, src_bufs, dst_bufs, rcp, tmp, tii, t1, t2):
    for sbuf, dbuf in zip(src_bufs, dst_bufs):
        plan.op("dve", lambda h, sbuf=sbuf: h.tensor_scalar(
            tmp.ap, sbuf.ap, rcp.ap[:, 0:1], None, ALU.mult),
            reads=[sbuf, rcp], writes=[tmp])
        plan.op("dve", lambda h: h.tensor_scalar(tmp.ap, tmp.ap, 0.0, 255.0,
                                                 ALU.max, ALU.min), rw=[tmp])
        round_nearest(plan, tmp, tii, t1, t2)
        plan.op("dve", lambda h, dbuf=dbuf: h.tensor_copy(dbuf.ap, tmp.ap),
                reads=[tmp], writes=[dbuf])


def emit_ffn1a(plan, nc, io, DB, P, seq):
    with contextlib.ExitStack() as st:
        wq = [sb(nc, st, f"fa_w{k}", [128, F1], BF16) for k in range(DT)]
        x3 = [sb(nc, st, f"fa_x{k}", [128, TC]) for k in range(DT)]
        xq = [sb(nc, st, f"fa_q{k}", [128, TC], BF16) for k in range(DT)]
        tmp = sb(nc, st, "fa_tmp", [128, TC])
        tq1 = sb(nc, st, "fa_tq1", [128, TC])
        tq2 = sb(nc, st, "fa_tq2", [128, TC])
        tii = sb(nc, st, "fa_ti", [128, TC], I32)
        pb = pbanks(P)
        eb = [sb(nc, st, f"fa_e{j}", [128, TC]) for j in range(3)]
        mxr = sb(nc, st, "fa_mx", [128, 1])
        for k in range(DT):
            plan.dma("sp", wq[k], DB["wq1T_s"].ap[k * 128:(k + 1) * 128, :],
                     reads=[DB["wq1T_s"]])
        for c in range(NCH):
            col = seq * L + c * TC
            for k in range(DT):
                plan.dma("sp", x3[k],
                         DB["x3_s"].ap[k * 128:(k + 1) * 128, col:col + TC],
                         reads=[DB["x3_s"]])
            quantize_tiles(plan, x3, xq, P["rcp1"], tmp, tii, tq1, tq2)
            for m in range(F1T):
                bank = m % 2
                mm_group(plan, pb[bank],
                         [(wq[k], wq[k].ap[:, m * 128:(m + 1) * 128], xq[k],
                           xq[k].ap) for k in range(DT)])
                e = eb[m % 3]
                plan.op("act", lambda h, e=e, bank=bank, m=m: h.activation(
                    e.ap, pb[bank].ap, AF.Relu,
                    scale=P["seff1"].ap[:, m:m + 1],
                    bias=P["beff1"].ap[:, m:m + 1]),
                    reads=[pb[bank], P["seff1"], P["beff1"]], writes=[e])
                plan.op("dve", lambda h, e=e: h.reduce_max(mxr.ap, e.ap,
                                                           axis=AXX),
                        reads=[e], writes=[mxr])
                plan.op("dve", lambda h: h.tensor_max(P["ghx"].ap, P["ghx"].ap,
                                                      mxr.ap),
                        reads=[mxr], rw=[P["ghx"]])
                plan.dma("pool", DB["h_s"].ap[m * 128:(m + 1) * 128,
                                              col:col + TC], e,
                         reads=[e], writes=[DB["h_s"]])


def emit_ffn1b(plan, nc, io, DB, P, seq):
    with contextlib.ExitStack() as st:
        wq = [sb(nc, st, f"fb_w{k}", [128, D], BF16) for k in range(F1T)]
        hb = [sb(nc, st, f"fb_h{j}", [128, TC]) for j in range(2)]
        hq = [sb(nc, st, f"fb_q{k}", [128, TC], BF16) for k in range(F1T)]
        tmp = sb(nc, st, "fb_tmp", [128, TC])
        tq1 = sb(nc, st, "fb_tq1", [128, TC])
        tq2 = sb(nc, st, "fb_tq2", [128, TC])
        tii = sb(nc, st, "fb_ti", [128, TC], I32)
        pb = pbanks(P)
        out = [sb(nc, st, f"fb_o{j}", [128, TC]) for j in range(DT)]
        res = [sb(nc, st, f"fb_r{j}", [128, TC]) for j in range(DT)]
        lns = LNScratch(plan, nc, st, "fb", P)
        for k in range(F1T):
            plan.dma("sp", wq[k], DB["wq2T_s"].ap[k * 128:(k + 1) * 128, :],
                     reads=[DB["wq2T_s"]])
        for c in range(NCH):
            col = seq * L + c * TC
            for k in range(F1T):
                b = hb[k % 2]
                plan.dma("sp", b,
                         DB["h_s"].ap[k * 128:(k + 1) * 128, col:col + TC],
                         reads=[DB["h_s"]])
                quantize_tiles(plan, [b], [hq[k]], P["rcp2"], tmp, tii, tq1, tq2)
            for m in range(DT):
                bank = m % 2
                mm_group(plan, pb[bank],
                         [(wq[k], wq[k].ap[:, m * 128:(m + 1) * 128], hq[k],
                           hq[k].ap) for k in range(F1T)])
                plan.op("dve", lambda h, m=m, bank=bank: h.tensor_scalar(
                    out[m].ap, pb[bank].ap, P["seff2"].ap[:, m:m + 1],
                    P["beff2"].ap[:, m:m + 1], ALU.mult, ALU.add),
                    reads=[pb[bank], P["seff2"], P["beff2"]], writes=[out[m]])
                plan.dma("sp", res[m],
                         DB["x3_s"].ap[m * 128:(m + 1) * 128, col:col + TC],
                         reads=[DB["x3_s"]])
                plan.op("dve", lambda h, m=m: h.tensor_tensor(
                    out[m].ap, out[m].ap, res[m].ap, ALU.add),
                    reads=[res[m]], rw=[out[m]])
            emit_postln(plan, P, lns, out,
                        lambda dt, col=col: io["xout"][dt * 128:(dt + 1) * 128,
                                                       col:col + TC],
                        DB["xout"])



def make_inputs(core, A):
    t = lambda x: np.ascontiguousarray(np.asarray(x, np.float32).T)
    col = lambda x: np.ascontiguousarray(np.asarray(x, np.float32).reshape(-1, 1))
    colk = lambda x, k: np.ascontiguousarray(
        np.asarray(x, np.float32).reshape(k, 128).T)
    m = {}
    x = np.asarray(A["L0_attn_x"][2 * core:2 * core + 2], np.float32)
    m["x0"] = np.ascontiguousarray(np.transpose(x, (2, 0, 1)).reshape(D, NT))
    m["ln_g"] = np.ascontiguousarray(np.asarray(A["L0_attn_ln_g"], np.float32).reshape(1, D))
    m["ln_b"] = np.ascontiguousarray(np.asarray(A["L0_attn_ln_b"], np.float32).reshape(1, D))
    for d in ("f", "b"):
        m[f"{d}_inT"] = t(A[f"L0_attn_{d}_in_w"])
        m[f"{d}_convw"] = np.ascontiguousarray(np.asarray(
            A[f"L0_attn_{d}_conv_w"], np.float32).reshape(DIT, 128, 4)
            .transpose(1, 0, 2).reshape(128, DIT * 4))
        m[f"{d}_convb"] = colk(A[f"L0_attn_{d}_conv_b"], DIT)
        m[f"{d}_xwT"] = t(A[f"L0_attn_{d}_x_w"])
        m[f"{d}_dtwT"] = t(A[f"L0_attn_{d}_dt_w"])
        m[f"{d}_dtb"] = colk(A[f"L0_attn_{d}_dt_b"], DIT)
        m[f"{d}_alog"] = np.ascontiguousarray(np.asarray(
            A[f"L0_attn_{d}_A_log"], np.float32))
        m[f"{d}_dp"] = colk(A[f"L0_attn_{d}_D"], DIT)
        m[f"{d}_outT"] = t(A[f"L0_attn_{d}_out_w"])
    m["combT"] = t(A["L0_attn_combine_w"])
    m["combb"] = colk(A["L0_attn_combine_b"], DT)
    m["ffn_k"] = np.ascontiguousarray(np.asarray(A["L0_ffn_kernel"], np.float32))
    m["ffn_bias"] = colk(A["L0_ffn_bias"], DT)
    m["ffn_a"] = np.ascontiguousarray(np.asarray(A["L0_ffn_a"], np.float32))
    m["ffn_b"] = np.ascontiguousarray(np.asarray(A["L0_ffn_b"], np.float32))
    m["in1T"] = t(A["L1_attn_in_proj_w"])
    m["conv1w"] = np.ascontiguousarray(np.asarray(
        A["L1_attn_conv_w"], np.float32).reshape(DIT, 128, 4)
        .transpose(1, 0, 2).reshape(128, DIT * 4))
    m["conv1b"] = colk(A["L1_attn_conv_b"], DIT)
    m["alog1b"] = np.ascontiguousarray(np.broadcast_to(
        np.asarray(A["L1_attn_A_log"], np.float32)[None, :], (128, N1)).copy())
    m["dp1"] = colk(A["L1_attn_D_param"], DIT)
    m["dt1T"] = t(A["L1_attn_dt_proj_w"])
    m["dt1b"] = colk(A["L1_attn_dt_proj_b"], DIT)
    m["out1T"] = t(A["L1_attn_out_proj_w"])
    m["w1f"] = np.ascontiguousarray(np.asarray(A["L1_ffn_w1"], np.float32))
    m["w1T"] = t(A["L1_ffn_w1"])
    m["b1"] = np.ascontiguousarray(
        np.asarray(A["L1_ffn_b1"], np.float32).reshape(F1T, 128).T)
    m["w2f"] = np.ascontiguousarray(np.asarray(A["L1_ffn_w2"], np.float32))
    m["w2T"] = t(A["L1_ffn_w2"])
    m["b2"] = np.ascontiguousarray(
        np.asarray(A["L1_ffn_b2"], np.float32).reshape(DT, 128).T)
    m["ident"] = np.eye(128, dtype=np.float32)
    return m


def kernel(**inputs):
    nc = build_program()
    in_maps = [make_inputs(c, inputs) for c in range(NCORE)]
    res = run_bass_kernel_spmd(nc, in_maps, core_ids=list(range(NCORE)))
    outs = []
    for c in range(NCORE):
        o = np.asarray(res.results[c]["xout"])
        outs.append(np.transpose(o.reshape(D, BSEQ, L), (1, 2, 0)))
    return np.concatenate(outs, axis=0).astype(np.float32)


# version 12
# speedup vs baseline: 1.0899x; 1.0899x over previous
import contextlib

import numpy as np

import concourse.bass as bass
import concourse.mybir as mybir
from concourse.bass_utils import run_bass_kernel_spmd

F32 = mybir.dt.float32
BF16 = mybir.dt.bfloat16
I32 = mybir.dt.int32
AF = mybir.ActivationFunctionType
ALU = mybir.AluOpType
AXX = mybir.AxisListType.X

NCORE = 8
BSEQ = 2
L = 2048
NT = BSEQ * L
D = 1024
DI = 2048
N0 = 16
N1 = 64
R0 = 64
F1 = 4096
TC = 512
NCH = L // TC
DT = D // 128
DIT = DI // 128
F1T = F1 // 128

ENGS = ("pe", "act", "dve", "pool", "sp")
ENG2SEC = {"pe": "tensor", "act": "scalar", "dve": "vector", "pool": "gpsimd",
           "sp": "sync"}
ESEM = {e: "s_" + e for e in ENGS}


class Buf:
    __slots__ = ("ap", "writers", "readers", "name")

    def __init__(self, ap, name=""):
        self.ap = ap
        self.writers = []
        self.readers = []
        self.name = name


class Op:
    __slots__ = ("eng", "fn", "deps", "seq", "queue", "fence", "is_fence",
                 "qsem")


QENG = {"d_sp": "sp", "d_pool": "pool", "d_act": "act"}


class Plan:

    def __init__(self, nc):
        self.nc = nc
        self.ops = {e: [] for e in ENGS}
        self.ntok = {e: 0 for e in ENGS}
        self.qgroup = {q: [] for q in QENG}
        self.nfence = {q: 0 for q in QENG}
        self.qlastfence = {q: None for q in QENG}
        self.qmap = {"sp": "sp", "pool": "pool"}
        self.nop_sem = None
        self.guards = {}
        self.cur_guard = ()

    def _resolve(self, d):
        if isinstance(d, tuple) or not getattr(d, "queue", None):
            return d
        if d.fence is None:
            self.fence(d.queue)
        return d.fence

    def snapshot(self):
        return [ops[-1] for ops in self.ops.values() if ops]

    def begin_stage(self, side, use_psum=True):
        g = list(self.guards.get(side, ()))
        if use_psum:
            g += list(self.guards.get("psum", ()))
        self.cur_guard = tuple(g)
        self._stage_psum = use_psum

    def end_stage(self, side):
        snap = self.snapshot()
        self.guards[side] = snap
        if self._stage_psum:
            self.guards["psum"] = snap
        self.cur_guard = ()

    def end_stage_tails(self, side, tails, use_psum):
        self.guards[side] = list(tails.values())
        if use_psum:
            self.guards["psum"] = list(self.guards.get("psum", ())) \
                if False else list(tails.values())
        self.cur_guard = ()

    def op(self, eng, fn, deps=(), reads=(), writes=(), rw=(), queue=None,
           is_fence=False):
        dd = []
        for d in deps:
            if d is not None:
                dd.append(d)
        if not is_fence:
            dd.extend(self.cur_guard)
        for b in reads:
            dd.extend(b.writers)
        for b in rw:
            dd.extend(b.writers)
            dd.extend(b.readers)
        for b in writes:
            dd.extend(b.writers)
            dd.extend(b.readers)
        if not is_fence:
            dd = [self._resolve(d) for d in dd]
        o = Op()
        o.eng = eng
        o.fn = fn
        o.deps = dd
        o.queue = queue
        o.fence = None
        o.qsem = None
        o.is_fence = is_fence
        if queue is None and not is_fence:
            self.ntok[eng] += 1
            o.seq = self.ntok[eng]
        else:
            o.seq = None
        self.ops[eng].append(o)
        if queue is not None:
            self.qgroup[queue].append(o)
        for b in reads:
            b.readers.append(o)
        for b in rw:
            b.writers = [o]
            b.readers = []
        for b in writes:
            b.writers = [o]
            b.readers = []
        return o

    def fence(self, queue):
        grp = self.qgroup[queue]
        if not grp:
            return self.qlastfence[queue]
        self.qgroup[queue] = []
        rot = self.nfence[queue] % 4
        qsem = f"{queue}{rot}"
        for d in grp:
            d.qsem = qsem
        f = self.op(QENG[queue], ("FENCE", qsem), is_fence=True)
        self.nfence[queue] += 1
        f.seq = ("f_" + queue[2:], self.nfence[queue])
        for d in grp:
            d.fence = f
        self.qlastfence[queue] = f
        return f

    def dma(self, eng, dst, src, reads=(), writes=(), rw=(), deps=(), **kw):
        if "accum_op" in kw:
            eng = "pool"
        else:
            eng = self.qmap.get(eng, eng)
        q = "d_" + eng
        dap = dst.ap if isinstance(dst, Buf) else dst
        sap = src.ap if isinstance(src, Buf) else src
        rd = list(reads) + ([src] if isinstance(src, Buf) else [])
        wr = list(writes) + ([dst] if isinstance(dst, Buf) and not rw else [])
        return self.op(eng, lambda h: h.dma_start(out=dap, in_=sap, **kw),
                       deps=deps, reads=rd, writes=wr, rw=rw, queue=q)

    def barrier(self):
        for q in QENG:
            self.fence(q)
        tails = [ops[-1] for ops in self.ops.values() if ops]
        bar = []
        for e in ENGS:
            bar.append(self.op(e, None, deps=tails))
        tails2 = list(bar)
        for e in ENGS:
            self.op(e, None, deps=tails2)

    def token(self, o):
        if isinstance(o, tuple):
            return o
        assert o.seq is not None, "dma token must resolve through its fence"
        if isinstance(o.seq, tuple):
            return o.seq
        return (ESEM[o.eng], o.seq)

    def emit(self, ctx):
        qtotal = {}
        for q in QENG:
            for r in range(4):
                qtotal[f"{q}{r}"] = [0]
        for eng in ENGS:
            ops = self.ops[eng]
            if not ops:
                continue

            def body(h, eng=eng, ops=ops):
                hw = ctx.waited[eng]
                own = ESEM[eng]
                for o in ops:
                    need = {}
                    for d in o.deps:
                        s, v = self.token(d)
                        if need.get(s, 0) < v:
                            need[s] = v
                    for s in sorted(need):
                        v = need[s]
                        if hw.get(s, 0) >= v:
                            continue
                        h.wait_ge(ctx.sem[s], v)
                        hw[s] = v
                    if isinstance(o.fn, tuple):
                        qs = o.fn[1]
                        h.wait_ge(ctx.sem[qs], qtotal[qs][0])
                        h.sem_inc(ctx.sem[o.seq[0]], 1)
                    elif o.fn is None:
                        h.sem_inc(ctx.sem[own], 1)
                    else:
                        inst = o.fn(h)
                        if o.queue is not None:
                            inst.then_inc(ctx.sem[o.qsem], 16)
                            qtotal[o.qsem][0] += 16
                        else:
                            inst.then_inc(ctx.sem[own], 1)
            getattr(ctx.block, ENG2SEC[eng])(body)


class Ctx:
    def __init__(self, nc, block):
        self.nc = nc
        self.block = block
        self.sem = {}
        names = [ESEM[e] for e in ENGS] + ["f_sp", "f_pool", "f_act"]
        for q in ("d_sp", "d_pool", "d_act"):
            names += [f"{q}{r}" for r in range(4)]
        for name in names:
            self.sem[name] = nc.semaphore(name).__enter__()
        self.waited = {e: {} for e in ENGS}



class StageRun:

    def __init__(self, plan, side, genfn, nslice, cost):
        self.plan = plan
        self.side = side
        self.guard = tuple(plan.guards.get(side, ()))
        self.gen = genfn()
        self.tails = {}
        self.done = False
        self.nslice = max(nslice, 1)
        self.cps = cost / max(nslice, 1)

    def resume(self):
        p = self.plan
        _CUR_SIDE[0] = self.side
        p.cur_guard = self.guard
        before = {e: len(p.ops[e]) for e in ENGS}
        try:
            next(self.gen)
        except StopIteration:
            self.done = True
        for e in ENGS:
            if len(p.ops[e]) > before[e]:
                self.tails[e] = p.ops[e][-1]
        p.cur_guard = ()
        p.qmap = {"sp": "sp", "pool": "pool"}
        _CUR_SIDE[0] = "left"

    def finish(self):
        self.plan.guards[self.side] = list(self.tails.values())


def lane_exec(plan, lanes):
    state = [{"q": list(lane), "cur": None, "acc": 0.0} for lane in lanes]
    while True:
        cands = [s for s in state if s["cur"] is not None or s["q"]]
        if not cands:
            break
        s = min(cands, key=lambda s: s["acc"])
        if s["cur"] is None:
            side, nslice, cost, genfn = s["q"].pop(0)
            import os
            if os.environ.get("KPRINT"):
                print("stage", side, genfn, "sbuf_free",
                      plan.nc.sbuf_bytes_remaining)
            s["cur"] = StageRun(plan, side, genfn, nslice, cost)
        r = s["cur"]
        r.resume()
        s["acc"] += r.cps
        if r.done:
            r.finish()
            s["cur"] = None



def build_program(debug=(), upto=10 ** 9):
    nc = bass.Bass("TRN2", target_bir_lowering=False)
    dbg = set(debug)

    def din(name, shape, dtype=F32):
        return nc.dram_tensor(name, shape, dtype, kind="ExternalInput").ap()

    def dscr(name, shape, dtype=F32):
        kind = "ExternalOutput" if name in dbg else "Internal"
        return nc.dram_tensor(name, shape, dtype, kind=kind).ap()

    io = {}
    io["x0"] = din("x0", [D, NT])
    io["ln_g"] = din("ln_g", [1, D])
    io["ln_b"] = din("ln_b", [1, D])
    for d in ("f", "b"):
        io[f"{d}_inT"] = din(f"{d}_inT", [D, 2 * DI])
        io[f"{d}_convw"] = din(f"{d}_convw", [128, DIT * 4])
        io[f"{d}_convb"] = din(f"{d}_convb", [128, DIT])
        io[f"{d}_xwT"] = din(f"{d}_xwT", [DI, R0 + 2 * N0])
        io[f"{d}_dtwT"] = din(f"{d}_dtwT", [R0, DI])
        io[f"{d}_dtb"] = din(f"{d}_dtb", [128, DIT])
        io[f"{d}_alog"] = din(f"{d}_alog", [DI, N0])
        io[f"{d}_dp"] = din(f"{d}_dp", [128, DIT])
        io[f"{d}_outT"] = din(f"{d}_outT", [DI, D])
    io["combT"] = din("combT", [2 * D, D])
    io["combb"] = din("combb", [128, DT])
    io["ffn_k"] = din("ffn_k", [D, D])
    io["ffn_bias"] = din("ffn_bias", [128, DT])
    io["ffn_a"] = din("ffn_a", [D, 16])
    io["ffn_b"] = din("ffn_b", [16, D])
    io["in1T"] = din("in1T", [D, 4288])
    io["conv1w"] = din("conv1w", [128, DIT * 4])
    io["conv1b"] = din("conv1b", [128, DIT])
    io["alog1b"] = din("alog1b", [128, N1])
    io["dp1"] = din("dp1", [128, DIT])
    io["dt1T"] = din("dt1T", [R0, DI])
    io["dt1b"] = din("dt1b", [128, DIT])
    io["out1T"] = din("out1T", [DI, D])
    io["w1f"] = din("w1f", [F1, D])
    io["w1T"] = din("w1T", [D, F1])
    io["b1"] = din("b1", [128, F1T])
    io["w2f"] = din("w2f", [D, F1])
    io["w2T"] = din("w2T", [F1, D])
    io["b2"] = din("b2", [128, DT])
    io["ident"] = din("ident", [128, 128])

    io["xout"] = nc.dram_tensor("xout", [D, NT], F32, kind="ExternalOutput").ap()

    io["xn_s"] = dscr("xn_s", [D, NT])
    for d in ("f", "b"):
        io[f"xi_{d}_s"] = dscr(f"xi_{d}_s", [DI, NT])
        io[f"z_{d}_s"] = dscr(f"z_{d}_s", [DI, NT])
        io[f"xc_{d}_s"] = dscr(f"xc_{d}_s", [DI, NT])
        io[f"dl_{d}_s"] = dscr(f"dl_{d}_s", [DI, NT])
        io[f"y_{d}_s"] = dscr(f"y_{d}_s", [DI, NT])
        io[f"bc_{d}_s"] = dscr(f"bc_{d}_s", [2 * N0, NT])
        io[f"dtr_{d}_s"] = dscr(f"dtr_{d}_s", [R0, NT])
        io[f"o_{d}_s"] = dscr(f"o_{d}_s", [D, NT])
    io["x1_s"] = dscr("x1_s", [D, NT])
    io["x2_s"] = dscr("x2_s", [D, NT])
    io["x3_s"] = dscr("x3_s", [D, NT])
    io["wq0_s"] = dscr("wq0_s", [D, D])
    io["xi1_s"] = dscr("xi1_s", [DI, NT])
    io["z1_s"] = dscr("z1_s", [DI, NT])
    io["xc1_s"] = dscr("xc1_s", [DI, NT])
    io["dl1_s"] = dscr("dl1_s", [DI, NT])
    io["y1_s"] = dscr("y1_s", [DI, NT])
    io["bc1_s"] = dscr("bc1_s", [2 * N1, NT])
    io["dtr1_s"] = dscr("dtr1_s", [R0, NT])
    io["wq1T_s"] = dscr("wq1T_s", [D, F1], BF16)
    io["wq2T_s"] = dscr("wq2T_s", [F1, D], BF16)
    io["s1_s"] = dscr("s1_s", [F1, 1])
    io["s2_s"] = dscr("s2_s", [D, 1])
    io["h_s"] = dscr("h_s", [F1, NT])
    for tag, n in (("1", F1T), ("2", DT)):
        io[f"qd{tag}_sin"] = dscr(f"qd{tag}_sin", [128, 1])
        io[f"qd{tag}_max"] = dscr(f"qd{tag}_max", [128, 1])
        io[f"qd{tag}_se"] = dscr(f"qd{tag}_se", [128, n])
        io[f"qd{tag}_be"] = dscr(f"qd{tag}_be", [128, n])
    io["cc_in"] = nc.dram_tensor("cc_in", [1, 1], F32, kind="Internal").ap()
    io["cc_out"] = nc.dram_tensor("cc_out", [1, 1], F32, kind="Internal").ap()

    DB = {k: Buf(v, k) for k, v in io.items()}

    P = {}

    def psb(name, shape, dtype=F32):
        P[name] = Buf(nc.alloc_sbuf_tensor("p_" + name, shape, dtype).ap(), name)
        return P[name]

    psb("onesc", [1, 128])
    psb("onesr", [1, TC])
    psb("ones1", [128, 1])
    psb("ident", [128, 128])
    psb("A0f", [128, DIT * N0])
    psb("A0b", [128, DIT * N0])
    psb("A1", [128, N1])
    psb("sw1", [128, F1T])
    psb("sw2", [128, DT])
    psb("b1", [128, F1T])
    psb("b2", [128, DT])
    psb("rcp1", [128, 1])
    psb("seff1", [128, F1T])
    psb("beff1", [128, F1T])
    psb("rcp2", [128, 1])
    psb("seff2", [128, DT])
    psb("beff2", [128, DT])
    psb("gmx", [128, 1])
    psb("ghx", [128, 1])
    for j in range(4):
        P[f"pb{j}"] = Buf(nc.alloc_psum_tensor(f"p_pb{j}", [128, TC], F32).ap(),
                          f"pb{j}")
    P["psx"] = Buf(nc.alloc_psum_tensor("p_psx", [1, TC], F32).ap(), "psx")
    P["psq"] = Buf(nc.alloc_psum_tensor("p_psq", [1, TC], F32).ap(), "psq")
    P["par"] = Buf(nc.alloc_psum_tensor("p_par", [128, TC], F32).ap(), "par")
    P["pbr"] = Buf(nc.alloc_psum_tensor("p_pbr", [128, TC], F32).ap(), "pbr")

    with nc.Block() as block:
        ctx = Ctx(nc, block)
        plan = Plan(nc)
        plan.nop_sem = nc.semaphore("nop").__enter__()
        emit_all(plan, nc, io, DB, P, upto=upto)
        plan.emit(ctx)
    return nc


_UNIQ = [0]
_CUR_SIDE = ["left"]


def pbanks(P):
    if _CUR_SIDE[0] == "right":
        return [P["pb2"], P["pb3"]]
    return [P["pb0"], P["pb1"]]


def sb(nc, st, name, shape, dtype=F32):
    _UNIQ[0] += 1
    return Buf(st.enter_context(
        nc.sbuf_tensor(f"{name}_{_UNIQ[0]}", shape, dtype,
                       side=_CUR_SIDE[0])).ap(), name)


def ps(nc, st, name, shape, dtype=F32):
    _UNIQ[0] += 1
    return Buf(st.enter_context(
        nc.psum_tensor(f"{name}_{_UNIQ[0]}", shape, dtype)).ap(), name)



def emit_all(plan, nc, io, DB, P, upto=10 ** 9):
    emit_prep(plan, nc, io, DB, P)
    plan.barrier()
    emit_prep_wq0(plan, nc, io, DB, P)
    plan.barrier()
    emit_prep_quantw(plan, nc, io, DB, P, "w1f", "w1T", F1, D, P["sw1"],
                     DB["wq1T_s"], DB["s1_s"], F1T)
    plan.barrier()
    emit_prep_quantw(plan, nc, io, DB, P, "w2f", "w2T", D, F1, P["sw2"],
                     DB["wq2T_s"], DB["s2_s"], DT)
    plan.barrier()
    for seq in range(BSEQ):
        emit_preln(plan, nc, io, DB, P, seq)
        plan.barrier()

    def chain(seq, d):
        return [
            ("x", 32, 1.0, lambda: emit_inproj0(plan, nc, io, DB, P, seq, d)),
            ("x", DIT, 0.45, lambda: emit_conv(
                plan, nc, io, DB, seq, DB[f"xi_{d}_s"], DB[f"xc_{d}_s"],
                io[f"{d}_convw"], io[f"{d}_convb"], DB[f"{d}_convw"],
                DB[f"{d}_convb"], fwd=(d == "f"))),
            ("x", NCH, 0.3, lambda: emit_xw0(plan, nc, io, DB, P, seq, d)),
            ("x", DIT, 0.5, lambda: emit_dtproj(
                plan, nc, io, DB, P, seq, io[f"{d}_dtwT"], DB[f"{d}_dtwT"],
                io[f"{d}_dtb"], DB[f"{d}_dtb"], DB[f"dtr_{d}_s"],
                DB[f"dl_{d}_s"])),
            ("x", 8, 3.2, lambda: emit_scan0(plan, nc, io, DB, P, seq, d)),
            ("x", DIT, 0.3, lambda: emit_gating(
                plan, nc, io, DB, seq, DB[f"y_{d}_s"], DB[f"z_{d}_s"])),
            ("x", NCH, 0.6, lambda: emit_proj_simple(
                plan, nc, io, DB, P, seq, io[f"{d}_outT"], DB[f"{d}_outT"],
                DB[f"y_{d}_s"], DIT, DT, DB[f"o_{d}_s"], name=f"op0{seq}{d}")),
        ]

    def l1chain(seq):
        return [
            ("x", 34, 1.2, lambda: emit_inproj1(plan, nc, io, DB, P, seq)),
            ("x", DIT, 0.45, lambda: emit_conv(
                plan, nc, io, DB, seq, DB["xi1_s"], DB["xc1_s"],
                io["conv1w"], io["conv1b"], DB["conv1w"], DB["conv1b"],
                fwd=True)),
            ("x", DIT, 0.5, lambda: emit_dtproj(
                plan, nc, io, DB, P, seq, io["dt1T"], DB["dt1T"],
                io["dt1b"], DB["dt1b"], DB["dtr1_s"], DB["dl1_s"])),
            ("x", 8, 6.4, lambda: emit_scan1(plan, nc, io, DB, P, seq)),
            ("x", DIT, 0.3, lambda: emit_gating(
                plan, nc, io, DB, seq, DB["y1_s"], DB["z1_s"])),
            ("x", NCH, 1.0, lambda: emit_outproj1_resln(plan, nc, io, DB, P,
                                                        seq)),
        ]

    def atomgen(fn):
        def g():
            fn()
            return
            yield
        return g

    laneL = (chain(0, "f") + chain(1, "f")
             + [("x", NCH, 0.8, lambda: emit_combine(plan, nc, io, DB, P, 0)),
                ("x", NCH, 0.7, lambda: emit_ffn0(plan, nc, io, DB, P, 0))]
             + l1chain(0))
    laneR = (chain(0, "b") + chain(1, "b")
             + [("x", NCH, 0.8, lambda: emit_combine(plan, nc, io, DB, P, 1)),
                ("x", NCH, 0.7, lambda: emit_ffn0(plan, nc, io, DB, P, 1))]
             + l1chain(1))
    laneL = [("left", n, c, g) for (_, n, c, g) in laneL]
    laneR = [("right", n, c, g) for (_, n, c, g) in laneR]
    lane_exec(plan, [laneL, laneR])
    plan.barrier()

    emit_quantprep(plan, nc, io, DB, P, P["gmx"], P["rcp1"], P["sw1"], P["b1"],
                   P["seff1"], P["beff1"], F1T,
                   (DB["qd1_sin"], DB["qd1_max"], DB["qd1_se"], DB["qd1_be"]))
    plan.barrier()
    for seq in range(BSEQ):
        emit_ffn1a(plan, nc, io, DB, P, seq)
        plan.barrier()
    emit_quantprep(plan, nc, io, DB, P, P["ghx"], P["rcp2"], P["sw2"], P["b2"],
                   P["seff2"], P["beff2"], DT,
                   (DB["qd2_sin"], DB["qd2_max"], DB["qd2_se"], DB["qd2_be"]))
    plan.barrier()
    for seq in range(BSEQ):
        emit_ffn1b(plan, nc, io, DB, P, seq)
        plan.barrier()



def emit_prep(plan, nc, io, DB, P):
    plan.op("dve", lambda h: h.memset(P["onesc"].ap, 1.0), writes=[P["onesc"]])
    plan.op("dve", lambda h: h.memset(P["onesr"].ap, 1.0), writes=[P["onesr"]])
    plan.op("dve", lambda h: h.memset(P["ones1"].ap, 1.0), writes=[P["ones1"]])
    plan.op("dve", lambda h: h.memset(P["gmx"].ap, -1e30), writes=[P["gmx"]])
    plan.op("dve", lambda h: h.memset(P["ghx"].ap, -1e30), writes=[P["ghx"]])
    plan.dma("sp", P["ident"], io["ident"], reads=[DB["ident"]])
    plan.dma("sp", P["b1"], io["b1"], reads=[DB["b1"]])
    plan.dma("sp", P["b2"], io["b2"], reads=[DB["b2"]])
    with contextlib.ExitStack() as st:
        at = sb(nc, st, "prep_at", [128, N0])
        for d in ("f", "b"):
            dst = P["A0f"] if d == "f" else P["A0b"]
            for dt in range(DIT):
                plan.dma("sp", at, io[f"{d}_alog"][dt * 128:(dt + 1) * 128, :],
                         reads=[DB[f"{d}_alog"]])
                plan.op("act",
                        lambda h, dt=dt, dst=dst: h.activation(
                            dst.ap[:, dt * N0:(dt + 1) * N0], at.ap, AF.Exp),
                        reads=[at], rw=[dst])
    plan.dma("sp", P["A1"], io["alog1b"], reads=[DB["alog1b"]])
    plan.op("act", lambda h: h.activation(P["A1"].ap, P["A1"].ap, AF.Exp),
            rw=[P["A1"]])


def round_nearest(plan, buf, ibuf, t1, t2):
    plan.op("dve", lambda h: h.tensor_copy(ibuf.ap, buf.ap),
            reads=[buf], writes=[ibuf])
    plan.op("dve", lambda h: h.tensor_copy(t1.ap, ibuf.ap),
            reads=[ibuf], writes=[t1])
    plan.op("dve", lambda h: h.tensor_tensor(buf.ap, buf.ap, t1.ap,
                                             ALU.subtract),
            reads=[t1], rw=[buf])
    plan.op("dve", lambda h: h.tensor_scalar(t2.ap, buf.ap, 0.5, None,
                                             ALU.is_gt),
            reads=[buf], writes=[t2])
    plan.op("dve", lambda h: h.tensor_scalar(buf.ap, buf.ap, -0.5, None,
                                             ALU.is_lt), rw=[buf])
    plan.op("dve", lambda h: h.tensor_tensor(t1.ap, t1.ap, t2.ap, ALU.add),
            reads=[t2], rw=[t1])
    plan.op("dve", lambda h: h.tensor_tensor(buf.ap, t1.ap, buf.ap,
                                             ALU.subtract),
            reads=[t1], rw=[buf])


def emit_prep_wq0(plan, nc, io, DB, P):
    with contextlib.ExitStack() as st:
        t = [sb(nc, st, f"wq_t{j}", [128, D]) for j in range(2)]
        sg = sb(nc, st, "wq_sg", [128, D])
        ti = sb(nc, st, "wq_ti", [128, D], I32)
        mx = sb(nc, st, "wq_mx", [128, 1])
        mx2 = sb(nc, st, "wq_mx2", [128, 1])
        sc = sb(nc, st, "wq_sc", [128, 1])
        rsc = sb(nc, st, "wq_rsc", [128, 1])
        srow = sb(nc, st, "wq_srow", [1, 1])
        pt = P["psx"]
        pb = P["par"]
        k = io["ffn_k"]
        for dt in range(DT):
            b = t[dt % 2]
            plan.dma("sp", b, k[dt * 128:(dt + 1) * 128, :], reads=[DB["ffn_k"]])
            if dt == 0:
                plan.op("dve", lambda h: h.reduce_max(
                    mx.ap, t[0].ap, axis=AXX, apply_absolute_value=True),
                    reads=[t[0]], writes=[mx])
            else:
                plan.op("dve", lambda h, b=b: h.reduce_max(
                    mx2.ap, b.ap, axis=AXX, apply_absolute_value=True),
                    reads=[b], writes=[mx2])
                plan.op("dve", lambda h: h.tensor_max(mx.ap, mx.ap, mx2.ap),
                        reads=[mx2], rw=[mx])
        plan.op("pe", lambda h: h.transpose(pt.ap[0:1, 0:128], mx.ap,
                                            P["ident"].ap),
                reads=[mx, P["ident"]], writes=[pt])
        plan.op("dve", lambda h: h.reduce_max(srow.ap, pt.ap[0:1, 0:128],
                                              axis=AXX),
                reads=[pt], writes=[srow])
        plan.op("dve", lambda h: h.tensor_scalar(srow.ap, srow.ap, 1.0 / 7.0,
                                                 None, ALU.mult), rw=[srow])
        plan.op("pe", lambda h: h.matmul(pb.ap[:, 0:1], P["onesc"].ap,
                                         srow.ap, start=True, stop=True),
                reads=[srow, P["onesc"]], writes=[pb])
        plan.op("dve", lambda h: h.tensor_copy(sc.ap, pb.ap[:, 0:1]),
                reads=[pb], writes=[sc])
        plan.op("dve", lambda h: h.reciprocal(rsc.ap, sc.ap),
                reads=[sc], writes=[rsc])
        sg2 = sb(nc, st, "wq_sg2", [128, D])
        for dt in range(DT):
            b = t[dt % 2]
            plan.dma("sp", b, k[dt * 128:(dt + 1) * 128, :], reads=[DB["ffn_k"]])
            plan.op("dve", lambda h, b=b: h.tensor_scalar(
                b.ap, b.ap, rsc.ap[:, 0:1], None, ALU.mult),
                reads=[rsc], rw=[b])
            round_nearest(plan, b, ti, sg, sg2)
            plan.op("dve", lambda h, b=b: h.tensor_scalar(
                b.ap, b.ap, -8.0, 7.0, ALU.max, ALU.min), rw=[b])
            plan.op("dve", lambda h, b=b: h.tensor_scalar(
                b.ap, b.ap, sc.ap[:, 0:1], None, ALU.mult),
                reads=[sc], rw=[b])
            plan.dma("pool", io["wq0_s"][dt * 128:(dt + 1) * 128, :], b,
                     writes=[DB["wq0_s"]])


def emit_prep_quantw(plan, nc, io, DB, P, wf_name, wT_name, NF, ND, s_sb,
                     outdb, srowdb, NFT):
    with contextlib.ExitStack() as st:
        t = [sb(nc, st, f"q_t{j}{wf_name}", [128, ND]) for j in range(2)]
        mx = sb(nc, st, f"q_mx{wf_name}", [128, 1])
        wf = io[wf_name]
        for m in range(NFT):
            b = t[m % 2]
            plan.dma("sp", b, wf[m * 128:(m + 1) * 128, :], reads=[DB[wf_name]])
            plan.op("dve", lambda h, b=b: h.reduce_max(
                mx.ap, b.ap, axis=AXX, apply_absolute_value=True),
                reads=[b], writes=[mx])
            plan.op("dve", lambda h, m=m: h.tensor_scalar(
                s_sb.ap[:, m:m + 1], mx.ap, 1e-8, 1.0 / 127.0, ALU.max,
                ALU.mult), reads=[mx], rw=[s_sb])
            plan.op("dve", lambda h, m=m: h.reciprocal(
                mx.ap, s_sb.ap[:, m:m + 1]), reads=[s_sb], writes=[mx])
            plan.dma("pool", srowdb.ap[m * 128:(m + 1) * 128, 0:1],
                     mx.ap, reads=[mx], writes=[srowdb])
        NDT = ND // 128
        CH = NF // TC
        w = [sb(nc, st, f"q_w{j}{wf_name}", [128, TC]) for j in range(2)]
        sr = [sb(nc, st, f"q_sr{j}{wf_name}", [128, TC]) for j in range(2)]
        sg = sb(nc, st, f"q_sg{wf_name}", [128, TC])
        sg2 = sb(nc, st, f"q_sg2{wf_name}", [128, TC])
        ti = sb(nc, st, f"q_ti{wf_name}", [128, TC], I32)
        ob = sb(nc, st, f"q_ob{wf_name}", [128, TC], BF16)
        wT = io[wT_name]
        i_s = 0
        for c in range(CH):
            for kd in range(NDT):
                bw, bs = w[i_s % 2], sr[i_s % 2]
                plan.dma("sp", bw,
                         wT[kd * 128:(kd + 1) * 128, c * TC:(c + 1) * TC],
                         reads=[DB[wT_name]])
                plan.dma("sp", bs,
                         srowdb.ap[c * TC:(c + 1) * TC, 0:1]
                         .transpose([1, 0]).broadcast_to([128, TC]),
                         reads=[srowdb], writes=[bs])
                plan.op("dve", lambda h, bw=bw, bs=bs: h.tensor_tensor(
                    bw.ap, bw.ap, bs.ap, ALU.mult), reads=[bs], rw=[bw])
                round_nearest(plan, bw, ti, sg, sg2)
                plan.op("dve", lambda h, bw=bw: h.tensor_scalar(
                    bw.ap, bw.ap, -128.0, 127.0, ALU.max, ALU.min), rw=[bw])
                plan.op("dve", lambda h, bw=bw: h.tensor_copy(ob.ap, bw.ap),
                        reads=[bw], writes=[ob])
                plan.dma("pool",
                         outdb.ap[kd * 128:(kd + 1) * 128, c * TC:(c + 1) * TC],
                         ob, reads=[ob], writes=[outdb])
                i_s += 1



class LNScratch:
    def __init__(self, plan, nc, st, tag, P=None):
        self.p_sx = P["psx"]
        self.p_sq = P["psq"]
        self.p_ar = P["par"]
        self.p_br = P["pbr"]
        self.sq = sb(nc, st, f"ln_sq{tag}", [128, TC])
        self.rows = [sb(nc, st, f"ln_r{j}{tag}", [1, TC]) for j in range(4)]
        self.mxr = sb(nc, st, f"ln_mx{tag}", [128, 1])


def emit_postln(plan, P, lns, tiles, dst_fn, dstdb, g_ap=None, b_ap=None,
                gdeps=(), maxacc=None):
    r_mean, r_var, r_b, r_rstd = lns.rows
    for dt in range(DT):
        tl = tiles[dt]
        plan.op("pe", lambda h, dt=dt, tl=tl: h.matmul(
            lns.p_sx.ap, P["ones1"].ap, tl.ap, start=(dt == 0),
            stop=(dt == DT - 1)),
            reads=[tl, P["ones1"]], writes=[lns.p_sx] if dt == 0 else (),
            rw=() if dt == 0 else ())
        plan.op("act", lambda h, tl=tl: h.activation(lns.sq.ap, tl.ap,
                                                     AF.Square),
                reads=[tl], writes=[lns.sq])
        plan.op("pe", lambda h, dt=dt: h.matmul(
            lns.p_sq.ap, P["ones1"].ap, lns.sq.ap, start=(dt == 0),
            stop=(dt == DT - 1)),
            reads=[lns.sq, P["ones1"]], writes=[lns.p_sq] if dt == 0 else ())
    last_sx = plan.ops["pe"][-1]
    lns.p_sx.writers = [plan.ops["pe"][-2]]
    lns.p_sq.writers = [last_sx]
    plan.op("dve", lambda h: h.tensor_scalar(r_mean.ap, lns.p_sx.ap, 1.0 / D,
                                             None, ALU.mult),
            reads=[lns.p_sx], writes=[r_mean])
    plan.op("dve", lambda h: h.tensor_scalar(r_var.ap, lns.p_sq.ap, 1.0 / D,
                                             1e-5, ALU.mult, ALU.add),
            reads=[lns.p_sq], writes=[r_var])
    plan.op("dve", lambda h: h.tensor_tensor(r_b.ap, r_mean.ap, r_mean.ap,
                                             ALU.mult),
            reads=[r_mean], writes=[r_b])
    plan.op("dve", lambda h: h.tensor_tensor(r_var.ap, r_var.ap, r_b.ap,
                                             ALU.subtract),
            reads=[r_b], rw=[r_var])
    plan.op("act", lambda h: h.activation(r_rstd.ap, r_var.ap, AF.Sqrt),
            reads=[r_var], writes=[r_rstd])
    plan.op("dve", lambda h: h.reciprocal(r_rstd.ap, r_rstd.ap), rw=[r_rstd])
    for _ in range(2):
        plan.op("dve", lambda h: h.tensor_tensor(r_b.ap, r_rstd.ap, r_rstd.ap,
                                                 ALU.mult),
                reads=[r_rstd], writes=[r_b])
        plan.op("dve", lambda h: h.tensor_tensor(r_b.ap, r_b.ap, r_var.ap,
                                                 ALU.mult),
                reads=[r_var], rw=[r_b])
        plan.op("dve", lambda h: h.tensor_scalar(r_b.ap, r_b.ap, -0.5, 1.5,
                                                 ALU.mult, ALU.add), rw=[r_b])
        plan.op("dve", lambda h: h.tensor_tensor(r_rstd.ap, r_rstd.ap, r_b.ap,
                                                 ALU.mult),
                reads=[r_b], rw=[r_rstd])
    plan.op("dve", lambda h: h.tensor_tensor(r_b.ap, r_mean.ap, r_rstd.ap,
                                             ALU.mult),
            reads=[r_mean, r_rstd], writes=[r_b])
    plan.op("dve", lambda h: h.tensor_scalar(r_b.ap, r_b.ap, -1.0, None,
                                             ALU.mult), rw=[r_b])
    if g_ap is None:
        plan.op("pe", lambda h: h.matmul(lns.p_ar.ap, P["onesc"].ap, r_rstd.ap,
                                         start=True, stop=True),
                reads=[r_rstd, P["onesc"]], writes=[lns.p_ar])
        plan.op("pe", lambda h: h.matmul(lns.p_br.ap, P["onesc"].ap, r_b.ap,
                                         start=True, stop=True),
                reads=[r_b, P["onesc"]], writes=[lns.p_br])
    for dt in range(DT):
        tl = tiles[dt]
        if g_ap is not None:
            plan.op("pe", lambda h, dt=dt: h.matmul(
                lns.p_ar.ap, g_ap[:, dt * 128:(dt + 1) * 128], r_rstd.ap,
                start=True, stop=True),
                deps=list(gdeps), reads=[r_rstd], writes=[lns.p_ar])
            plan.op("pe", lambda h, dt=dt: h.matmul(
                lns.p_br.ap, g_ap[:, dt * 128:(dt + 1) * 128], r_b.ap,
                start=True, stop=False),
                deps=list(gdeps), reads=[r_b], writes=[lns.p_br])
            o = plan.op("pe", lambda h, dt=dt: h.matmul(
                lns.p_br.ap, b_ap[:, dt * 128:(dt + 1) * 128], P["onesr"].ap,
                start=False, stop=True),
                deps=list(gdeps), reads=[P["onesr"]])
            lns.p_br.writers = [o]
            lns.p_br.readers = []
        plan.op("dve", lambda h, tl=tl: h.tensor_tensor(
            tl.ap, tl.ap, lns.p_ar.ap, ALU.mult), reads=[lns.p_ar], rw=[tl])
        plan.op("dve", lambda h, tl=tl: h.tensor_tensor(
            tl.ap, tl.ap, lns.p_br.ap, ALU.add), reads=[lns.p_br], rw=[tl])
        if maxacc is not None:
            plan.op("dve", lambda h, tl=tl: h.reduce_max(lns.mxr.ap, tl.ap,
                                                         axis=AXX),
                    reads=[tl], writes=[lns.mxr])
            plan.op("dve", lambda h: h.tensor_max(maxacc.ap, maxacc.ap,
                                                  lns.mxr.ap),
                    reads=[lns.mxr], rw=[maxacc])
        plan.dma("pool", dst_fn(dt), tl, reads=[tl], writes=[dstdb])


def emit_preln(plan, nc, io, DB, P, seq):
    with contextlib.ExitStack() as st:
        tiles = [sb(nc, st, f"pl_t{j}", [128, TC]) for j in range(DT)]
        gsb = sb(nc, st, "pl_g", [1, D])
        bsb = sb(nc, st, "pl_b", [1, D])
        lns = LNScratch(plan, nc, st, "pl", P)
        g1 = plan.dma("sp", gsb, io["ln_g"], reads=[DB["ln_g"]])
        g2 = plan.dma("sp", bsb, io["ln_b"], reads=[DB["ln_b"]])
        for c in range(NCH):
            col = seq * L + c * TC
            for dt in range(DT):
                plan.dma("sp", tiles[dt],
                         io["x0"][dt * 128:(dt + 1) * 128, col:col + TC],
                         reads=[DB["x0"]])
            emit_postln(plan, P, lns, tiles,
                        lambda dt, col=col: io["xn_s"][dt * 128:(dt + 1) * 128,
                                                       col:col + TC],
                        DB["xn_s"], g_ap=gsb.ap, b_ap=bsb.ap,
                        gdeps=[g1, g2, gsb.writers[0] if gsb.writers else None])



def mm_group(plan, psum, pairs, mw=None):
    n = len(pairs)
    first_deps_bufs = [psum]
    for k, (wb, wa, mb, ma) in enumerate(pairs):
        start = (k == 0)
        stop = (k == n - 1)
        if start:
            o = plan.op("pe", lambda h, wa=wa, ma=ma, start=start, stop=stop:
                        h.matmul(psum.ap if mw is None else psum.ap[0:mw, :],
                                 wa, ma, start=start, stop=stop),
                        reads=[wb, mb], writes=[psum])
        else:
            o = plan.op("pe", lambda h, wa=wa, ma=ma, start=start, stop=stop:
                        h.matmul(psum.ap if mw is None else psum.ap[0:mw, :],
                                 wa, ma, start=start, stop=stop),
                        reads=[wb, mb])
    psum.writers = [o]
    psum.readers = []
    return o


def emit_proj_simple(plan, nc, io, DB, P, seq, wT_ap, wT_db, mov_db, KT, MT,
                     dst_db, name=""):
    with contextlib.ExitStack() as st:
        wb = [[sb(nc, st, f"pj_w{s}_{k}{name}", [128, 128]) for k in range(KT)]
              for s in range(2)]
        mb = [sb(nc, st, f"pj_m{k}{name}", [128, TC]) for k in range(KT)]
        pb = pbanks(P)
        eb = [sb(nc, st, f"pj_e{j}{name}", [128, TC]) for j in range(2)]
        for c in range(NCH):
            yield
            col = seq * L + c * TC
            for k in range(KT):
                plan.dma("sp", mb[k],
                         mov_db.ap[k * 128:(k + 1) * 128, col:col + TC],
                         reads=[mov_db])
            for m in range(MT):
                gi = c * MT + m
                ws = gi % 2
                for k in range(KT):
                    plan.dma("sp", wb[ws][k],
                             wT_ap[k * 128:(k + 1) * 128,
                                   m * 128:(m + 1) * 128],
                             reads=[wT_db])
                bank = gi % 2
                mm_group(plan, pb[bank],
                         [(wb[ws][k], wb[ws][k].ap, mb[k], mb[k].ap)
                          for k in range(KT)])
                e = eb[gi % 2]
                plan.op("act", lambda h, e=e, bank=bank: h.activation(
                    e.ap, pb[bank].ap, AF.Copy), reads=[pb[bank]], writes=[e])
                plan.dma("pool", dst_db.ap[m * 128:(m + 1) * 128, col:col + TC],
                         e, reads=[e], writes=[dst_db])


def emit_inproj0(plan, nc, io, DB, P, seq, d):
    with contextlib.ExitStack() as st:
        xn = [sb(nc, st, f"ip_xn{k}", [128, L]) for k in range(DT)]
        wb = [[sb(nc, st, f"ip_w{s}_{k}", [128, 128]) for k in range(DT)]
              for s in range(2)]
        pb = pbanks(P)
        eb = [sb(nc, st, f"ip_e{j}", [128, TC]) for j in range(3)]
        for k in range(DT):
            plan.dma("sp", xn[k], io["xn_s"][k * 128:(k + 1) * 128,
                                             seq * L:(seq + 1) * L],
                     reads=[DB["xn_s"]])
        gi = 0
        for m in range(2 * DIT):
            yield
            ws = m % 2
            for k in range(DT):
                plan.dma("sp", wb[ws][k],
                         io[f"{d}_inT"][k * 128:(k + 1) * 128,
                                        m * 128:(m + 1) * 128],
                         reads=[DB[f"{d}_inT"]])
            for c in range(NCH):
                col = seq * L + c * TC
                bank = gi % 2
                mm_group(plan, pb[bank],
                         [(wb[ws][k], wb[ws][k].ap, xn[k],
                           xn[k].ap[:, c * TC:(c + 1) * TC]) for k in range(DT)])
                e = eb[gi % 3]
                plan.op("act", lambda h, e=e, bank=bank: h.activation(
                    e.ap, pb[bank].ap, AF.Copy), reads=[pb[bank]], writes=[e])
                dst = DB[f"xi_{d}_s"] if m < DIT else DB[f"z_{d}_s"]
                row = (m if m < DIT else m - DIT) * 128
                plan.dma("pool", dst.ap[row:row + 128, col:col + TC], e,
                         reads=[e], writes=[dst])
                gi += 1


def emit_conv(plan, nc, io, DB, seq, src_db, dst_db, convw_ap, convb_ap,
              convw_db, convb_db, fwd=True):
    with contextlib.ExitStack() as st:
        xb = [sb(nc, st, f"cv_x{j}", [128, L + 3]) for j in range(2)]
        ac = [sb(nc, st, f"cv_a{j}", [128, L]) for j in range(2)]
        wc = sb(nc, st, "cv_w", [128, DIT * 4])
        bc = sb(nc, st, "cv_b", [128, DIT])
        plan.dma("sp", wc, convw_ap, reads=[convw_db])
        plan.dma("sp", bc, convb_ap, reads=[convb_db])
        for kt in range(DIT):
            yield
            b = xb[kt % 2]
            a = ac[kt % 2]
            if fwd:
                plan.dma("sp", b.ap[:, 3:3 + L],
                         src_db.ap[kt * 128:(kt + 1) * 128,
                                   seq * L:(seq + 1) * L],
                         reads=[src_db], writes=[b])
                plan.op("dve", lambda h, b=b: h.memset(b.ap[:, 0:3], 0.0),
                        rw=[b])
            else:
                plan.dma("sp", b.ap[:, 0:L],
                         src_db.ap[kt * 128:(kt + 1) * 128,
                                   seq * L:(seq + 1) * L],
                         reads=[src_db], writes=[b])
                plan.op("dve", lambda h, b=b: h.memset(b.ap[:, L:L + 3], 0.0),
                        rw=[b])
            off0 = 0 if fwd else 3
            plan.op("dve", lambda h, b=b, a=a, kt=kt, off0=off0: h.tensor_scalar(
                a.ap, b.ap[:, off0:off0 + L], wc.ap[:, kt * 4:kt * 4 + 1],
                None, ALU.mult), reads=[b, wc], writes=[a])
            for j in range(1, 4):
                off = j if fwd else 3 - j
                plan.op("dve", lambda h, b=b, a=a, kt=kt, j=j, off=off:
                        h.scalar_tensor_tensor(
                            a.ap, b.ap[:, off:off + L],
                            wc.ap[:, kt * 4 + j:kt * 4 + j + 1], a.ap,
                            ALU.mult, ALU.add), reads=[b, wc], rw=[a])
            plan.op("act", lambda h, a=a, kt=kt: h.activation(
                a.ap, a.ap, AF.Silu, bias=bc.ap[:, kt:kt + 1]),
                reads=[bc], rw=[a])
            plan.dma("pool", dst_db.ap[kt * 128:(kt + 1) * 128,
                                       seq * L:(seq + 1) * L], a,
                     reads=[a], writes=[dst_db])


def emit_xw0(plan, nc, io, DB, P, seq, d):
    with contextlib.ExitStack() as st:
        wb = [sb(nc, st, f"xw_w{k}", [128, 96]) for k in range(DIT)]
        mb = [sb(nc, st, f"xw_m{k}", [128, TC]) for k in range(DIT)]
        pb = pbanks(P)
        eb = [sb(nc, st, f"xw_e{j}", [96, TC]) for j in range(2)]
        for k in range(DIT):
            plan.dma("sp", wb[k], io[f"{d}_xwT"][k * 128:(k + 1) * 128, :],
                     reads=[DB[f"{d}_xwT"]])
        for c in range(NCH):
            yield
            col = seq * L + c * TC
            s2 = c % 2
            for k in range(DIT):
                plan.dma("sp", mb[k],
                         DB[f"xc_{d}_s"].ap[k * 128:(k + 1) * 128, col:col + TC],
                         reads=[DB[f"xc_{d}_s"]])
            mm_group(plan, pb[s2],
                     [(wb[k], wb[k].ap, mb[k], mb[k].ap)
                      for k in range(DIT)], mw=96)
            e = eb[s2]
            plan.op("act", lambda h, e=e, s2=s2: h.activation(
                e.ap, pb[s2].ap[0:96, :], AF.Copy), reads=[pb[s2]], writes=[e])
            plan.dma("pool", DB[f"dtr_{d}_s"].ap[:, col:col + TC],
                     e.ap[0:64, :], reads=[e], writes=[DB[f"dtr_{d}_s"]])
            plan.dma("pool", DB[f"bc_{d}_s"].ap[:, col:col + TC],
                     e.ap[64:96, :], reads=[e], writes=[DB[f"bc_{d}_s"]])


def emit_dtproj(plan, nc, io, DB, P, seq, dtwT_ap, dtwT_db, dtb_ap, dtb_db,
                dtr_db, dl_db):
    with contextlib.ExitStack() as st:
        wb = [sb(nc, st, f"dp_w{j}", [64, 128]) for j in range(2)]
        mbb = sb(nc, st, "dp_m", [64, L])
        pb = pbanks(P)
        sg = [sb(nc, st, f"dp_s{j}", [128, TC]) for j in range(2)]
        nb = sb(nc, st, "dp_nb", [128, DIT])
        bt = sb(nc, st, "dp_bt", [128, DIT])
        plan.dma("sp", bt, dtb_ap, reads=[dtb_db])
        plan.op("dve", lambda h: h.tensor_scalar(nb.ap, bt.ap, -1.0, None,
                                                 ALU.mult),
                reads=[bt], writes=[nb])
        plan.dma("sp", mbb, dtr_db.ap[:, seq * L:(seq + 1) * L],
                 reads=[dtr_db])
        gi = 0
        for m in range(DIT):
            yield
            ws = m % 2
            plan.dma("sp", wb[ws], dtwT_ap[:, m * 128:(m + 1) * 128],
                     reads=[dtwT_db])
            for c in range(NCH):
                col = seq * L + c * TC
                bank = gi % 2
                mm_group(plan, pb[bank],
                         [(wb[ws], wb[ws].ap, mbb,
                           mbb.ap[:, c * TC:(c + 1) * TC])])
                s = sg[gi % 2]
                plan.op("act", lambda h, s=s, bank=bank, m=m: h.activation(
                    s.ap, pb[bank].ap, AF.Sigmoid, scale=-1.0,
                    bias=nb.ap[:, m:m + 1]),
                    reads=[pb[bank], nb], writes=[s])
                plan.op("act", lambda h, s=s: h.activation(s.ap, s.ap, AF.Ln),
                        rw=[s])
                plan.dma("pool", dl_db.ap[m * 128:(m + 1) * 128, col:col + TC],
                         s, reads=[s], writes=[dl_db])
                gi += 1



def emit_scan(plan, nc, io, DB, P, seq, *, N, NG, TCS, dl_db, xc_db, bc_db,
              y_db, dp_ap, dp_db, a_col_fn, reverse, name):
    nh = L // TCS
    with contextlib.ExitStack() as st:
        brep = [sb(nc, st, f"sc_B{j}{name}", [128, TCS]) for j in range(NG)]
        crep = [sb(nc, st, f"sc_C{j}{name}", [128, TCS]) for j in range(NG)]
        dl = [sb(nc, st, f"sc_dl{j}{name}", [128, TCS]) for j in range(2)]
        xc = [sb(nc, st, f"sc_xc{j}{name}", [128, TCS]) for j in range(2)]
        du = sb(nc, st, f"sc_du{name}", [128, TCS])
        ya = sb(nc, st, f"sc_ya{name}", [128, TCS])
        da = [sb(nc, st, f"sc_da{j}{name}", [128, TCS]) for j in range(2)]
        bb = [sb(nc, st, f"sc_b{j}{name}", [128, TCS]) for j in range(2)]
        hh = [sb(nc, st, f"sc_h{j}{name}", [128, TCS]) for j in range(2)]
        hc = [sb(nc, st, f"sc_hc{j}{name}", [128, TCS]) for j in range(2)]
        dpb = sb(nc, st, f"sc_dp{name}", [128, DIT])
        carry = (sb(nc, st, f"sc_cr{name}", [128, DIT * N]) if nh > 1 else None)
        plan.dma("sp", dpb, dp_ap, reads=[dp_db])
        chunks = list(range(nh)) if not reverse else list(range(nh - 1, -1, -1))
        for ci, ch in enumerate(chunks):
            col = seq * L + ch * TCS
            for g in range(N // NG):
                yield
                ns = [g * NG + j for j in range(NG)]
                for j, n in enumerate(ns):
                    plan.dma("sp", brep[j],
                             bc_db.ap[n:n + 1, col:col + TCS]
                             .broadcast_to([128, TCS]),
                             reads=[bc_db], writes=[brep[j]])
                    plan.dma("sp", crep[j],
                             bc_db.ap[N + n:N + n + 1, col:col + TCS]
                             .broadcast_to([128, TCS]),
                             reads=[bc_db], writes=[crep[j]])
                for dt in range(DIT):
                    dlb, xcb = dl[dt % 2], xc[dt % 2]
                    plan.dma("sp", dlb, dl_db.ap[dt * 128:(dt + 1) * 128,
                                                 col:col + TCS],
                             reads=[dl_db])
                    plan.dma("sp", xcb, xc_db.ap[dt * 128:(dt + 1) * 128,
                                                 col:col + TCS],
                             reads=[xc_db])
                    plan.op("dve", lambda h, dlb=dlb, xcb=xcb:
                            h.scalar_tensor_tensor(du.ap, xcb.ap, -1.0, dlb.ap,
                                                   ALU.mult, ALU.mult),
                            reads=[dlb, xcb], writes=[du])
                    if g == 0:
                        plan.op("dve", lambda h, xcb=xcb, dt=dt: h.tensor_scalar(
                            ya.ap, xcb.ap, dpb.ap[:, dt:dt + 1], None, ALU.mult),
                            reads=[xcb, dpb], writes=[ya])
                    def emit_pre(j):
                        aap = a_col_fn(dt, ns[j])
                        p = j % 2
                        dap, bbp = da[p], bb[p]
                        plan.op("act", lambda h, dlb=dlb, aap=aap, dap=dap:
                                h.activation(dap.ap, dlb.ap, AF.Exp, scale=aap),
                                reads=[dlb], writes=[dap])
                        plan.op("dve", lambda h, j=j, bbp=bbp: h.tensor_tensor(
                            bbp.ap, du.ap, brep[j].ap, ALU.mult),
                            reads=[du, brep[j]], writes=[bbp])

                    emit_pre(0)
                    for j, n in enumerate(ns):
                        if j + 1 < len(ns):
                            emit_pre(j + 1)
                        p = j % 2
                        dap, bbp, hhp, hcp = da[p], bb[p], hh[p], hc[p]
                        if nh > 1 and ci > 0:
                            cc = dt * N + n
                            init = carry.ap[:, cc:cc + 1]
                            ini_reads = [carry]
                        else:
                            init = 0.0
                            ini_reads = []
                        if not reverse:
                            plan.op("dve", lambda h, dap=dap, bbp=bbp,
                                    hhp=hhp, init=init:
                                    h.tensor_tensor_scan(hhp.ap, dap.ap,
                                                         bbp.ap, init,
                                                         ALU.mult, ALU.add),
                                    reads=[dap, bbp] + ini_reads,
                                    writes=[hhp])
                        else:
                            plan.op("dve", lambda h, dap=dap, bbp=bbp,
                                    hhp=hhp, init=init:
                                    h.tensor_tensor_scan(hhp.ap[:, ::-1],
                                                         dap.ap[:, ::-1],
                                                         bbp.ap[:, ::-1],
                                                         init, ALU.mult,
                                                         ALU.add),
                                    reads=[dap, bbp] + ini_reads,
                                    writes=[hhp])
                        if nh > 1 and ci < nh - 1:
                            cc = dt * N + n
                            lastc = (TCS - 1) if not reverse else 0
                            plan.op("dve", lambda h, cc=cc, lastc=lastc,
                                    hhp=hhp:
                                    h.tensor_copy(carry.ap[:, cc:cc + 1],
                                                  hhp.ap[:, lastc:lastc + 1]),
                                    reads=[hhp], rw=[carry])
                        plan.op("dve", lambda h, j=j, hhp=hhp, hcp=hcp:
                                h.tensor_tensor(hcp.ap, hhp.ap, crep[j].ap,
                                                ALU.mult),
                                reads=[hhp, crep[j]], writes=[hcp])
                        if g == 0 or j > 0:
                            plan.op("dve", lambda h, hcp=hcp: h.tensor_tensor(
                                ya.ap, ya.ap, hcp.ap, ALU.add),
                                reads=[hcp], rw=[ya])
                        else:
                            plan.op("dve", lambda h, hcp=hcp: h.tensor_copy(
                                ya.ap, hcp.ap), reads=[hcp], writes=[ya])
                    if g == 0:
                        plan.dma("pool", y_db.ap[dt * 128:(dt + 1) * 128,
                                                 col:col + TCS], ya,
                                 reads=[ya], rw=[y_db])
                    else:
                        plan.dma("pool", y_db.ap[dt * 128:(dt + 1) * 128,
                                                 col:col + TCS], ya,
                                 reads=[ya], rw=[y_db], accum_op=ALU.add)


def emit_scan0(plan, nc, io, DB, P, seq, d):
    A = P["A0f"] if d == "f" else P["A0b"]

    def acol(dt, n):
        return A.ap[:, dt * N0 + n:dt * N0 + n + 1]

    yield from emit_scan(plan, nc, io, DB, P, seq, N=N0, NG=4, TCS=L // 2,
                         dl_db=DB[f"dl_{d}_s"], xc_db=DB[f"xc_{d}_s"],
              bc_db=DB[f"bc_{d}_s"], y_db=DB[f"y_{d}_s"],
              dp_ap=io[f"{d}_dp"], dp_db=DB[f"{d}_dp"], a_col_fn=acol,
              reverse=(d == "b"), name=f"s0{d}")


def emit_scan1(plan, nc, io, DB, P, seq):
    def acol(dt, n):
        return P["A1"].ap[:, n:n + 1]

    yield from emit_scan(plan, nc, io, DB, P, seq, N=N1, NG=4, TCS=L // 2,
              dl_db=DB["dl1_s"], xc_db=DB["xc1_s"], bc_db=DB["bc1_s"],
              y_db=DB["y1_s"], dp_ap=io["dp1"], dp_db=DB["dp1"],
              a_col_fn=acol, reverse=False, name="s1")


def emit_gating(plan, nc, io, DB, seq, y_db, z_db):
    with contextlib.ExitStack() as st:
        yb = [sb(nc, st, f"gt_y{j}", [128, L]) for j in range(2)]
        zb = [sb(nc, st, f"gt_z{j}", [128, L]) for j in range(2)]
        for kt in range(DIT):
            yield
            y, z = yb[kt % 2], zb[kt % 2]
            plan.dma("sp", y, y_db.ap[kt * 128:(kt + 1) * 128,
                                      seq * L:(seq + 1) * L], reads=[y_db])
            plan.dma("sp", z, z_db.ap[kt * 128:(kt + 1) * 128,
                                      seq * L:(seq + 1) * L], reads=[z_db])
            plan.op("act", lambda h, z=z: h.activation(z.ap, z.ap, AF.Silu),
                    rw=[z])
            plan.op("dve", lambda h, y=y, z=z: h.tensor_tensor(
                y.ap, y.ap, z.ap, ALU.mult), reads=[z], rw=[y])
            plan.dma("pool", y_db.ap[kt * 128:(kt + 1) * 128,
                                     seq * L:(seq + 1) * L], y,
                     reads=[y], rw=[y_db])



def emit_combine(plan, nc, io, DB, P, seq):
    with contextlib.ExitStack() as st:
        wb = [[sb(nc, st, f"cb_w{s}_{k}", [128, 128]) for k in range(2 * DT)]
              for s in range(2)]
        mb = [sb(nc, st, f"cb_m{k}", [128, TC]) for k in range(2 * DT)]
        pb = pbanks(P)
        res = [sb(nc, st, f"cb_r{j}", [128, TC]) for j in range(DT)]
        out = [sb(nc, st, f"cb_o{j}", [128, TC]) for j in range(DT)]
        cbb = sb(nc, st, "cb_bias", [128, DT])
        lns = LNScratch(plan, nc, st, "cb", P)
        plan.dma("sp", cbb, io["combb"], reads=[DB["combb"]])
        for c in range(NCH):
            yield
            col = seq * L + c * TC
            ms = c % 2
            for k in range(2 * DT):
                srcdb = DB["o_f_s"] if k < DT else DB["o_b_s"]
                row = (k if k < DT else k - DT) * 128
                plan.dma("sp", mb[k], srcdb.ap[row:row + 128, col:col + TC],
                         reads=[srcdb])
            for dt in range(DT):
                plan.dma("sp", res[dt],
                         io["x0"][dt * 128:(dt + 1) * 128, col:col + TC],
                         reads=[DB["x0"]])
            for m in range(DT):
                gi = c * DT + m
                ws = gi % 2
                for k in range(2 * DT):
                    plan.dma("sp", wb[ws][k],
                             io["combT"][k * 128:(k + 1) * 128,
                                         m * 128:(m + 1) * 128],
                             reads=[DB["combT"]])
                bank = gi % 2
                mm_group(plan, pb[bank],
                         [(wb[ws][k], wb[ws][k].ap, mb[k], mb[k].ap)
                          for k in range(2 * DT)])
                plan.op("dve", lambda h, m=m, bank=bank: h.scalar_tensor_tensor(
                    out[m].ap, res[m].ap, 2.0, pb[bank].ap, ALU.mult, ALU.add),
                    reads=[pb[bank], res[m]], writes=[out[m]])
                plan.op("dve", lambda h, m=m: h.tensor_scalar(
                    out[m].ap, out[m].ap, cbb.ap[:, m:m + 1], None, ALU.add),
                    reads=[cbb], rw=[out[m]])
            emit_postln(plan, P, lns, out,
                        lambda dt, col=col: io["x1_s"][dt * 128:(dt + 1) * 128,
                                                       col:col + TC],
                        DB["x1_s"])


def emit_ffn0(plan, nc, io, DB, P, seq):
    with contextlib.ExitStack() as st:
        wq = [sb(nc, st, f"f0_wq{k}", [128, D]) for k in range(DT)]
        ab = [sb(nc, st, f"f0_a{k}", [128, 16]) for k in range(DT)]
        bb = sb(nc, st, "f0_b", [16, D])
        bias = sb(nc, st, "f0_bias", [128, DT])
        x1 = [sb(nc, st, f"f0_x{k}", [128, TC]) for k in range(DT)]
        pu = P["par"]
        ub = sb(nc, st, "f0_u", [16, TC])
        pb = pbanks(P)
        out = [sb(nc, st, f"f0_o{j}", [128, TC]) for j in range(DT)]
        lns = LNScratch(plan, nc, st, "f0", P)
        for k in range(DT):
            plan.dma("sp", wq[k], DB["wq0_s"].ap[k * 128:(k + 1) * 128, :],
                     reads=[DB["wq0_s"]])
            plan.dma("sp", ab[k], io["ffn_a"][k * 128:(k + 1) * 128, :],
                     reads=[DB["ffn_a"]])
        plan.dma("sp", bb, io["ffn_b"], reads=[DB["ffn_b"]])
        plan.dma("sp", bias, io["ffn_bias"], reads=[DB["ffn_bias"]])
        for c in range(NCH):
            yield
            col = seq * L + c * TC
            for k in range(DT):
                plan.dma("sp", x1[k],
                         DB["x1_s"].ap[k * 128:(k + 1) * 128, col:col + TC],
                         reads=[DB["x1_s"]])
            mm_group(plan, pu, [(ab[k], ab[k].ap, x1[k], x1[k].ap)
                                for k in range(DT)], mw=16)
            plan.op("act", lambda h: h.activation(ub.ap, pu.ap[0:16, :],
                                                  AF.Copy),
                    reads=[pu], writes=[ub])
            for m in range(DT):
                bank = m % 2
                pairs = [(wq[k], wq[k].ap[:, m * 128:(m + 1) * 128], x1[k],
                          x1[k].ap) for k in range(DT)]
                pairs.append((bb, bb.ap[:, m * 128:(m + 1) * 128], ub, ub.ap))
                mm_group(plan, pb[bank], pairs)
                plan.op("dve", lambda h, m=m, bank=bank: h.scalar_tensor_tensor(
                    out[m].ap, pb[bank].ap, bias.ap[:, m:m + 1], x1[m].ap,
                    ALU.add, ALU.add),
                    reads=[pb[bank], bias, x1[m]], writes=[out[m]])
            emit_postln(plan, P, lns, out,
                        lambda dt, col=col: io["x2_s"][dt * 128:(dt + 1) * 128,
                                                       col:col + TC],
                        DB["x2_s"])


def emit_inproj1(plan, nc, io, DB, P, seq):
    with contextlib.ExitStack() as st:
        x2 = [sb(nc, st, f"i1_x{k}", [128, L]) for k in range(DT)]
        wb = [[sb(nc, st, f"i1_w{s}_{k}", [128, 128]) for k in range(DT)]
              for s in range(2)]
        pb = pbanks(P)
        eb = [sb(nc, st, f"i1_e{j}", [128, TC]) for j in range(3)]
        for k in range(DT):
            plan.dma("sp", x2[k], DB["x2_s"].ap[k * 128:(k + 1) * 128,
                                                seq * L:(seq + 1) * L],
                     reads=[DB["x2_s"]])
        mdefs = [(m * 128, 128) for m in range(33)] + [(4224, 64)]
        gi = 0
        for mi, (moff, mw) in enumerate(mdefs):
            yield
            ws = mi % 2
            for k in range(DT):
                plan.dma("sp", wb[ws][k].ap[:, 0:mw],
                         io["in1T"][k * 128:(k + 1) * 128, moff:moff + mw],
                         reads=[DB["in1T"]], writes=[wb[ws][k]])
            for c in range(NCH):
                col = seq * L + c * TC
                bank = gi % 2
                mm_group(plan, pb[bank],
                         [(wb[ws][k], wb[ws][k].ap[:, 0:mw], x2[k],
                           x2[k].ap[:, c * TC:(c + 1) * TC])
                          for k in range(DT)], mw=mw)
                e = eb[gi % 3]
                plan.op("act", lambda h, e=e, bank=bank, mw=mw: h.activation(
                    e.ap[0:mw, :], pb[bank].ap[0:mw, :], AF.Copy),
                    reads=[pb[bank]], writes=[e])
                if moff < DI:
                    dst, row = DB["xi1_s"], moff
                elif moff < 2 * DI:
                    dst, row = DB["z1_s"], moff - DI
                elif moff == 4096:
                    dst, row = DB["bc1_s"], 0
                else:
                    dst, row = DB["dtr1_s"], 0
                plan.dma("pool", dst.ap[row:row + mw, col:col + TC],
                         e.ap[0:mw, :], reads=[e], writes=[dst])
                gi += 1


def emit_outproj1_resln(plan, nc, io, DB, P, seq):
    with contextlib.ExitStack() as st:
        wb = [[sb(nc, st, f"o1_w{s}_{k}", [128, 128]) for k in range(DIT)]
              for s in range(2)]
        mb = [sb(nc, st, f"o1_m{k}", [128, TC]) for k in range(DIT)]
        pb = pbanks(P)
        out = [sb(nc, st, f"o1_o{j}", [128, TC]) for j in range(DT)]
        res = [sb(nc, st, f"o1_r{j}", [128, TC]) for j in range(DT)]
        lns = LNScratch(plan, nc, st, "o1", P)
        for c in range(NCH):
            yield
            col = seq * L + c * TC
            for k in range(DIT):
                plan.dma("sp", mb[k],
                         DB["y1_s"].ap[k * 128:(k + 1) * 128, col:col + TC],
                         reads=[DB["y1_s"]])
            for m in range(DT):
                gi = c * DT + m
                ws = gi % 2
                for k in range(DIT):
                    plan.dma("sp", wb[ws][k],
                             io["out1T"][k * 128:(k + 1) * 128,
                                         m * 128:(m + 1) * 128],
                             reads=[DB["out1T"]])
                bank = gi % 2
                mm_group(plan, pb[bank],
                         [(wb[ws][k], wb[ws][k].ap, mb[k], mb[k].ap)
                          for k in range(DIT)])
                plan.op("act", lambda h, m=m, bank=bank: h.activation(
                    out[m].ap, pb[bank].ap, AF.Copy),
                    reads=[pb[bank]], writes=[out[m]])
            for dt in range(DT):
                plan.dma("sp", res[dt],
                         DB["x2_s"].ap[dt * 128:(dt + 1) * 128, col:col + TC],
                         reads=[DB["x2_s"]])
                plan.op("dve", lambda h, dt=dt: h.tensor_tensor(
                    out[dt].ap, out[dt].ap, res[dt].ap, ALU.add),
                    reads=[res[dt]], rw=[out[dt]])
            emit_postln(plan, P, lns, out,
                        lambda dt, col=col: io["x3_s"][dt * 128:(dt + 1) * 128,
                                                       col:col + TC],
                        DB["x3_s"], maxacc=P["gmx"])


def emit_quantprep(plan, nc, io, DB, P, macc, rcp, sw, bvec, seff, beff, NFT,
                   dbgdb=None):
    with contextlib.ExitStack() as st:
        pt = P["psx"]
        pbb = P["par"]
        srow = sb(nc, st, "qp_srow", [1, 1])
        sbc = sb(nc, st, "qp_sbc", [128, 1])
        sgn = sb(nc, st, "qp_sg", [128, NFT])
        tii = sb(nc, st, "qp_ti", [128, NFT], I32)
        plan.op("pe", lambda h: h.transpose(pt.ap[0:1, 0:128], macc.ap,
                                            P["ident"].ap),
                reads=[macc, P["ident"]], writes=[pt])
        plan.op("dve", lambda h: h.reduce_max(srow.ap, pt.ap[0:1, 0:128],
                                              axis=AXX),
                reads=[pt], writes=[srow])
        plan.dma("pool", io["cc_in"], srow, reads=[srow], writes=[DB["cc_in"]])
        plan.op("pool", lambda h: h.collective_compute(
            "AllReduce", ALU.max,
            replica_groups=[[0, 1, 2, 3, 4, 5, 6, 7]],
            ins=[io["cc_in"]], outs=[io["cc_out"]]),
            reads=[DB["cc_in"]], writes=[DB["cc_out"]])
        plan.dma("pool", srow, io["cc_out"], reads=[DB["cc_out"]],
                 writes=[srow])
        plan.op("dve", lambda h: h.tensor_scalar(srow.ap, srow.ap, 1e-8,
                                                 1.0 / 255.0, ALU.max,
                                                 ALU.mult), rw=[srow])
        plan.op("pe", lambda h: h.matmul(pbb.ap[:, 0:1], P["onesc"].ap,
                                         srow.ap, start=True, stop=True),
                reads=[srow, P["onesc"]], writes=[pbb])
        plan.op("dve", lambda h: h.tensor_copy(sbc.ap, pbb.ap[:, 0:1]),
                reads=[pbb], writes=[sbc])
        plan.op("dve", lambda h: h.reciprocal(rcp.ap, sbc.ap),
                reads=[sbc], writes=[rcp])
        plan.op("dve", lambda h: h.tensor_scalar(seff.ap, sw.ap, sbc.ap[:, 0:1],
                                                 None, ALU.mult),
                reads=[sw, sbc], writes=[seff])
        rcq = sb(nc, st, "qp_rcq", [128, NFT])
        plan.op("dve", lambda h: h.reciprocal(rcq.ap, seff.ap),
                reads=[seff], writes=[rcq])
        plan.op("dve", lambda h: h.tensor_tensor(beff.ap, bvec.ap, rcq.ap,
                                                 ALU.mult),
                reads=[bvec, rcq], writes=[beff])
        sgn2 = sb(nc, st, "qp_sg2", [128, NFT])
        round_nearest(plan, beff, tii, sgn, sgn2)
        plan.op("dve", lambda h: h.tensor_tensor(beff.ap, beff.ap, seff.ap,
                                                 ALU.mult),
                reads=[seff], rw=[beff])
        if dbgdb is not None:
            dsin, dmax, dse, dbe = dbgdb
            plan.dma("pool", dsin.ap, sbc, reads=[sbc], writes=[dsin])
            plan.dma("pool", dmax.ap, macc, reads=[macc], writes=[dmax])
            plan.dma("pool", dse.ap, seff, reads=[seff], writes=[dse])
            plan.dma("pool", dbe.ap, beff, reads=[beff], writes=[dbe])


def quantize_tiles(plan, src_bufs, dst_bufs, rcp, tmp, tii, t1, t2):
    for sbuf, dbuf in zip(src_bufs, dst_bufs):
        plan.op("dve", lambda h, sbuf=sbuf: h.tensor_scalar(
            tmp.ap, sbuf.ap, rcp.ap[:, 0:1], None, ALU.mult),
            reads=[sbuf, rcp], writes=[tmp])
        plan.op("dve", lambda h: h.tensor_scalar(tmp.ap, tmp.ap, 0.0, 255.0,
                                                 ALU.max, ALU.min), rw=[tmp])
        round_nearest(plan, tmp, tii, t1, t2)
        plan.op("dve", lambda h, dbuf=dbuf: h.tensor_copy(dbuf.ap, tmp.ap),
                reads=[tmp], writes=[dbuf])


def emit_ffn1a(plan, nc, io, DB, P, seq):
    with contextlib.ExitStack() as st:
        wq = [sb(nc, st, f"fa_w{k}", [128, F1], BF16) for k in range(DT)]
        x3 = [sb(nc, st, f"fa_x{k}", [128, TC]) for k in range(DT)]
        xq = [sb(nc, st, f"fa_q{k}", [128, TC], BF16) for k in range(DT)]
        tmp = sb(nc, st, "fa_tmp", [128, TC])
        tq1 = sb(nc, st, "fa_tq1", [128, TC])
        tq2 = sb(nc, st, "fa_tq2", [128, TC])
        tii = sb(nc, st, "fa_ti", [128, TC], I32)
        pb = pbanks(P)
        eb = [sb(nc, st, f"fa_e{j}", [128, TC]) for j in range(3)]
        mxr = sb(nc, st, "fa_mx", [128, 1])
        for k in range(DT):
            plan.dma("sp", wq[k], DB["wq1T_s"].ap[k * 128:(k + 1) * 128, :],
                     reads=[DB["wq1T_s"]])
        for c in range(NCH):
            col = seq * L + c * TC
            for k in range(DT):
                plan.dma("sp", x3[k],
                         DB["x3_s"].ap[k * 128:(k + 1) * 128, col:col + TC],
                         reads=[DB["x3_s"]])
            quantize_tiles(plan, x3, xq, P["rcp1"], tmp, tii, tq1, tq2)
            for m in range(F1T):
                bank = m % 2
                mm_group(plan, pb[bank],
                         [(wq[k], wq[k].ap[:, m * 128:(m + 1) * 128], xq[k],
                           xq[k].ap) for k in range(DT)])
                e = eb[m % 3]
                plan.op("act", lambda h, e=e, bank=bank, m=m: h.activation(
                    e.ap, pb[bank].ap, AF.Relu,
                    scale=P["seff1"].ap[:, m:m + 1],
                    bias=P["beff1"].ap[:, m:m + 1]),
                    reads=[pb[bank], P["seff1"], P["beff1"]], writes=[e])
                plan.op("dve", lambda h, e=e: h.reduce_max(mxr.ap, e.ap,
                                                           axis=AXX),
                        reads=[e], writes=[mxr])
                plan.op("dve", lambda h: h.tensor_max(P["ghx"].ap, P["ghx"].ap,
                                                      mxr.ap),
                        reads=[mxr], rw=[P["ghx"]])
                plan.dma("pool", DB["h_s"].ap[m * 128:(m + 1) * 128,
                                              col:col + TC], e,
                         reads=[e], writes=[DB["h_s"]])


def emit_ffn1b(plan, nc, io, DB, P, seq):
    with contextlib.ExitStack() as st:
        wq = [sb(nc, st, f"fb_w{k}", [128, D], BF16) for k in range(F1T)]
        hb = [sb(nc, st, f"fb_h{j}", [128, TC]) for j in range(2)]
        hq = [sb(nc, st, f"fb_q{k}", [128, TC], BF16) for k in range(F1T)]
        tmp = sb(nc, st, "fb_tmp", [128, TC])
        tq1 = sb(nc, st, "fb_tq1", [128, TC])
        tq2 = sb(nc, st, "fb_tq2", [128, TC])
        tii = sb(nc, st, "fb_ti", [128, TC], I32)
        pb = pbanks(P)
        out = [sb(nc, st, f"fb_o{j}", [128, TC]) for j in range(DT)]
        res = [sb(nc, st, f"fb_r{j}", [128, TC]) for j in range(DT)]
        lns = LNScratch(plan, nc, st, "fb", P)
        for k in range(F1T):
            plan.dma("sp", wq[k], DB["wq2T_s"].ap[k * 128:(k + 1) * 128, :],
                     reads=[DB["wq2T_s"]])
        for c in range(NCH):
            col = seq * L + c * TC
            for k in range(F1T):
                b = hb[k % 2]
                plan.dma("sp", b,
                         DB["h_s"].ap[k * 128:(k + 1) * 128, col:col + TC],
                         reads=[DB["h_s"]])
                quantize_tiles(plan, [b], [hq[k]], P["rcp2"], tmp, tii, tq1, tq2)
            for m in range(DT):
                bank = m % 2
                mm_group(plan, pb[bank],
                         [(wq[k], wq[k].ap[:, m * 128:(m + 1) * 128], hq[k],
                           hq[k].ap) for k in range(F1T)])
                plan.op("dve", lambda h, m=m, bank=bank: h.tensor_scalar(
                    out[m].ap, pb[bank].ap, P["seff2"].ap[:, m:m + 1],
                    P["beff2"].ap[:, m:m + 1], ALU.mult, ALU.add),
                    reads=[pb[bank], P["seff2"], P["beff2"]], writes=[out[m]])
                plan.dma("sp", res[m],
                         DB["x3_s"].ap[m * 128:(m + 1) * 128, col:col + TC],
                         reads=[DB["x3_s"]])
                plan.op("dve", lambda h, m=m: h.tensor_tensor(
                    out[m].ap, out[m].ap, res[m].ap, ALU.add),
                    reads=[res[m]], rw=[out[m]])
            emit_postln(plan, P, lns, out,
                        lambda dt, col=col: io["xout"][dt * 128:(dt + 1) * 128,
                                                       col:col + TC],
                        DB["xout"])



def make_inputs(core, A):
    t = lambda x: np.ascontiguousarray(np.asarray(x, np.float32).T)
    col = lambda x: np.ascontiguousarray(np.asarray(x, np.float32).reshape(-1, 1))
    colk = lambda x, k: np.ascontiguousarray(
        np.asarray(x, np.float32).reshape(k, 128).T)
    m = {}
    x = np.asarray(A["L0_attn_x"][2 * core:2 * core + 2], np.float32)
    m["x0"] = np.ascontiguousarray(np.transpose(x, (2, 0, 1)).reshape(D, NT))
    m["ln_g"] = np.ascontiguousarray(np.asarray(A["L0_attn_ln_g"], np.float32).reshape(1, D))
    m["ln_b"] = np.ascontiguousarray(np.asarray(A["L0_attn_ln_b"], np.float32).reshape(1, D))
    for d in ("f", "b"):
        m[f"{d}_inT"] = t(A[f"L0_attn_{d}_in_w"])
        m[f"{d}_convw"] = np.ascontiguousarray(np.asarray(
            A[f"L0_attn_{d}_conv_w"], np.float32).reshape(DIT, 128, 4)
            .transpose(1, 0, 2).reshape(128, DIT * 4))
        m[f"{d}_convb"] = colk(A[f"L0_attn_{d}_conv_b"], DIT)
        m[f"{d}_xwT"] = t(A[f"L0_attn_{d}_x_w"])
        m[f"{d}_dtwT"] = t(A[f"L0_attn_{d}_dt_w"])
        m[f"{d}_dtb"] = colk(A[f"L0_attn_{d}_dt_b"], DIT)
        m[f"{d}_alog"] = np.ascontiguousarray(np.asarray(
            A[f"L0_attn_{d}_A_log"], np.float32))
        m[f"{d}_dp"] = colk(A[f"L0_attn_{d}_D"], DIT)
        m[f"{d}_outT"] = t(A[f"L0_attn_{d}_out_w"])
    m["combT"] = t(A["L0_attn_combine_w"])
    m["combb"] = colk(A["L0_attn_combine_b"], DT)
    m["ffn_k"] = np.ascontiguousarray(np.asarray(A["L0_ffn_kernel"], np.float32))
    m["ffn_bias"] = colk(A["L0_ffn_bias"], DT)
    m["ffn_a"] = np.ascontiguousarray(np.asarray(A["L0_ffn_a"], np.float32))
    m["ffn_b"] = np.ascontiguousarray(np.asarray(A["L0_ffn_b"], np.float32))
    m["in1T"] = t(A["L1_attn_in_proj_w"])
    m["conv1w"] = np.ascontiguousarray(np.asarray(
        A["L1_attn_conv_w"], np.float32).reshape(DIT, 128, 4)
        .transpose(1, 0, 2).reshape(128, DIT * 4))
    m["conv1b"] = colk(A["L1_attn_conv_b"], DIT)
    m["alog1b"] = np.ascontiguousarray(np.broadcast_to(
        np.asarray(A["L1_attn_A_log"], np.float32)[None, :], (128, N1)).copy())
    m["dp1"] = colk(A["L1_attn_D_param"], DIT)
    m["dt1T"] = t(A["L1_attn_dt_proj_w"])
    m["dt1b"] = colk(A["L1_attn_dt_proj_b"], DIT)
    m["out1T"] = t(A["L1_attn_out_proj_w"])
    m["w1f"] = np.ascontiguousarray(np.asarray(A["L1_ffn_w1"], np.float32))
    m["w1T"] = t(A["L1_ffn_w1"])
    m["b1"] = np.ascontiguousarray(
        np.asarray(A["L1_ffn_b1"], np.float32).reshape(F1T, 128).T)
    m["w2f"] = np.ascontiguousarray(np.asarray(A["L1_ffn_w2"], np.float32))
    m["w2T"] = t(A["L1_ffn_w2"])
    m["b2"] = np.ascontiguousarray(
        np.asarray(A["L1_ffn_b2"], np.float32).reshape(DT, 128).T)
    m["ident"] = np.eye(128, dtype=np.float32)
    return m


def kernel(**inputs):
    nc = build_program()
    in_maps = [make_inputs(c, inputs) for c in range(NCORE)]
    res = run_bass_kernel_spmd(nc, in_maps, core_ids=list(range(NCORE)))
    outs = []
    for c in range(NCORE):
        o = np.asarray(res.results[c]["xout"])
        outs.append(np.transpose(o.reshape(D, BSEQ, L), (1, 2, 0)))
    return np.concatenate(outs, axis=0).astype(np.float32)


# version 13
# speedup vs baseline: 1.1333x; 1.0399x over previous
import contextlib

import numpy as np

import concourse.bass as bass
import concourse.mybir as mybir
from concourse.bass_utils import run_bass_kernel_spmd

F32 = mybir.dt.float32
BF16 = mybir.dt.bfloat16
I32 = mybir.dt.int32
AF = mybir.ActivationFunctionType
ALU = mybir.AluOpType
AXX = mybir.AxisListType.X

NCORE = 8
BSEQ = 2
L = 2048
NT = BSEQ * L
D = 1024
DI = 2048
N0 = 16
N1 = 64
R0 = 64
F1 = 4096
TC = 512
NCH = L // TC
DT = D // 128
DIT = DI // 128
F1T = F1 // 128

ENGS = ("pe", "act", "dve", "pool", "sp")
ENG2SEC = {"pe": "tensor", "act": "scalar", "dve": "vector", "pool": "gpsimd",
           "sp": "sync"}
ESEM = {e: "s_" + e for e in ENGS}


class Buf:
    __slots__ = ("ap", "writers", "readers", "name")

    def __init__(self, ap, name=""):
        self.ap = ap
        self.writers = []
        self.readers = []
        self.name = name


class Op:
    __slots__ = ("eng", "fn", "deps", "seq", "queue", "fence", "is_fence",
                 "qsem")


QENG = {"d_sp": "sp", "d_pool": "pool"}


class Plan:

    def __init__(self, nc):
        self.nc = nc
        self.ops = {e: [] for e in ENGS}
        self.ntok = {e: 0 for e in ENGS}
        self.qgroup = {"d_sp": [], "d_pool": []}
        self.nfence = {"d_sp": 0, "d_pool": 0}
        self.qlastfence = {"d_sp": None, "d_pool": None}
        self.nop_sem = None

    def _resolve(self, d):
        if isinstance(d, tuple) or not getattr(d, "queue", None):
            return d
        if d.fence is None:
            self.fence(d.queue)
        return d.fence

    def op(self, eng, fn, deps=(), reads=(), writes=(), rw=(), queue=None,
           is_fence=False):
        dd = []
        for d in deps:
            if d is not None:
                dd.append(d)
        for b in reads:
            dd.extend(b.writers)
        for b in rw:
            dd.extend(b.writers)
            dd.extend(b.readers)
        for b in writes:
            dd.extend(b.writers)
            dd.extend(b.readers)
        if not is_fence:
            dd = [self._resolve(d) for d in dd]
        o = Op()
        o.eng = eng
        o.fn = fn
        o.deps = dd
        o.queue = queue
        o.fence = None
        o.qsem = None
        o.is_fence = is_fence
        if queue is None and not is_fence:
            self.ntok[eng] += 1
            o.seq = self.ntok[eng]
        else:
            o.seq = None
        self.ops[eng].append(o)
        if queue is not None:
            self.qgroup[queue].append(o)
        for b in reads:
            b.readers.append(o)
        for b in rw:
            b.writers = [o]
            b.readers = []
        for b in writes:
            b.writers = [o]
            b.readers = []
        return o

    def fence(self, queue):
        grp = self.qgroup[queue]
        if not grp:
            return self.qlastfence[queue]
        self.qgroup[queue] = []
        rot = self.nfence[queue] % 4
        qsem = f"{queue}{rot}"
        for d in grp:
            d.qsem = qsem
        f = self.op(QENG[queue], ("FENCE", qsem), is_fence=True)
        self.nfence[queue] += 1
        f.seq = ("f_" + queue[2:], self.nfence[queue])
        for d in grp:
            d.fence = f
        self.qlastfence[queue] = f
        return f

    def dma(self, eng, dst, src, reads=(), writes=(), rw=(), deps=(), **kw):
        q = "d_sp" if eng == "sp" else "d_pool"
        dap = dst.ap if isinstance(dst, Buf) else dst
        sap = src.ap if isinstance(src, Buf) else src
        rd = list(reads) + ([src] if isinstance(src, Buf) else [])
        wr = list(writes) + ([dst] if isinstance(dst, Buf) and not rw else [])
        return self.op(eng, lambda h: h.dma_start(out=dap, in_=sap, **kw),
                       deps=deps, reads=rd, writes=wr, rw=rw, queue=q)

    def barrier(self):
        self.fence("d_sp")
        self.fence("d_pool")
        tails = [ops[-1] for ops in self.ops.values() if ops]
        bar = []
        for e in ENGS:
            bar.append(self.op(e, None, deps=tails))
        tails2 = list(bar)
        for e in ENGS:
            self.op(e, None, deps=tails2)

    def token(self, o):
        if isinstance(o, tuple):
            return o
        assert o.seq is not None, "dma token must resolve through its fence"
        if isinstance(o.seq, tuple):
            return o.seq
        return (ESEM[o.eng], o.seq)

    def emit(self, ctx):
        qtotal = {}
        for q in QENG:
            for r in range(4):
                qtotal[f"{q}{r}"] = [0]
        for eng in ENGS:
            ops = self.ops[eng]
            if not ops:
                continue

            def body(h, eng=eng, ops=ops):
                hw = ctx.waited[eng]
                own = ESEM[eng]
                for o in ops:
                    need = {}
                    for d in o.deps:
                        s, v = self.token(d)
                        if need.get(s, 0) < v:
                            need[s] = v
                    for s in sorted(need):
                        v = need[s]
                        if hw.get(s, 0) >= v:
                            continue
                        h.wait_ge(ctx.sem[s], v)
                        hw[s] = v
                    if isinstance(o.fn, tuple):
                        q = o.fn[1]
                        h.wait_ge(ctx.sem[q], qtotal[q][0])
                        h.sem_inc(ctx.sem[o.seq[0]], 1)
                    elif o.fn is None:
                        h.sem_inc(ctx.sem[own], 1)
                    else:
                        inst = o.fn(h)
                        if o.queue is not None:
                            inst.then_inc(ctx.sem[o.qsem], 16)
                            qtotal[o.qsem][0] += 16
                        else:
                            inst.then_inc(ctx.sem[own], 1)
            getattr(ctx.block, ENG2SEC[eng])(body)


class Ctx:
    def __init__(self, nc, block):
        self.nc = nc
        self.block = block
        self.sem = {}
        names = [ESEM[e] for e in ENGS] + ["f_sp", "f_pool"]
        for q in ("d_sp", "d_pool"):
            names += [f"{q}{r}" for r in range(4)]
        for name in names:
            self.sem[name] = nc.semaphore(name).__enter__()
        self.waited = {e: {} for e in ENGS}



def build_program(debug=(), upto=10 ** 9):
    nc = bass.Bass("TRN2", target_bir_lowering=False)
    dbg = set(debug)

    def din(name, shape, dtype=F32):
        return nc.dram_tensor(name, shape, dtype, kind="ExternalInput").ap()

    def dscr(name, shape, dtype=F32):
        kind = "ExternalOutput" if name in dbg else "Internal"
        return nc.dram_tensor(name, shape, dtype, kind=kind).ap()

    io = {}
    io["x0"] = din("x0", [D, NT])
    io["ln_g"] = din("ln_g", [1, D])
    io["ln_b"] = din("ln_b", [1, D])
    for d in ("f", "b"):
        io[f"{d}_inT"] = din(f"{d}_inT", [D, 2 * DI])
        io[f"{d}_convw"] = din(f"{d}_convw", [128, DIT * 4])
        io[f"{d}_convb"] = din(f"{d}_convb", [128, DIT])
        io[f"{d}_xwT"] = din(f"{d}_xwT", [DI, R0 + 2 * N0])
        io[f"{d}_dtwT"] = din(f"{d}_dtwT", [R0, DI])
        io[f"{d}_dtb"] = din(f"{d}_dtb", [128, DIT])
        io[f"{d}_alog"] = din(f"{d}_alog", [DI, N0])
        io[f"{d}_dp"] = din(f"{d}_dp", [128, DIT])
        io[f"{d}_outT"] = din(f"{d}_outT", [DI, D])
    io["combT"] = din("combT", [2 * D, D])
    io["combb"] = din("combb", [128, DT])
    io["ffn_k"] = din("ffn_k", [D, D])
    io["ffn_bias"] = din("ffn_bias", [128, DT])
    io["ffn_a"] = din("ffn_a", [D, 16])
    io["ffn_b"] = din("ffn_b", [16, D])
    io["in1T"] = din("in1T", [D, 4288])
    io["conv1w"] = din("conv1w", [128, DIT * 4])
    io["conv1b"] = din("conv1b", [128, DIT])
    io["alog1b"] = din("alog1b", [128, N1])
    io["dp1"] = din("dp1", [128, DIT])
    io["dt1T"] = din("dt1T", [R0, DI])
    io["dt1b"] = din("dt1b", [128, DIT])
    io["out1T"] = din("out1T", [DI, D])
    io["w1f"] = din("w1f", [F1, D])
    io["w1T"] = din("w1T", [D, F1])
    io["b1"] = din("b1", [128, F1T])
    io["w2f"] = din("w2f", [D, F1])
    io["w2T"] = din("w2T", [F1, D])
    io["b2"] = din("b2", [128, DT])
    io["ident"] = din("ident", [128, 128])

    io["xout"] = nc.dram_tensor("xout", [D, NT], F32, kind="ExternalOutput").ap()

    io["xn_s"] = dscr("xn_s", [D, NT])
    for d in ("f", "b"):
        io[f"xi_{d}_s"] = dscr(f"xi_{d}_s", [DI, NT])
        io[f"z_{d}_s"] = dscr(f"z_{d}_s", [DI, NT])
        io[f"xc_{d}_s"] = dscr(f"xc_{d}_s", [DI, NT])
        io[f"dl_{d}_s"] = dscr(f"dl_{d}_s", [DI, NT])
        io[f"y_{d}_s"] = dscr(f"y_{d}_s", [DI, NT])
        io[f"bc_{d}_s"] = dscr(f"bc_{d}_s", [2 * N0, NT])
        io[f"dtr_{d}_s"] = dscr(f"dtr_{d}_s", [R0, NT])
        io[f"o_{d}_s"] = dscr(f"o_{d}_s", [D, NT])
    io["x1_s"] = dscr("x1_s", [D, NT])
    io["x2_s"] = dscr("x2_s", [D, NT])
    io["x3_s"] = dscr("x3_s", [D, NT])
    io["wq0_s"] = dscr("wq0_s", [D, D])
    io["xi1_s"] = dscr("xi1_s", [DI, NT])
    io["z1_s"] = dscr("z1_s", [DI, NT])
    io["xc1_s"] = dscr("xc1_s", [DI, NT])
    io["dl1_s"] = dscr("dl1_s", [DI, NT])
    io["y1_s"] = dscr("y1_s", [DI, NT])
    io["bc1_s"] = dscr("bc1_s", [2 * N1, NT])
    io["dtr1_s"] = dscr("dtr1_s", [R0, NT])
    io["wq1T_s"] = dscr("wq1T_s", [D, F1], BF16)
    io["wq2T_s"] = dscr("wq2T_s", [F1, D], BF16)
    io["s1_s"] = dscr("s1_s", [F1, 1])
    io["s2_s"] = dscr("s2_s", [D, 1])
    io["h_s"] = dscr("h_s", [F1, NT])
    for tag, n in (("1", F1T), ("2", DT)):
        io[f"qd{tag}_sin"] = dscr(f"qd{tag}_sin", [128, 1])
        io[f"qd{tag}_max"] = dscr(f"qd{tag}_max", [128, 1])
        io[f"qd{tag}_se"] = dscr(f"qd{tag}_se", [128, n])
        io[f"qd{tag}_be"] = dscr(f"qd{tag}_be", [128, n])
    io["cc_in"] = nc.dram_tensor("cc_in", [1, 1], F32, kind="Internal").ap()
    io["cc_out"] = nc.dram_tensor("cc_out", [1, 1], F32, kind="Internal").ap()

    DB = {k: Buf(v, k) for k, v in io.items()}

    P = {}

    def psb(name, shape, dtype=F32):
        P[name] = Buf(nc.alloc_sbuf_tensor("p_" + name, shape, dtype).ap(), name)
        return P[name]

    psb("onesc", [1, 128])
    psb("onesr", [1, TC])
    psb("ones1", [128, 1])
    psb("ident", [128, 128])
    psb("A0f", [128, DIT * N0])
    psb("A0b", [128, DIT * N0])
    psb("A1", [128, N1])
    psb("sw1", [128, F1T])
    psb("sw2", [128, DT])
    psb("b1", [128, F1T])
    psb("b2", [128, DT])
    psb("rcp1", [128, 1])
    psb("seff1", [128, F1T])
    psb("beff1", [128, F1T])
    psb("rcp2", [128, 1])
    psb("seff2", [128, DT])
    psb("beff2", [128, DT])
    psb("gmx", [128, 1])
    psb("ghx", [128, 1])

    with nc.Block() as block:
        ctx = Ctx(nc, block)
        plan = Plan(nc)
        plan.nop_sem = nc.semaphore("nop").__enter__()
        emit_all(plan, nc, io, DB, P, upto=upto)
        plan.emit(ctx)
    return nc


_UNIQ = [0]


def sb(nc, st, name, shape, dtype=F32):
    _UNIQ[0] += 1
    return Buf(st.enter_context(
        nc.sbuf_tensor(f"{name}_{_UNIQ[0]}", shape, dtype)).ap(), name)


def ps(nc, st, name, shape, dtype=F32):
    _UNIQ[0] += 1
    return Buf(st.enter_context(
        nc.psum_tensor(f"{name}_{_UNIQ[0]}", shape, dtype)).ap(), name)



def emit_all(plan, nc, io, DB, P, upto=10 ** 9):
    stage_i = [0]

    def gate():
        stage_i[0] += 1
        return stage_i[0] <= upto

    if not gate():
        return
    emit_prep(plan, nc, io, DB, P)
    plan.barrier()
    if not gate():
        return
    emit_prep_wq0(plan, nc, io, DB, P)
    plan.barrier()
    if not gate():
        return
    emit_prep_quantw(plan, nc, io, DB, P, "w1f", "w1T", F1, D, P["sw1"],
                     DB["wq1T_s"], DB["s1_s"], F1T)
    plan.barrier()
    if not gate():
        return
    emit_prep_quantw(plan, nc, io, DB, P, "w2f", "w2T", D, F1, P["sw2"],
                     DB["wq2T_s"], DB["s2_s"], DT)
    plan.barrier()
    for seq in range(BSEQ):
        if not gate():
            return
        emit_preln(plan, nc, io, DB, P, seq)
        plan.barrier()
    for seq in range(BSEQ):
        for d in ("f", "b"):
            if not gate():
                return
            emit_inproj0(plan, nc, io, DB, P, seq, d)
            plan.barrier()
            if not gate():
                return
            emit_conv(plan, nc, io, DB, seq, DB[f"xi_{d}_s"], DB[f"xc_{d}_s"],
                      io[f"{d}_convw"], io[f"{d}_convb"], DB[f"{d}_convw"],
                      DB[f"{d}_convb"], fwd=(d == "f"))
            plan.barrier()
            if not gate():
                return
            emit_xw0(plan, nc, io, DB, P, seq, d)
            plan.barrier()
            if not gate():
                return
            emit_dtproj(plan, nc, io, DB, P, seq,
                        io[f"{d}_dtwT"], DB[f"{d}_dtwT"],
                        io[f"{d}_dtb"], DB[f"{d}_dtb"],
                        DB[f"dtr_{d}_s"], DB[f"dl_{d}_s"])
            plan.barrier()
            if not gate():
                return
            emit_scan0(plan, nc, io, DB, P, seq, d)
            plan.barrier()
            if not gate():
                return
            emit_gating(plan, nc, io, DB, seq, DB[f"y_{d}_s"], DB[f"z_{d}_s"])
            plan.barrier()
            if not gate():
                return
            emit_proj_simple(plan, nc, io, DB, P, seq, io[f"{d}_outT"],
                             DB[f"{d}_outT"], DB[f"y_{d}_s"], DIT, DT,
                             DB[f"o_{d}_s"], name=f"op0{seq}{d}")
            plan.barrier()
    for seq in range(BSEQ):
        if not gate():
            return
        emit_combine(plan, nc, io, DB, P, seq)
        plan.barrier()
    for seq in range(BSEQ):
        if not gate():
            return
        emit_ffn0(plan, nc, io, DB, P, seq)
        plan.barrier()
    for seq in range(BSEQ):
        if not gate():
            return
        emit_inproj1(plan, nc, io, DB, P, seq)
        plan.barrier()
        if not gate():
            return
        emit_conv(plan, nc, io, DB, seq, DB["xi1_s"], DB["xc1_s"],
                  io["conv1w"], io["conv1b"], DB["conv1w"], DB["conv1b"],
                  fwd=True)
        plan.barrier()
        if not gate():
            return
        emit_dtproj(plan, nc, io, DB, P, seq, io["dt1T"], DB["dt1T"],
                    io["dt1b"], DB["dt1b"], DB["dtr1_s"], DB["dl1_s"])
        plan.barrier()
        if not gate():
            return
        emit_scan1(plan, nc, io, DB, P, seq)
        plan.barrier()
        if not gate():
            return
        emit_gating(plan, nc, io, DB, seq, DB["y1_s"], DB["z1_s"])
        plan.barrier()
        if not gate():
            return
        emit_outproj1_resln(plan, nc, io, DB, P, seq)
        plan.barrier()
    if not gate():
        return
    emit_quantprep(plan, nc, io, DB, P, P["gmx"], P["rcp1"], P["sw1"], P["b1"],
                   P["seff1"], P["beff1"], F1T,
                   (DB["qd1_sin"], DB["qd1_max"], DB["qd1_se"], DB["qd1_be"]))
    plan.barrier()
    for seq in range(BSEQ):
        if not gate():
            return
        emit_ffn1a(plan, nc, io, DB, P, seq)
        plan.barrier()
    if not gate():
        return
    emit_quantprep(plan, nc, io, DB, P, P["ghx"], P["rcp2"], P["sw2"], P["b2"],
                   P["seff2"], P["beff2"], DT,
                   (DB["qd2_sin"], DB["qd2_max"], DB["qd2_se"], DB["qd2_be"]))
    plan.barrier()
    for seq in range(BSEQ):
        if not gate():
            return
        emit_ffn1b(plan, nc, io, DB, P, seq)
        plan.barrier()



def emit_prep(plan, nc, io, DB, P):
    plan.op("dve", lambda h: h.memset(P["onesc"].ap, 1.0), writes=[P["onesc"]])
    plan.op("dve", lambda h: h.memset(P["onesr"].ap, 1.0), writes=[P["onesr"]])
    plan.op("dve", lambda h: h.memset(P["ones1"].ap, 1.0), writes=[P["ones1"]])
    plan.op("dve", lambda h: h.memset(P["gmx"].ap, -1e30), writes=[P["gmx"]])
    plan.op("dve", lambda h: h.memset(P["ghx"].ap, -1e30), writes=[P["ghx"]])
    plan.dma("sp", P["ident"], io["ident"], reads=[DB["ident"]])
    plan.dma("sp", P["b1"], io["b1"], reads=[DB["b1"]])
    plan.dma("sp", P["b2"], io["b2"], reads=[DB["b2"]])
    with contextlib.ExitStack() as st:
        at = sb(nc, st, "prep_at", [128, N0])
        for d in ("f", "b"):
            dst = P["A0f"] if d == "f" else P["A0b"]
            for dt in range(DIT):
                plan.dma("sp", at, io[f"{d}_alog"][dt * 128:(dt + 1) * 128, :],
                         reads=[DB[f"{d}_alog"]])
                plan.op("act",
                        lambda h, dt=dt, dst=dst: h.activation(
                            dst.ap[:, dt * N0:(dt + 1) * N0], at.ap, AF.Exp),
                        reads=[at], rw=[dst])
    plan.dma("sp", P["A1"], io["alog1b"], reads=[DB["alog1b"]])
    plan.op("act", lambda h: h.activation(P["A1"].ap, P["A1"].ap, AF.Exp),
            rw=[P["A1"]])


def round_nearest(plan, buf, ibuf, t1, t2):
    plan.op("dve", lambda h: h.tensor_copy(ibuf.ap, buf.ap),
            reads=[buf], writes=[ibuf])
    plan.op("dve", lambda h: h.tensor_copy(t1.ap, ibuf.ap),
            reads=[ibuf], writes=[t1])
    plan.op("dve", lambda h: h.tensor_tensor(buf.ap, buf.ap, t1.ap,
                                             ALU.subtract),
            reads=[t1], rw=[buf])
    plan.op("dve", lambda h: h.tensor_scalar(t2.ap, buf.ap, 0.5, None,
                                             ALU.is_gt),
            reads=[buf], writes=[t2])
    plan.op("dve", lambda h: h.tensor_scalar(buf.ap, buf.ap, -0.5, None,
                                             ALU.is_lt), rw=[buf])
    plan.op("dve", lambda h: h.tensor_tensor(t1.ap, t1.ap, t2.ap, ALU.add),
            reads=[t2], rw=[t1])
    plan.op("dve", lambda h: h.tensor_tensor(buf.ap, t1.ap, buf.ap,
                                             ALU.subtract),
            reads=[t1], rw=[buf])


def emit_prep_wq0(plan, nc, io, DB, P):
    with contextlib.ExitStack() as st:
        t = [sb(nc, st, f"wq_t{j}", [128, D]) for j in range(2)]
        sg = sb(nc, st, "wq_sg", [128, D])
        ti = sb(nc, st, "wq_ti", [128, D], I32)
        mx = sb(nc, st, "wq_mx", [128, 1])
        mx2 = sb(nc, st, "wq_mx2", [128, 1])
        sc = sb(nc, st, "wq_sc", [128, 1])
        rsc = sb(nc, st, "wq_rsc", [128, 1])
        srow = sb(nc, st, "wq_srow", [1, 1])
        pt = ps(nc, st, "wq_pt", [1, 128])
        pb = ps(nc, st, "wq_pb", [128, 1])
        k = io["ffn_k"]
        for dt in range(DT):
            b = t[dt % 2]
            plan.dma("sp", b, k[dt * 128:(dt + 1) * 128, :], reads=[DB["ffn_k"]])
            if dt == 0:
                plan.op("dve", lambda h: h.reduce_max(
                    mx.ap, t[0].ap, axis=AXX, apply_absolute_value=True),
                    reads=[t[0]], writes=[mx])
            else:
                plan.op("dve", lambda h, b=b: h.reduce_max(
                    mx2.ap, b.ap, axis=AXX, apply_absolute_value=True),
                    reads=[b], writes=[mx2])
                plan.op("dve", lambda h: h.tensor_max(mx.ap, mx.ap, mx2.ap),
                        reads=[mx2], rw=[mx])
        plan.op("pe", lambda h: h.transpose(pt.ap, mx.ap, P["ident"].ap),
                reads=[mx, P["ident"]], writes=[pt])
        plan.op("dve", lambda h: h.reduce_max(srow.ap, pt.ap, axis=AXX),
                reads=[pt], writes=[srow])
        plan.op("dve", lambda h: h.tensor_scalar(srow.ap, srow.ap, 1.0 / 7.0,
                                                 None, ALU.mult), rw=[srow])
        plan.op("pe", lambda h: h.matmul(pb.ap, P["onesc"].ap, srow.ap,
                                         start=True, stop=True),
                reads=[srow, P["onesc"]], writes=[pb])
        plan.op("dve", lambda h: h.tensor_copy(sc.ap, pb.ap),
                reads=[pb], writes=[sc])
        plan.op("dve", lambda h: h.reciprocal(rsc.ap, sc.ap),
                reads=[sc], writes=[rsc])
        sg2 = sb(nc, st, "wq_sg2", [128, D])
        for dt in range(DT):
            b = t[dt % 2]
            plan.dma("sp", b, k[dt * 128:(dt + 1) * 128, :], reads=[DB["ffn_k"]])
            plan.op("dve", lambda h, b=b: h.tensor_scalar(
                b.ap, b.ap, rsc.ap[:, 0:1], None, ALU.mult),
                reads=[rsc], rw=[b])
            round_nearest(plan, b, ti, sg, sg2)
            plan.op("dve", lambda h, b=b: h.tensor_scalar(
                b.ap, b.ap, -8.0, 7.0, ALU.max, ALU.min), rw=[b])
            plan.op("dve", lambda h, b=b: h.tensor_scalar(
                b.ap, b.ap, sc.ap[:, 0:1], None, ALU.mult),
                reads=[sc], rw=[b])
            plan.dma("pool", io["wq0_s"][dt * 128:(dt + 1) * 128, :], b,
                     writes=[DB["wq0_s"]])


def emit_prep_quantw(plan, nc, io, DB, P, wf_name, wT_name, NF, ND, s_sb,
                     outdb, srowdb, NFT):
    with contextlib.ExitStack() as st:
        t = [sb(nc, st, f"q_t{j}{wf_name}", [128, ND]) for j in range(2)]
        mx = sb(nc, st, f"q_mx{wf_name}", [128, 1])
        wf = io[wf_name]
        for m in range(NFT):
            b = t[m % 2]
            plan.dma("sp", b, wf[m * 128:(m + 1) * 128, :], reads=[DB[wf_name]])
            plan.op("dve", lambda h, b=b: h.reduce_max(
                mx.ap, b.ap, axis=AXX, apply_absolute_value=True),
                reads=[b], writes=[mx])
            plan.op("dve", lambda h, m=m: h.tensor_scalar(
                s_sb.ap[:, m:m + 1], mx.ap, 1e-8, 1.0 / 127.0, ALU.max,
                ALU.mult), reads=[mx], rw=[s_sb])
            plan.op("dve", lambda h, m=m: h.reciprocal(
                mx.ap, s_sb.ap[:, m:m + 1]), reads=[s_sb], writes=[mx])
            plan.dma("pool", srowdb.ap[m * 128:(m + 1) * 128, 0:1],
                     mx.ap, reads=[mx], writes=[srowdb])
        NDT = ND // 128
        CH = NF // TC
        w = [sb(nc, st, f"q_w{j}{wf_name}", [128, TC]) for j in range(2)]
        sr = [sb(nc, st, f"q_sr{j}{wf_name}", [128, TC]) for j in range(2)]
        sg = sb(nc, st, f"q_sg{wf_name}", [128, TC])
        sg2 = sb(nc, st, f"q_sg2{wf_name}", [128, TC])
        ti = sb(nc, st, f"q_ti{wf_name}", [128, TC], I32)
        ob = sb(nc, st, f"q_ob{wf_name}", [128, TC], BF16)
        wT = io[wT_name]
        i_s = 0
        for c in range(CH):
            for kd in range(NDT):
                bw, bs = w[i_s % 2], sr[i_s % 2]
                plan.dma("sp", bw,
                         wT[kd * 128:(kd + 1) * 128, c * TC:(c + 1) * TC],
                         reads=[DB[wT_name]])
                plan.dma("sp", bs,
                         srowdb.ap[c * TC:(c + 1) * TC, 0:1]
                         .transpose([1, 0]).broadcast_to([128, TC]),
                         reads=[srowdb], writes=[bs])
                plan.op("dve", lambda h, bw=bw, bs=bs: h.tensor_tensor(
                    bw.ap, bw.ap, bs.ap, ALU.mult), reads=[bs], rw=[bw])
                round_nearest(plan, bw, ti, sg, sg2)
                plan.op("dve", lambda h, bw=bw: h.tensor_scalar(
                    bw.ap, bw.ap, -128.0, 127.0, ALU.max, ALU.min), rw=[bw])
                plan.op("dve", lambda h, bw=bw: h.tensor_copy(ob.ap, bw.ap),
                        reads=[bw], writes=[ob])
                plan.dma("pool",
                         outdb.ap[kd * 128:(kd + 1) * 128, c * TC:(c + 1) * TC],
                         ob, reads=[ob], writes=[outdb])
                i_s += 1



class LNScratch:
    def __init__(self, plan, nc, st, tag):
        self.p_sx = ps(nc, st, f"ln_psx{tag}", [1, TC])
        self.p_sq = ps(nc, st, f"ln_psq{tag}", [1, TC])
        self.p_ar = ps(nc, st, f"ln_par{tag}", [128, TC])
        self.p_br = ps(nc, st, f"ln_pbr{tag}", [128, TC])
        self.sq = sb(nc, st, f"ln_sq{tag}", [128, TC])
        self.rows = [sb(nc, st, f"ln_r{j}{tag}", [1, TC]) for j in range(4)]
        self.mxr = sb(nc, st, f"ln_mx{tag}", [128, 1])


def emit_postln(plan, P, lns, tiles, dst_fn, dstdb, g_ap=None, b_ap=None,
                gdeps=(), maxacc=None):
    r_mean, r_var, r_b, r_rstd = lns.rows
    for dt in range(DT):
        tl = tiles[dt]
        plan.op("pe", lambda h, dt=dt, tl=tl: h.matmul(
            lns.p_sx.ap, P["ones1"].ap, tl.ap, start=(dt == 0),
            stop=(dt == DT - 1)),
            reads=[tl, P["ones1"]], writes=[lns.p_sx] if dt == 0 else (),
            rw=() if dt == 0 else ())
        plan.op("act", lambda h, tl=tl: h.activation(lns.sq.ap, tl.ap,
                                                     AF.Square),
                reads=[tl], writes=[lns.sq])
        plan.op("pe", lambda h, dt=dt: h.matmul(
            lns.p_sq.ap, P["ones1"].ap, lns.sq.ap, start=(dt == 0),
            stop=(dt == DT - 1)),
            reads=[lns.sq, P["ones1"]], writes=[lns.p_sq] if dt == 0 else ())
    last_sx = plan.ops["pe"][-1]
    lns.p_sx.writers = [plan.ops["pe"][-2]]
    lns.p_sq.writers = [last_sx]
    plan.op("dve", lambda h: h.tensor_scalar(r_mean.ap, lns.p_sx.ap, 1.0 / D,
                                             None, ALU.mult),
            reads=[lns.p_sx], writes=[r_mean])
    plan.op("dve", lambda h: h.tensor_scalar(r_var.ap, lns.p_sq.ap, 1.0 / D,
                                             1e-5, ALU.mult, ALU.add),
            reads=[lns.p_sq], writes=[r_var])
    plan.op("dve", lambda h: h.tensor_tensor(r_b.ap, r_mean.ap, r_mean.ap,
                                             ALU.mult),
            reads=[r_mean], writes=[r_b])
    plan.op("dve", lambda h: h.tensor_tensor(r_var.ap, r_var.ap, r_b.ap,
                                             ALU.subtract),
            reads=[r_b], rw=[r_var])
    plan.op("act", lambda h: h.activation(r_rstd.ap, r_var.ap, AF.Sqrt),
            reads=[r_var], writes=[r_rstd])
    plan.op("dve", lambda h: h.reciprocal(r_rstd.ap, r_rstd.ap), rw=[r_rstd])
    for _ in range(2):
        plan.op("dve", lambda h: h.tensor_tensor(r_b.ap, r_rstd.ap, r_rstd.ap,
                                                 ALU.mult),
                reads=[r_rstd], writes=[r_b])
        plan.op("dve", lambda h: h.tensor_tensor(r_b.ap, r_b.ap, r_var.ap,
                                                 ALU.mult),
                reads=[r_var], rw=[r_b])
        plan.op("dve", lambda h: h.tensor_scalar(r_b.ap, r_b.ap, -0.5, 1.5,
                                                 ALU.mult, ALU.add), rw=[r_b])
        plan.op("dve", lambda h: h.tensor_tensor(r_rstd.ap, r_rstd.ap, r_b.ap,
                                                 ALU.mult),
                reads=[r_b], rw=[r_rstd])
    plan.op("dve", lambda h: h.tensor_tensor(r_b.ap, r_mean.ap, r_rstd.ap,
                                             ALU.mult),
            reads=[r_mean, r_rstd], writes=[r_b])
    plan.op("dve", lambda h: h.tensor_scalar(r_b.ap, r_b.ap, -1.0, None,
                                             ALU.mult), rw=[r_b])
    if g_ap is None:
        plan.op("pe", lambda h: h.matmul(lns.p_ar.ap, P["onesc"].ap, r_rstd.ap,
                                         start=True, stop=True),
                reads=[r_rstd, P["onesc"]], writes=[lns.p_ar])
        plan.op("pe", lambda h: h.matmul(lns.p_br.ap, P["onesc"].ap, r_b.ap,
                                         start=True, stop=True),
                reads=[r_b, P["onesc"]], writes=[lns.p_br])
    for dt in range(DT):
        tl = tiles[dt]
        if g_ap is not None:
            plan.op("pe", lambda h, dt=dt: h.matmul(
                lns.p_ar.ap, g_ap[:, dt * 128:(dt + 1) * 128], r_rstd.ap,
                start=True, stop=True),
                deps=list(gdeps), reads=[r_rstd], writes=[lns.p_ar])
            plan.op("pe", lambda h, dt=dt: h.matmul(
                lns.p_br.ap, g_ap[:, dt * 128:(dt + 1) * 128], r_b.ap,
                start=True, stop=False),
                deps=list(gdeps), reads=[r_b], writes=[lns.p_br])
            o = plan.op("pe", lambda h, dt=dt: h.matmul(
                lns.p_br.ap, b_ap[:, dt * 128:(dt + 1) * 128], P["onesr"].ap,
                start=False, stop=True),
                deps=list(gdeps), reads=[P["onesr"]])
            lns.p_br.writers = [o]
            lns.p_br.readers = []
        plan.op("dve", lambda h, tl=tl: h.tensor_tensor(
            tl.ap, tl.ap, lns.p_ar.ap, ALU.mult), reads=[lns.p_ar], rw=[tl])
        plan.op("dve", lambda h, tl=tl: h.tensor_tensor(
            tl.ap, tl.ap, lns.p_br.ap, ALU.add), reads=[lns.p_br], rw=[tl])
        if maxacc is not None:
            plan.op("dve", lambda h, tl=tl: h.reduce_max(lns.mxr.ap, tl.ap,
                                                         axis=AXX),
                    reads=[tl], writes=[lns.mxr])
            plan.op("dve", lambda h: h.tensor_max(maxacc.ap, maxacc.ap,
                                                  lns.mxr.ap),
                    reads=[lns.mxr], rw=[maxacc])
        plan.dma("pool", dst_fn(dt), tl, reads=[tl], writes=[dstdb])


def emit_preln(plan, nc, io, DB, P, seq):
    with contextlib.ExitStack() as st:
        tiles = [sb(nc, st, f"pl_t{j}", [128, TC]) for j in range(DT)]
        gsb = sb(nc, st, "pl_g", [1, D])
        bsb = sb(nc, st, "pl_b", [1, D])
        lns = LNScratch(plan, nc, st, "pl")
        g1 = plan.dma("sp", gsb, io["ln_g"], reads=[DB["ln_g"]])
        g2 = plan.dma("sp", bsb, io["ln_b"], reads=[DB["ln_b"]])
        for c in range(NCH):
            col = seq * L + c * TC
            for dt in range(DT):
                plan.dma("sp", tiles[dt],
                         io["x0"][dt * 128:(dt + 1) * 128, col:col + TC],
                         reads=[DB["x0"]])
            emit_postln(plan, P, lns, tiles,
                        lambda dt, col=col: io["xn_s"][dt * 128:(dt + 1) * 128,
                                                       col:col + TC],
                        DB["xn_s"], g_ap=gsb.ap, b_ap=bsb.ap,
                        gdeps=[g1, g2, gsb.writers[0] if gsb.writers else None])



def mm_group(plan, psum, pairs, mw=None):
    n = len(pairs)
    first_deps_bufs = [psum]
    for k, (wb, wa, mb, ma) in enumerate(pairs):
        start = (k == 0)
        stop = (k == n - 1)
        if start:
            o = plan.op("pe", lambda h, wa=wa, ma=ma, start=start, stop=stop:
                        h.matmul(psum.ap if mw is None else psum.ap[0:mw, :],
                                 wa, ma, start=start, stop=stop),
                        reads=[wb, mb], writes=[psum])
        else:
            o = plan.op("pe", lambda h, wa=wa, ma=ma, start=start, stop=stop:
                        h.matmul(psum.ap if mw is None else psum.ap[0:mw, :],
                                 wa, ma, start=start, stop=stop),
                        reads=[wb, mb])
    psum.writers = [o]
    psum.readers = []
    return o


def emit_proj_simple(plan, nc, io, DB, P, seq, wT_ap, wT_db, mov_db, KT, MT,
                     dst_db, name=""):
    with contextlib.ExitStack() as st:
        wb = [sb(nc, st, f"pj_w{k}{name}", [128, MT * 128]) for k in range(KT)]
        mb = [[sb(nc, st, f"pj_m{s}_{k}{name}", [128, TC]) for k in range(KT)]
              for s in range(2)]
        pb = [ps(nc, st, f"pj_p{j}{name}", [128, TC]) for j in range(2)]
        eb = [sb(nc, st, f"pj_e{j}{name}", [128, TC]) for j in range(2)]
        for k in range(KT):
            plan.dma("sp", wb[k], wT_ap[k * 128:(k + 1) * 128, 0:MT * 128],
                     reads=[wT_db])
        for c in range(NCH):
            col = seq * L + c * TC
            ms = c % 2
            for k in range(KT):
                plan.dma("sp", mb[ms][k],
                         mov_db.ap[k * 128:(k + 1) * 128, col:col + TC],
                         reads=[mov_db])
            for m in range(MT):
                gi = c * MT + m
                bank = gi % 2
                mm_group(plan, pb[bank],
                         [(wb[k], wb[k].ap[:, m * 128:(m + 1) * 128],
                           mb[ms][k], mb[ms][k].ap) for k in range(KT)])
                e = eb[gi % 2]
                plan.op("act", lambda h, e=e, bank=bank: h.activation(
                    e.ap, pb[bank].ap, AF.Copy), reads=[pb[bank]], writes=[e])
                plan.dma("pool", dst_db.ap[m * 128:(m + 1) * 128, col:col + TC],
                         e, reads=[e], writes=[dst_db])


def emit_inproj0(plan, nc, io, DB, P, seq, d):
    with contextlib.ExitStack() as st:
        xn = [sb(nc, st, f"ip_xn{k}", [128, L]) for k in range(DT)]
        wb = [[sb(nc, st, f"ip_w{s}_{k}", [128, 128]) for k in range(DT)]
              for s in range(2)]
        pb = [ps(nc, st, f"ip_p{j}", [128, TC]) for j in range(2)]
        eb = [sb(nc, st, f"ip_e{j}", [128, TC]) for j in range(3)]
        for k in range(DT):
            plan.dma("sp", xn[k], io["xn_s"][k * 128:(k + 1) * 128,
                                             seq * L:(seq + 1) * L],
                     reads=[DB["xn_s"]])
        gi = 0
        for m in range(2 * DIT):
            ws = m % 2
            for k in range(DT):
                plan.dma("sp", wb[ws][k],
                         io[f"{d}_inT"][k * 128:(k + 1) * 128,
                                        m * 128:(m + 1) * 128],
                         reads=[DB[f"{d}_inT"]])
            for c in range(NCH):
                col = seq * L + c * TC
                bank = gi % 2
                mm_group(plan, pb[bank],
                         [(wb[ws][k], wb[ws][k].ap, xn[k],
                           xn[k].ap[:, c * TC:(c + 1) * TC]) for k in range(DT)])
                e = eb[gi % 3]
                plan.op("act", lambda h, e=e, bank=bank: h.activation(
                    e.ap, pb[bank].ap, AF.Copy), reads=[pb[bank]], writes=[e])
                dst = DB[f"xi_{d}_s"] if m < DIT else DB[f"z_{d}_s"]
                row = (m if m < DIT else m - DIT) * 128
                plan.dma("pool", dst.ap[row:row + 128, col:col + TC], e,
                         reads=[e], writes=[dst])
                gi += 1


def emit_conv(plan, nc, io, DB, seq, src_db, dst_db, convw_ap, convb_ap,
              convw_db, convb_db, fwd=True):
    with contextlib.ExitStack() as st:
        xb = [sb(nc, st, f"cv_x{j}", [128, L + 3]) for j in range(2)]
        ac = [sb(nc, st, f"cv_a{j}", [128, L]) for j in range(2)]
        wc = sb(nc, st, "cv_w", [128, DIT * 4])
        bc = sb(nc, st, "cv_b", [128, DIT])
        plan.dma("sp", wc, convw_ap, reads=[convw_db])
        plan.dma("sp", bc, convb_ap, reads=[convb_db])
        for kt in range(DIT):
            b = xb[kt % 2]
            a = ac[kt % 2]
            if fwd:
                plan.dma("sp", b.ap[:, 3:3 + L],
                         src_db.ap[kt * 128:(kt + 1) * 128,
                                   seq * L:(seq + 1) * L],
                         reads=[src_db], writes=[b])
                plan.op("dve", lambda h, b=b: h.memset(b.ap[:, 0:3], 0.0),
                        rw=[b])
            else:
                plan.dma("sp", b.ap[:, 0:L],
                         src_db.ap[kt * 128:(kt + 1) * 128,
                                   seq * L:(seq + 1) * L],
                         reads=[src_db], writes=[b])
                plan.op("dve", lambda h, b=b: h.memset(b.ap[:, L:L + 3], 0.0),
                        rw=[b])
            off0 = 0 if fwd else 3
            plan.op("dve", lambda h, b=b, a=a, kt=kt, off0=off0: h.tensor_scalar(
                a.ap, b.ap[:, off0:off0 + L], wc.ap[:, kt * 4:kt * 4 + 1],
                None, ALU.mult), reads=[b, wc], writes=[a])
            for j in range(1, 4):
                off = j if fwd else 3 - j
                plan.op("dve", lambda h, b=b, a=a, kt=kt, j=j, off=off:
                        h.scalar_tensor_tensor(
                            a.ap, b.ap[:, off:off + L],
                            wc.ap[:, kt * 4 + j:kt * 4 + j + 1], a.ap,
                            ALU.mult, ALU.add), reads=[b, wc], rw=[a])
            plan.op("act", lambda h, a=a, kt=kt: h.activation(
                a.ap, a.ap, AF.Silu, bias=bc.ap[:, kt:kt + 1]),
                reads=[bc], rw=[a])
            plan.dma("pool", dst_db.ap[kt * 128:(kt + 1) * 128,
                                       seq * L:(seq + 1) * L], a,
                     reads=[a], writes=[dst_db])


def emit_xw0(plan, nc, io, DB, P, seq, d):
    with contextlib.ExitStack() as st:
        wb = [sb(nc, st, f"xw_w{k}", [128, 96]) for k in range(DIT)]
        mb = [[sb(nc, st, f"xw_m{s}_{k}", [128, TC]) for k in range(DIT)]
              for s in range(2)]
        pb = [ps(nc, st, f"xw_p{j}", [96, TC]) for j in range(2)]
        eb = [sb(nc, st, f"xw_e{j}", [96, TC]) for j in range(2)]
        for k in range(DIT):
            plan.dma("sp", wb[k], io[f"{d}_xwT"][k * 128:(k + 1) * 128, :],
                     reads=[DB[f"{d}_xwT"]])
        for c in range(NCH):
            col = seq * L + c * TC
            s2 = c % 2
            for k in range(DIT):
                plan.dma("sp", mb[s2][k],
                         DB[f"xc_{d}_s"].ap[k * 128:(k + 1) * 128, col:col + TC],
                         reads=[DB[f"xc_{d}_s"]])
            mm_group(plan, pb[s2],
                     [(wb[k], wb[k].ap, mb[s2][k], mb[s2][k].ap)
                      for k in range(DIT)])
            e = eb[s2]
            plan.op("act", lambda h, e=e, s2=s2: h.activation(
                e.ap, pb[s2].ap, AF.Copy), reads=[pb[s2]], writes=[e])
            plan.dma("pool", DB[f"dtr_{d}_s"].ap[:, col:col + TC],
                     e.ap[0:64, :], reads=[e], writes=[DB[f"dtr_{d}_s"]])
            plan.dma("pool", DB[f"bc_{d}_s"].ap[:, col:col + TC],
                     e.ap[64:96, :], reads=[e], writes=[DB[f"bc_{d}_s"]])


def emit_dtproj(plan, nc, io, DB, P, seq, dtwT_ap, dtwT_db, dtb_ap, dtb_db,
                dtr_db, dl_db):
    with contextlib.ExitStack() as st:
        wb = [sb(nc, st, f"dp_w{j}", [64, 128]) for j in range(2)]
        mbb = sb(nc, st, "dp_m", [64, L])
        pb = [ps(nc, st, f"dp_p{j}", [128, TC]) for j in range(2)]
        sg = [sb(nc, st, f"dp_s{j}", [128, TC]) for j in range(2)]
        nb = sb(nc, st, "dp_nb", [128, DIT])
        bt = sb(nc, st, "dp_bt", [128, DIT])
        plan.dma("sp", bt, dtb_ap, reads=[dtb_db])
        plan.op("dve", lambda h: h.tensor_scalar(nb.ap, bt.ap, -1.0, None,
                                                 ALU.mult),
                reads=[bt], writes=[nb])
        plan.dma("sp", mbb, dtr_db.ap[:, seq * L:(seq + 1) * L],
                 reads=[dtr_db])
        gi = 0
        for m in range(DIT):
            ws = m % 2
            plan.dma("sp", wb[ws], dtwT_ap[:, m * 128:(m + 1) * 128],
                     reads=[dtwT_db])
            for c in range(NCH):
                col = seq * L + c * TC
                bank = gi % 2
                mm_group(plan, pb[bank],
                         [(wb[ws], wb[ws].ap, mbb,
                           mbb.ap[:, c * TC:(c + 1) * TC])])
                s = sg[gi % 2]
                plan.op("act", lambda h, s=s, bank=bank, m=m: h.activation(
                    s.ap, pb[bank].ap, AF.Sigmoid, scale=-1.0,
                    bias=nb.ap[:, m:m + 1]),
                    reads=[pb[bank], nb], writes=[s])
                plan.op("act", lambda h, s=s: h.activation(s.ap, s.ap, AF.Ln),
                        rw=[s])
                plan.dma("pool", dl_db.ap[m * 128:(m + 1) * 128, col:col + TC],
                         s, reads=[s], writes=[dl_db])
                gi += 1



def emit_scan(plan, nc, io, DB, P, seq, *, N, NG, TCS, dl_db, xc_db, bc_db,
              y_db, dp_ap, dp_db, a_col_fn, reverse, name):
    nh = L // TCS
    with contextlib.ExitStack() as st:
        brep = [sb(nc, st, f"sc_B{j}{name}", [128, TCS]) for j in range(NG)]
        crep = [sb(nc, st, f"sc_C{j}{name}", [128, TCS]) for j in range(NG)]
        dl = [sb(nc, st, f"sc_dl{j}{name}", [128, TCS]) for j in range(2)]
        xc = [sb(nc, st, f"sc_xc{j}{name}", [128, TCS]) for j in range(2)]
        du = sb(nc, st, f"sc_du{name}", [128, TCS])
        ya = sb(nc, st, f"sc_ya{name}", [128, TCS])
        da = [sb(nc, st, f"sc_da{j}{name}", [128, TCS]) for j in range(2)]
        bb = [sb(nc, st, f"sc_b{j}{name}", [128, TCS]) for j in range(2)]
        hh = [sb(nc, st, f"sc_h{j}{name}", [128, TCS]) for j in range(2)]
        hc = [sb(nc, st, f"sc_hc{j}{name}", [128, TCS]) for j in range(2)]
        dpb = sb(nc, st, f"sc_dp{name}", [128, DIT])
        assert nh == 1, "chunked scans unused now"
        plan.dma("sp", dpb, dp_ap, reads=[dp_db])
        chunks = list(range(nh)) if not reverse else list(range(nh - 1, -1, -1))
        for ci, ch in enumerate(chunks):
            col = seq * L + ch * TCS
            for g in range(N // NG):
                ns = [g * NG + j for j in range(NG)]
                for j, n in enumerate(ns):
                    plan.dma("sp", brep[j],
                             bc_db.ap[n:n + 1, col:col + TCS]
                             .broadcast_to([128, TCS]),
                             reads=[bc_db], writes=[brep[j]])
                    plan.dma("sp", crep[j],
                             bc_db.ap[N + n:N + n + 1, col:col + TCS]
                             .broadcast_to([128, TCS]),
                             reads=[bc_db], writes=[crep[j]])
                for dt in range(DIT):
                    dlb, xcb = dl[dt % 2], xc[dt % 2]
                    plan.dma("sp", dlb, dl_db.ap[dt * 128:(dt + 1) * 128,
                                                 col:col + TCS], reads=[dl_db])
                    plan.dma("sp", xcb, xc_db.ap[dt * 128:(dt + 1) * 128,
                                                 col:col + TCS], reads=[xc_db])
                    plan.op("dve", lambda h, dlb=dlb, xcb=xcb:
                            h.scalar_tensor_tensor(du.ap, xcb.ap, -1.0, dlb.ap,
                                                   ALU.mult, ALU.mult),
                            reads=[dlb, xcb], writes=[du])
                    if g == 0:
                        plan.op("dve", lambda h, xcb=xcb, dt=dt: h.tensor_scalar(
                            ya.ap, xcb.ap, dpb.ap[:, dt:dt + 1], None, ALU.mult),
                            reads=[xcb, dpb], writes=[ya])
                    def emit_pre(j):
                        aap = a_col_fn(dt, ns[j])
                        p = j % 2
                        dap, bbp = da[p], bb[p]
                        plan.op("act", lambda h, dlb=dlb, aap=aap, dap=dap:
                                h.activation(dap.ap, dlb.ap, AF.Exp, scale=aap),
                                reads=[dlb], writes=[dap])
                        plan.op("dve", lambda h, j=j, bbp=bbp: h.tensor_tensor(
                            bbp.ap, du.ap, brep[j].ap, ALU.mult),
                            reads=[du, brep[j]], writes=[bbp])

                    emit_pre(0)
                    for j, n in enumerate(ns):
                        if j + 1 < len(ns):
                            emit_pre(j + 1)
                        p = j % 2
                        dap, bbp, hhp, hcp = da[p], bb[p], hh[p], hc[p]
                        if not reverse:
                            plan.op("dve", lambda h, dap=dap, bbp=bbp, hhp=hhp:
                                    h.tensor_tensor_scan(hhp.ap, dap.ap,
                                                         bbp.ap, 0.0,
                                                         ALU.mult, ALU.add),
                                    reads=[dap, bbp], writes=[hhp])
                        else:
                            plan.op("dve", lambda h, dap=dap, bbp=bbp, hhp=hhp:
                                    h.tensor_tensor_scan(hhp.ap[:, ::-1],
                                                         dap.ap[:, ::-1],
                                                         bbp.ap[:, ::-1],
                                                         0.0, ALU.mult,
                                                         ALU.add),
                                    reads=[dap, bbp], writes=[hhp])
                        plan.op("dve", lambda h, j=j, hhp=hhp, hcp=hcp:
                                h.tensor_tensor(hcp.ap, hhp.ap, crep[j].ap,
                                                ALU.mult),
                                reads=[hhp, crep[j]], writes=[hcp])
                        if g == 0 or j > 0:
                            plan.op("dve", lambda h, hcp=hcp: h.tensor_tensor(
                                ya.ap, ya.ap, hcp.ap, ALU.add),
                                reads=[hcp], rw=[ya])
                        else:
                            plan.op("dve", lambda h, hcp=hcp: h.tensor_copy(
                                ya.ap, hcp.ap), reads=[hcp], writes=[ya])
                    if g == 0:
                        plan.dma("pool", y_db.ap[dt * 128:(dt + 1) * 128,
                                                 col:col + TCS], ya,
                                 reads=[ya], rw=[y_db])
                    else:
                        plan.dma("pool", y_db.ap[dt * 128:(dt + 1) * 128,
                                                 col:col + TCS], ya,
                                 reads=[ya], rw=[y_db], accum_op=ALU.add)


def emit_scan0(plan, nc, io, DB, P, seq, d):
    A = P["A0f"] if d == "f" else P["A0b"]

    def acol(dt, n):
        return A.ap[:, dt * N0 + n:dt * N0 + n + 1]

    emit_scan(plan, nc, io, DB, P, seq, N=N0, NG=4, TCS=L,
              dl_db=DB[f"dl_{d}_s"], xc_db=DB[f"xc_{d}_s"],
              bc_db=DB[f"bc_{d}_s"], y_db=DB[f"y_{d}_s"],
              dp_ap=io[f"{d}_dp"], dp_db=DB[f"{d}_dp"], a_col_fn=acol,
              reverse=(d == "b"), name=f"s0{d}")


def emit_scan1(plan, nc, io, DB, P, seq):
    def acol(dt, n):
        return P["A1"].ap[:, n:n + 1]

    emit_scan(plan, nc, io, DB, P, seq, N=N1, NG=4, TCS=L,
              dl_db=DB["dl1_s"], xc_db=DB["xc1_s"], bc_db=DB["bc1_s"],
              y_db=DB["y1_s"], dp_ap=io["dp1"], dp_db=DB["dp1"],
              a_col_fn=acol, reverse=False, name="s1")


def emit_gating(plan, nc, io, DB, seq, y_db, z_db):
    with contextlib.ExitStack() as st:
        yb = [sb(nc, st, f"gt_y{j}", [128, L]) for j in range(2)]
        zb = [sb(nc, st, f"gt_z{j}", [128, L]) for j in range(2)]
        for kt in range(DIT):
            y, z = yb[kt % 2], zb[kt % 2]
            plan.dma("sp", y, y_db.ap[kt * 128:(kt + 1) * 128,
                                      seq * L:(seq + 1) * L], reads=[y_db])
            plan.dma("sp", z, z_db.ap[kt * 128:(kt + 1) * 128,
                                      seq * L:(seq + 1) * L], reads=[z_db])
            plan.op("act", lambda h, z=z: h.activation(z.ap, z.ap, AF.Silu),
                    rw=[z])
            plan.op("dve", lambda h, y=y, z=z: h.tensor_tensor(
                y.ap, y.ap, z.ap, ALU.mult), reads=[z], rw=[y])
            plan.dma("pool", y_db.ap[kt * 128:(kt + 1) * 128,
                                     seq * L:(seq + 1) * L], y,
                     reads=[y], rw=[y_db])



def emit_combine(plan, nc, io, DB, P, seq):
    with contextlib.ExitStack() as st:
        wb = [sb(nc, st, f"cb_w{k}", [128, D]) for k in range(2 * DT)]
        mb = [[sb(nc, st, f"cb_m{s}_{k}", [128, TC]) for k in range(2 * DT)]
              for s in range(2)]
        pb = [ps(nc, st, f"cb_p{j}", [128, TC]) for j in range(2)]
        res = [sb(nc, st, f"cb_r{j}", [128, TC]) for j in range(DT)]
        out = [sb(nc, st, f"cb_o{j}", [128, TC]) for j in range(DT)]
        cbb = sb(nc, st, "cb_bias", [128, DT])
        lns = LNScratch(plan, nc, st, "cb")
        plan.dma("sp", cbb, io["combb"], reads=[DB["combb"]])
        for k in range(2 * DT):
            plan.dma("sp", wb[k], io["combT"][k * 128:(k + 1) * 128, 0:D],
                     reads=[DB["combT"]])
        for c in range(NCH):
            col = seq * L + c * TC
            ms = c % 2
            for k in range(2 * DT):
                src = DB["o_f_s"] if k < DT else DB["o_b_s"]
                row = (k if k < DT else k - DT) * 128
                plan.dma("sp", mb[ms][k], src.ap[row:row + 128, col:col + TC],
                         reads=[src])
            for dt in range(DT):
                plan.dma("sp", res[dt],
                         io["x0"][dt * 128:(dt + 1) * 128, col:col + TC],
                         reads=[DB["x0"]])
            for m in range(DT):
                gi = c * DT + m
                bank = gi % 2
                mm_group(plan, pb[bank],
                         [(wb[k], wb[k].ap[:, m * 128:(m + 1) * 128],
                           mb[ms][k], mb[ms][k].ap) for k in range(2 * DT)])
                plan.op("dve", lambda h, m=m, bank=bank: h.scalar_tensor_tensor(
                    out[m].ap, res[m].ap, 2.0, pb[bank].ap, ALU.mult, ALU.add),
                    reads=[pb[bank], res[m]], writes=[out[m]])
                plan.op("dve", lambda h, m=m: h.tensor_scalar(
                    out[m].ap, out[m].ap, cbb.ap[:, m:m + 1], None, ALU.add),
                    reads=[cbb], rw=[out[m]])
            emit_postln(plan, P, lns, out,
                        lambda dt, col=col: io["x1_s"][dt * 128:(dt + 1) * 128,
                                                       col:col + TC],
                        DB["x1_s"])


def emit_ffn0(plan, nc, io, DB, P, seq):
    with contextlib.ExitStack() as st:
        wq = [sb(nc, st, f"f0_wq{k}", [128, D]) for k in range(DT)]
        ab = [sb(nc, st, f"f0_a{k}", [128, 16]) for k in range(DT)]
        bb = sb(nc, st, "f0_b", [16, D])
        bias = sb(nc, st, "f0_bias", [128, DT])
        x1 = [sb(nc, st, f"f0_x{k}", [128, TC]) for k in range(DT)]
        pu = ps(nc, st, "f0_pu", [16, TC])
        ub = sb(nc, st, "f0_u", [16, TC])
        pb = [ps(nc, st, f"f0_p{j}", [128, TC]) for j in range(2)]
        out = [sb(nc, st, f"f0_o{j}", [128, TC]) for j in range(DT)]
        lns = LNScratch(plan, nc, st, "f0")
        for k in range(DT):
            plan.dma("sp", wq[k], DB["wq0_s"].ap[k * 128:(k + 1) * 128, :],
                     reads=[DB["wq0_s"]])
            plan.dma("sp", ab[k], io["ffn_a"][k * 128:(k + 1) * 128, :],
                     reads=[DB["ffn_a"]])
        plan.dma("sp", bb, io["ffn_b"], reads=[DB["ffn_b"]])
        plan.dma("sp", bias, io["ffn_bias"], reads=[DB["ffn_bias"]])
        for c in range(NCH):
            col = seq * L + c * TC
            for k in range(DT):
                plan.dma("sp", x1[k],
                         DB["x1_s"].ap[k * 128:(k + 1) * 128, col:col + TC],
                         reads=[DB["x1_s"]])
            mm_group(plan, pu, [(ab[k], ab[k].ap, x1[k], x1[k].ap)
                                for k in range(DT)])
            plan.op("act", lambda h: h.activation(ub.ap, pu.ap, AF.Copy),
                    reads=[pu], writes=[ub])
            for m in range(DT):
                bank = m % 2
                pairs = [(wq[k], wq[k].ap[:, m * 128:(m + 1) * 128], x1[k],
                          x1[k].ap) for k in range(DT)]
                pairs.append((bb, bb.ap[:, m * 128:(m + 1) * 128], ub, ub.ap))
                mm_group(plan, pb[bank], pairs)
                plan.op("dve", lambda h, m=m, bank=bank: h.scalar_tensor_tensor(
                    out[m].ap, pb[bank].ap, bias.ap[:, m:m + 1], x1[m].ap,
                    ALU.add, ALU.add),
                    reads=[pb[bank], bias, x1[m]], writes=[out[m]])
            emit_postln(plan, P, lns, out,
                        lambda dt, col=col: io["x2_s"][dt * 128:(dt + 1) * 128,
                                                       col:col + TC],
                        DB["x2_s"])


def emit_inproj1(plan, nc, io, DB, P, seq):
    with contextlib.ExitStack() as st:
        x2 = [sb(nc, st, f"i1_x{k}", [128, L]) for k in range(DT)]
        wb = [[sb(nc, st, f"i1_w{s}_{k}", [128, 128]) for k in range(DT)]
              for s in range(2)]
        pb = [ps(nc, st, f"i1_p{j}", [128, TC]) for j in range(2)]
        eb = [sb(nc, st, f"i1_e{j}", [128, TC]) for j in range(3)]
        for k in range(DT):
            plan.dma("sp", x2[k], DB["x2_s"].ap[k * 128:(k + 1) * 128,
                                                seq * L:(seq + 1) * L],
                     reads=[DB["x2_s"]])
        mdefs = [(m * 128, 128) for m in range(33)] + [(4224, 64)]
        gi = 0
        for mi, (moff, mw) in enumerate(mdefs):
            ws = mi % 2
            for k in range(DT):
                plan.dma("sp", wb[ws][k].ap[:, 0:mw],
                         io["in1T"][k * 128:(k + 1) * 128, moff:moff + mw],
                         reads=[DB["in1T"]], writes=[wb[ws][k]])
            for c in range(NCH):
                col = seq * L + c * TC
                bank = gi % 2
                mm_group(plan, pb[bank],
                         [(wb[ws][k], wb[ws][k].ap[:, 0:mw], x2[k],
                           x2[k].ap[:, c * TC:(c + 1) * TC])
                          for k in range(DT)], mw=mw)
                e = eb[gi % 3]
                plan.op("act", lambda h, e=e, bank=bank, mw=mw: h.activation(
                    e.ap[0:mw, :], pb[bank].ap[0:mw, :], AF.Copy),
                    reads=[pb[bank]], writes=[e])
                if moff < DI:
                    dst, row = DB["xi1_s"], moff
                elif moff < 2 * DI:
                    dst, row = DB["z1_s"], moff - DI
                elif moff == 4096:
                    dst, row = DB["bc1_s"], 0
                else:
                    dst, row = DB["dtr1_s"], 0
                plan.dma("pool", dst.ap[row:row + mw, col:col + TC],
                         e.ap[0:mw, :], reads=[e], writes=[dst])
                gi += 1


def emit_outproj1_resln(plan, nc, io, DB, P, seq):
    with contextlib.ExitStack() as st:
        wb = [sb(nc, st, f"o1_w{k}", [128, D]) for k in range(DIT)]
        mb = [[sb(nc, st, f"o1_m{s}_{k}", [128, TC]) for k in range(DIT)]
              for s in range(2)]
        pb = [ps(nc, st, f"o1_p{j}", [128, TC]) for j in range(2)]
        out = [sb(nc, st, f"o1_o{j}", [128, TC]) for j in range(DT)]
        res = [sb(nc, st, f"o1_r{j}", [128, TC]) for j in range(DT)]
        lns = LNScratch(plan, nc, st, "o1")
        for k in range(DIT):
            plan.dma("sp", wb[k], io["out1T"][k * 128:(k + 1) * 128, 0:D],
                     reads=[DB["out1T"]])
        for c in range(NCH):
            col = seq * L + c * TC
            ms = c % 2
            for k in range(DIT):
                plan.dma("sp", mb[ms][k],
                         DB["y1_s"].ap[k * 128:(k + 1) * 128, col:col + TC],
                         reads=[DB["y1_s"]])
            for m in range(DT):
                gi = c * DT + m
                bank = gi % 2
                mm_group(plan, pb[bank],
                         [(wb[k], wb[k].ap[:, m * 128:(m + 1) * 128],
                           mb[ms][k], mb[ms][k].ap) for k in range(DIT)])
                plan.op("act", lambda h, m=m, bank=bank: h.activation(
                    out[m].ap, pb[bank].ap, AF.Copy),
                    reads=[pb[bank]], writes=[out[m]])
            for dt in range(DT):
                plan.dma("sp", res[dt],
                         DB["x2_s"].ap[dt * 128:(dt + 1) * 128, col:col + TC],
                         reads=[DB["x2_s"]])
                plan.op("dve", lambda h, dt=dt: h.tensor_tensor(
                    out[dt].ap, out[dt].ap, res[dt].ap, ALU.add),
                    reads=[res[dt]], rw=[out[dt]])
            emit_postln(plan, P, lns, out,
                        lambda dt, col=col: io["x3_s"][dt * 128:(dt + 1) * 128,
                                                       col:col + TC],
                        DB["x3_s"], maxacc=P["gmx"])


def emit_quantprep(plan, nc, io, DB, P, macc, rcp, sw, bvec, seff, beff, NFT,
                   dbgdb=None):
    with contextlib.ExitStack() as st:
        pt = ps(nc, st, "qp_pt", [1, 128])
        pbb = ps(nc, st, "qp_pb", [128, 1])
        srow = sb(nc, st, "qp_srow", [1, 1])
        sbc = sb(nc, st, "qp_sbc", [128, 1])
        sgn = sb(nc, st, "qp_sg", [128, NFT])
        tii = sb(nc, st, "qp_ti", [128, NFT], I32)
        plan.op("pe", lambda h: h.transpose(pt.ap, macc.ap, P["ident"].ap),
                reads=[macc, P["ident"]], writes=[pt])
        plan.op("dve", lambda h: h.reduce_max(srow.ap, pt.ap, axis=AXX),
                reads=[pt], writes=[srow])
        plan.dma("pool", io["cc_in"], srow, reads=[srow], writes=[DB["cc_in"]])
        plan.op("pool", lambda h: h.collective_compute(
            "AllReduce", ALU.max,
            replica_groups=[[0, 1, 2, 3, 4, 5, 6, 7]],
            ins=[io["cc_in"]], outs=[io["cc_out"]]),
            reads=[DB["cc_in"]], writes=[DB["cc_out"]])
        plan.dma("pool", srow, io["cc_out"], reads=[DB["cc_out"]],
                 writes=[srow])
        plan.op("dve", lambda h: h.tensor_scalar(srow.ap, srow.ap, 1e-8,
                                                 1.0 / 255.0, ALU.max,
                                                 ALU.mult), rw=[srow])
        plan.op("pe", lambda h: h.matmul(pbb.ap, P["onesc"].ap, srow.ap,
                                         start=True, stop=True),
                reads=[srow, P["onesc"]], writes=[pbb])
        plan.op("dve", lambda h: h.tensor_copy(sbc.ap, pbb.ap),
                reads=[pbb], writes=[sbc])
        plan.op("dve", lambda h: h.reciprocal(rcp.ap, sbc.ap),
                reads=[sbc], writes=[rcp])
        plan.op("dve", lambda h: h.tensor_scalar(seff.ap, sw.ap, sbc.ap[:, 0:1],
                                                 None, ALU.mult),
                reads=[sw, sbc], writes=[seff])
        rcq = sb(nc, st, "qp_rcq", [128, NFT])
        plan.op("dve", lambda h: h.reciprocal(rcq.ap, seff.ap),
                reads=[seff], writes=[rcq])
        plan.op("dve", lambda h: h.tensor_tensor(beff.ap, bvec.ap, rcq.ap,
                                                 ALU.mult),
                reads=[bvec, rcq], writes=[beff])
        sgn2 = sb(nc, st, "qp_sg2", [128, NFT])
        round_nearest(plan, beff, tii, sgn, sgn2)
        plan.op("dve", lambda h: h.tensor_tensor(beff.ap, beff.ap, seff.ap,
                                                 ALU.mult),
                reads=[seff], rw=[beff])
        if dbgdb is not None:
            dsin, dmax, dse, dbe = dbgdb
            plan.dma("pool", dsin.ap, sbc, reads=[sbc], writes=[dsin])
            plan.dma("pool", dmax.ap, macc, reads=[macc], writes=[dmax])
            plan.dma("pool", dse.ap, seff, reads=[seff], writes=[dse])
            plan.dma("pool", dbe.ap, beff, reads=[beff], writes=[dbe])


def quantize_tiles(plan, src_bufs, dst_bufs, rcp, tmp, tii, t1, t2):
    for sbuf, dbuf in zip(src_bufs, dst_bufs):
        plan.op("dve", lambda h, sbuf=sbuf: h.tensor_scalar(
            tmp.ap, sbuf.ap, rcp.ap[:, 0:1], None, ALU.mult),
            reads=[sbuf, rcp], writes=[tmp])
        plan.op("dve", lambda h: h.tensor_scalar(tmp.ap, tmp.ap, 0.0, 255.0,
                                                 ALU.max, ALU.min), rw=[tmp])
        round_nearest(plan, tmp, tii, t1, t2)
        plan.op("dve", lambda h, dbuf=dbuf: h.tensor_copy(dbuf.ap, tmp.ap),
                reads=[tmp], writes=[dbuf])


def emit_ffn1a(plan, nc, io, DB, P, seq):
    with contextlib.ExitStack() as st:
        wq = [sb(nc, st, f"fa_w{k}", [128, F1], BF16) for k in range(DT)]
        x3 = [sb(nc, st, f"fa_x{k}", [128, TC]) for k in range(DT)]
        xq = [sb(nc, st, f"fa_q{k}", [128, TC], BF16) for k in range(DT)]
        tmp = sb(nc, st, "fa_tmp", [128, TC])
        tq1 = sb(nc, st, "fa_tq1", [128, TC])
        tq2 = sb(nc, st, "fa_tq2", [128, TC])
        tii = sb(nc, st, "fa_ti", [128, TC], I32)
        pb = [ps(nc, st, f"fa_p{j}", [128, TC]) for j in range(2)]
        eb = [sb(nc, st, f"fa_e{j}", [128, TC]) for j in range(3)]
        mxr = sb(nc, st, "fa_mx", [128, 1])
        for k in range(DT):
            plan.dma("sp", wq[k], DB["wq1T_s"].ap[k * 128:(k + 1) * 128, :],
                     reads=[DB["wq1T_s"]])
        for c in range(NCH):
            col = seq * L + c * TC
            for k in range(DT):
                plan.dma("sp", x3[k],
                         DB["x3_s"].ap[k * 128:(k + 1) * 128, col:col + TC],
                         reads=[DB["x3_s"]])
            quantize_tiles(plan, x3, xq, P["rcp1"], tmp, tii, tq1, tq2)
            for m in range(F1T):
                bank = m % 2
                mm_group(plan, pb[bank],
                         [(wq[k], wq[k].ap[:, m * 128:(m + 1) * 128], xq[k],
                           xq[k].ap) for k in range(DT)])
                e = eb[m % 3]
                plan.op("act", lambda h, e=e, bank=bank, m=m: h.activation(
                    e.ap, pb[bank].ap, AF.Relu,
                    scale=P["seff1"].ap[:, m:m + 1],
                    bias=P["beff1"].ap[:, m:m + 1]),
                    reads=[pb[bank], P["seff1"], P["beff1"]], writes=[e])
                plan.op("dve", lambda h, e=e: h.reduce_max(mxr.ap, e.ap,
                                                           axis=AXX),
                        reads=[e], writes=[mxr])
                plan.op("dve", lambda h: h.tensor_max(P["ghx"].ap, P["ghx"].ap,
                                                      mxr.ap),
                        reads=[mxr], rw=[P["ghx"]])
                plan.dma("pool", DB["h_s"].ap[m * 128:(m + 1) * 128,
                                              col:col + TC], e,
                         reads=[e], writes=[DB["h_s"]])


def emit_ffn1b(plan, nc, io, DB, P, seq):
    with contextlib.ExitStack() as st:
        wq = [sb(nc, st, f"fb_w{k}", [128, D], BF16) for k in range(F1T)]
        hb = [sb(nc, st, f"fb_h{j}", [128, TC]) for j in range(2)]
        hq = [sb(nc, st, f"fb_q{k}", [128, TC], BF16) for k in range(F1T)]
        tmp = sb(nc, st, "fb_tmp", [128, TC])
        tq1 = sb(nc, st, "fb_tq1", [128, TC])
        tq2 = sb(nc, st, "fb_tq2", [128, TC])
        tii = sb(nc, st, "fb_ti", [128, TC], I32)
        pb = [ps(nc, st, f"fb_p{j}", [128, TC]) for j in range(2)]
        out = [sb(nc, st, f"fb_o{j}", [128, TC]) for j in range(DT)]
        res = [sb(nc, st, f"fb_r{j}", [128, TC]) for j in range(DT)]
        lns = LNScratch(plan, nc, st, "fb")
        for k in range(F1T):
            plan.dma("sp", wq[k], DB["wq2T_s"].ap[k * 128:(k + 1) * 128, :],
                     reads=[DB["wq2T_s"]])
        for c in range(NCH):
            col = seq * L + c * TC
            for k in range(F1T):
                b = hb[k % 2]
                plan.dma("sp", b,
                         DB["h_s"].ap[k * 128:(k + 1) * 128, col:col + TC],
                         reads=[DB["h_s"]])
                quantize_tiles(plan, [b], [hq[k]], P["rcp2"], tmp, tii, tq1, tq2)
            for m in range(DT):
                bank = m % 2
                mm_group(plan, pb[bank],
                         [(wq[k], wq[k].ap[:, m * 128:(m + 1) * 128], hq[k],
                           hq[k].ap) for k in range(F1T)])
                plan.op("dve", lambda h, m=m, bank=bank: h.tensor_scalar(
                    out[m].ap, pb[bank].ap, P["seff2"].ap[:, m:m + 1],
                    P["beff2"].ap[:, m:m + 1], ALU.mult, ALU.add),
                    reads=[pb[bank], P["seff2"], P["beff2"]], writes=[out[m]])
                plan.dma("sp", res[m],
                         DB["x3_s"].ap[m * 128:(m + 1) * 128, col:col + TC],
                         reads=[DB["x3_s"]])
                plan.op("dve", lambda h, m=m: h.tensor_tensor(
                    out[m].ap, out[m].ap, res[m].ap, ALU.add),
                    reads=[res[m]], rw=[out[m]])
            emit_postln(plan, P, lns, out,
                        lambda dt, col=col: io["xout"][dt * 128:(dt + 1) * 128,
                                                       col:col + TC],
                        DB["xout"])



def make_inputs(core, A):
    t = lambda x: np.ascontiguousarray(np.asarray(x, np.float32).T)
    col = lambda x: np.ascontiguousarray(np.asarray(x, np.float32).reshape(-1, 1))
    colk = lambda x, k: np.ascontiguousarray(
        np.asarray(x, np.float32).reshape(k, 128).T)
    m = {}
    x = np.asarray(A["L0_attn_x"][2 * core:2 * core + 2], np.float32)
    m["x0"] = np.ascontiguousarray(np.transpose(x, (2, 0, 1)).reshape(D, NT))
    m["ln_g"] = np.ascontiguousarray(np.asarray(A["L0_attn_ln_g"], np.float32).reshape(1, D))
    m["ln_b"] = np.ascontiguousarray(np.asarray(A["L0_attn_ln_b"], np.float32).reshape(1, D))
    for d in ("f", "b"):
        m[f"{d}_inT"] = t(A[f"L0_attn_{d}_in_w"])
        m[f"{d}_convw"] = np.ascontiguousarray(np.asarray(
            A[f"L0_attn_{d}_conv_w"], np.float32).reshape(DIT, 128, 4)
            .transpose(1, 0, 2).reshape(128, DIT * 4))
        m[f"{d}_convb"] = colk(A[f"L0_attn_{d}_conv_b"], DIT)
        m[f"{d}_xwT"] = t(A[f"L0_attn_{d}_x_w"])
        m[f"{d}_dtwT"] = t(A[f"L0_attn_{d}_dt_w"])
        m[f"{d}_dtb"] = colk(A[f"L0_attn_{d}_dt_b"], DIT)
        m[f"{d}_alog"] = np.ascontiguousarray(np.asarray(
            A[f"L0_attn_{d}_A_log"], np.float32))
        m[f"{d}_dp"] = colk(A[f"L0_attn_{d}_D"], DIT)
        m[f"{d}_outT"] = t(A[f"L0_attn_{d}_out_w"])
    m["combT"] = t(A["L0_attn_combine_w"])
    m["combb"] = colk(A["L0_attn_combine_b"], DT)
    m["ffn_k"] = np.ascontiguousarray(np.asarray(A["L0_ffn_kernel"], np.float32))
    m["ffn_bias"] = colk(A["L0_ffn_bias"], DT)
    m["ffn_a"] = np.ascontiguousarray(np.asarray(A["L0_ffn_a"], np.float32))
    m["ffn_b"] = np.ascontiguousarray(np.asarray(A["L0_ffn_b"], np.float32))
    m["in1T"] = t(A["L1_attn_in_proj_w"])
    m["conv1w"] = np.ascontiguousarray(np.asarray(
        A["L1_attn_conv_w"], np.float32).reshape(DIT, 128, 4)
        .transpose(1, 0, 2).reshape(128, DIT * 4))
    m["conv1b"] = colk(A["L1_attn_conv_b"], DIT)
    m["alog1b"] = np.ascontiguousarray(np.broadcast_to(
        np.asarray(A["L1_attn_A_log"], np.float32)[None, :], (128, N1)).copy())
    m["dp1"] = colk(A["L1_attn_D_param"], DIT)
    m["dt1T"] = t(A["L1_attn_dt_proj_w"])
    m["dt1b"] = colk(A["L1_attn_dt_proj_b"], DIT)
    m["out1T"] = t(A["L1_attn_out_proj_w"])
    m["w1f"] = np.ascontiguousarray(np.asarray(A["L1_ffn_w1"], np.float32))
    m["w1T"] = t(A["L1_ffn_w1"])
    m["b1"] = np.ascontiguousarray(
        np.asarray(A["L1_ffn_b1"], np.float32).reshape(F1T, 128).T)
    m["w2f"] = np.ascontiguousarray(np.asarray(A["L1_ffn_w2"], np.float32))
    m["w2T"] = t(A["L1_ffn_w2"])
    m["b2"] = np.ascontiguousarray(
        np.asarray(A["L1_ffn_b2"], np.float32).reshape(DT, 128).T)
    m["ident"] = np.eye(128, dtype=np.float32)
    return m


def kernel(**inputs):
    nc = build_program()
    in_maps = [make_inputs(c, inputs) for c in range(NCORE)]
    res = run_bass_kernel_spmd(nc, in_maps, core_ids=list(range(NCORE)))
    outs = []
    for c in range(NCORE):
        o = np.asarray(res.results[c]["xout"])
        outs.append(np.transpose(o.reshape(D, BSEQ, L), (1, 2, 0)))
    return np.concatenate(outs, axis=0).astype(np.float32)


# version 14
# speedup vs baseline: 1.2348x; 1.0896x over previous
import contextlib

import numpy as np

import concourse.bass as bass
import concourse.mybir as mybir
from concourse.bass_utils import run_bass_kernel_spmd

F32 = mybir.dt.float32
BF16 = mybir.dt.bfloat16
I32 = mybir.dt.int32
AF = mybir.ActivationFunctionType
ALU = mybir.AluOpType
AXX = mybir.AxisListType.X

NCORE = 8
BSEQ = 2
L = 2048
NT = BSEQ * L
D = 1024
DI = 2048
N0 = 16
N1 = 64
R0 = 64
F1 = 4096
TC = 512
NCH = L // TC
DT = D // 128
DIT = DI // 128
F1T = F1 // 128

ENGS = ("pe", "act", "dve", "pool", "sp")
ENG2SEC = {"pe": "tensor", "act": "scalar", "dve": "vector", "pool": "gpsimd",
           "sp": "sync"}
ESEM = {e: "s_" + e for e in ENGS}


class Buf:
    __slots__ = ("ap", "writers", "readers", "name")

    def __init__(self, ap, name=""):
        self.ap = ap
        self.writers = []
        self.readers = []
        self.name = name


class Op:
    __slots__ = ("eng", "fn", "deps", "seq", "queue", "fence", "is_fence",
                 "qsem")


QENG = {"d_sp": "sp", "d_pool": "pool"}


class Plan:

    def __init__(self, nc):
        self.nc = nc
        self.ops = {e: [] for e in ENGS}
        self.ntok = {e: 0 for e in ENGS}
        self.qgroup = {"d_sp": [], "d_pool": []}
        self.nfence = {"d_sp": 0, "d_pool": 0}
        self.qlastfence = {"d_sp": None, "d_pool": None}
        self.nop_sem = None

    def _resolve(self, d):
        if isinstance(d, tuple) or not getattr(d, "queue", None):
            return d
        if d.fence is None:
            self.fence(d.queue)
        return d.fence

    def op(self, eng, fn, deps=(), reads=(), writes=(), rw=(), queue=None,
           is_fence=False):
        dd = []
        for d in deps:
            if d is not None:
                dd.append(d)
        for b in reads:
            dd.extend(b.writers)
        for b in rw:
            dd.extend(b.writers)
            dd.extend(b.readers)
        for b in writes:
            dd.extend(b.writers)
            dd.extend(b.readers)
        if not is_fence:
            dd = [self._resolve(d) for d in dd]
        o = Op()
        o.eng = eng
        o.fn = fn
        o.deps = dd
        o.queue = queue
        o.fence = None
        o.qsem = None
        o.is_fence = is_fence
        if queue is None and not is_fence:
            self.ntok[eng] += 1
            o.seq = self.ntok[eng]
        else:
            o.seq = None
        self.ops[eng].append(o)
        if queue is not None:
            self.qgroup[queue].append(o)
        for b in reads:
            b.readers.append(o)
        for b in rw:
            b.writers = [o]
            b.readers = []
        for b in writes:
            b.writers = [o]
            b.readers = []
        return o

    def fence(self, queue):
        grp = self.qgroup[queue]
        if not grp:
            return self.qlastfence[queue]
        self.qgroup[queue] = []
        rot = self.nfence[queue] % 4
        qsem = f"{queue}{rot}"
        for d in grp:
            d.qsem = qsem
        f = self.op(QENG[queue], ("FENCE", qsem), is_fence=True)
        self.nfence[queue] += 1
        f.seq = ("f_" + queue[2:], self.nfence[queue])
        for d in grp:
            d.fence = f
        self.qlastfence[queue] = f
        return f

    def dma(self, eng, dst, src, reads=(), writes=(), rw=(), deps=(), **kw):
        q = "d_sp" if eng == "sp" else "d_pool"
        dap = dst.ap if isinstance(dst, Buf) else dst
        sap = src.ap if isinstance(src, Buf) else src
        rd = list(reads) + ([src] if isinstance(src, Buf) else [])
        wr = list(writes) + ([dst] if isinstance(dst, Buf) and not rw else [])
        return self.op(eng, lambda h: h.dma_start(out=dap, in_=sap, **kw),
                       deps=deps, reads=rd, writes=wr, rw=rw, queue=q)

    def barrier(self):
        self.fence("d_sp")
        self.fence("d_pool")
        tails = [ops[-1] for ops in self.ops.values() if ops]
        bar = []
        for e in ENGS:
            bar.append(self.op(e, None, deps=tails))
        tails2 = list(bar)
        for e in ENGS:
            self.op(e, None, deps=tails2)

    def token(self, o):
        if isinstance(o, tuple):
            return o
        assert o.seq is not None, "dma token must resolve through its fence"
        if isinstance(o.seq, tuple):
            return o.seq
        return (ESEM[o.eng], o.seq)

    def emit(self, ctx):
        qtotal = {}
        for q in QENG:
            for r in range(4):
                qtotal[f"{q}{r}"] = [0]
        for eng in ENGS:
            ops = self.ops[eng]
            if not ops:
                continue

            def body(h, eng=eng, ops=ops):
                hw = ctx.waited[eng]
                own = ESEM[eng]
                for o in ops:
                    need = {}
                    for d in o.deps:
                        s, v = self.token(d)
                        if need.get(s, 0) < v:
                            need[s] = v
                    for s in sorted(need):
                        v = need[s]
                        if hw.get(s, 0) >= v:
                            continue
                        h.wait_ge(ctx.sem[s], v)
                        hw[s] = v
                    if isinstance(o.fn, tuple):
                        q = o.fn[1]
                        h.wait_ge(ctx.sem[q], qtotal[q][0])
                        h.sem_inc(ctx.sem[o.seq[0]], 1)
                    elif o.fn is None:
                        h.sem_inc(ctx.sem[own], 1)
                    else:
                        inst = o.fn(h)
                        if o.queue is not None:
                            inst.then_inc(ctx.sem[o.qsem], 16)
                            qtotal[o.qsem][0] += 16
                        else:
                            inst.then_inc(ctx.sem[own], 1)
            getattr(ctx.block, ENG2SEC[eng])(body)


class Ctx:
    def __init__(self, nc, block):
        self.nc = nc
        self.block = block
        self.sem = {}
        names = [ESEM[e] for e in ENGS] + ["f_sp", "f_pool"]
        for q in ("d_sp", "d_pool"):
            names += [f"{q}{r}" for r in range(4)]
        for name in names:
            self.sem[name] = nc.semaphore(name).__enter__()
        self.waited = {e: {} for e in ENGS}



def build_program(debug=(), upto=10 ** 9):
    nc = bass.Bass("TRN2", target_bir_lowering=False)
    dbg = set(debug)

    def din(name, shape, dtype=F32):
        return nc.dram_tensor(name, shape, dtype, kind="ExternalInput").ap()

    def dscr(name, shape, dtype=F32):
        kind = "ExternalOutput" if name in dbg else "Internal"
        return nc.dram_tensor(name, shape, dtype, kind=kind).ap()

    io = {}
    io["x0"] = din("x0", [D, NT])
    io["ln_g"] = din("ln_g", [1, D])
    io["ln_b"] = din("ln_b", [1, D])
    for d in ("f", "b"):
        io[f"{d}_inT"] = din(f"{d}_inT", [D, 2 * DI])
        io[f"{d}_convw"] = din(f"{d}_convw", [128, DIT * 4])
        io[f"{d}_convb"] = din(f"{d}_convb", [128, DIT])
        io[f"{d}_xwT"] = din(f"{d}_xwT", [DI, R0 + 2 * N0])
        io[f"{d}_dtwT"] = din(f"{d}_dtwT", [R0, DI])
        io[f"{d}_dtb"] = din(f"{d}_dtb", [128, DIT])
        io[f"{d}_alog"] = din(f"{d}_alog", [DI, N0])
        io[f"{d}_dp"] = din(f"{d}_dp", [128, DIT])
        io[f"{d}_outT"] = din(f"{d}_outT", [DI, D])
    io["combT"] = din("combT", [2 * D, D])
    io["combb"] = din("combb", [128, DT])
    io["ffn_k"] = din("ffn_k", [D, D])
    io["ffn_bias"] = din("ffn_bias", [128, DT])
    io["ffn_a"] = din("ffn_a", [D, 16])
    io["ffn_b"] = din("ffn_b", [16, D])
    io["in1T"] = din("in1T", [D, 4288])
    io["conv1w"] = din("conv1w", [128, DIT * 4])
    io["conv1b"] = din("conv1b", [128, DIT])
    io["alog1b"] = din("alog1b", [128, N1])
    io["dp1"] = din("dp1", [128, DIT])
    io["dt1T"] = din("dt1T", [R0, DI])
    io["dt1b"] = din("dt1b", [128, DIT])
    io["out1T"] = din("out1T", [DI, D])
    io["w1f"] = din("w1f", [F1, D])
    io["w1T"] = din("w1T", [D, F1])
    io["b1"] = din("b1", [128, F1T])
    io["w2f"] = din("w2f", [D, F1])
    io["w2T"] = din("w2T", [F1, D])
    io["b2"] = din("b2", [128, DT])
    io["ident"] = din("ident", [128, 128])

    io["xout"] = nc.dram_tensor("xout", [D, NT], F32, kind="ExternalOutput").ap()

    io["xn_s"] = dscr("xn_s", [D, NT])
    for d in ("f", "b"):
        io[f"xi_{d}_s"] = dscr(f"xi_{d}_s", [DI, NT])
        io[f"z_{d}_s"] = dscr(f"z_{d}_s", [DI, NT])
        io[f"xc_{d}_s"] = dscr(f"xc_{d}_s", [DI, NT])
        io[f"dl_{d}_s"] = dscr(f"dl_{d}_s", [DI, NT])
        io[f"y_{d}_s"] = dscr(f"y_{d}_s", [DI, NT])
        io[f"bc_{d}_s"] = dscr(f"bc_{d}_s", [2 * N0, NT])
        io[f"dtr_{d}_s"] = dscr(f"dtr_{d}_s", [R0, NT])
        io[f"o_{d}_s"] = dscr(f"o_{d}_s", [D, NT])
    io["x1_s"] = dscr("x1_s", [D, NT])
    io["x2_s"] = dscr("x2_s", [D, NT])
    io["x3_s"] = dscr("x3_s", [D, NT])
    io["wq0_s"] = dscr("wq0_s", [D, D])
    io["xi1_s"] = dscr("xi1_s", [DI, NT])
    io["z1_s"] = dscr("z1_s", [DI, NT])
    io["xc1_s"] = dscr("xc1_s", [DI, NT])
    io["dl1_s"] = dscr("dl1_s", [DI, NT])
    io["y1_s"] = dscr("y1_s", [DI, NT])
    io["bc1_s"] = dscr("bc1_s", [2 * N1, NT])
    io["dtr1_s"] = dscr("dtr1_s", [R0, NT])
    io["wq1T_s"] = dscr("wq1T_s", [D, F1], BF16)
    io["wq2T_s"] = dscr("wq2T_s", [F1, D], BF16)
    io["s1_s"] = dscr("s1_s", [F1, 1])
    io["s2_s"] = dscr("s2_s", [D, 1])
    io["h_s"] = dscr("h_s", [F1, NT])
    for tag, n in (("1", F1T), ("2", DT)):
        io[f"qd{tag}_sin"] = dscr(f"qd{tag}_sin", [128, 1])
        io[f"qd{tag}_max"] = dscr(f"qd{tag}_max", [128, 1])
        io[f"qd{tag}_se"] = dscr(f"qd{tag}_se", [128, n])
        io[f"qd{tag}_be"] = dscr(f"qd{tag}_be", [128, n])
    io["cc_in"] = nc.dram_tensor("cc_in", [1, 1], F32, kind="Internal").ap()
    io["cc_out"] = nc.dram_tensor("cc_out", [1, 1], F32, kind="Internal").ap()

    DB = {k: Buf(v, k) for k, v in io.items()}

    P = {}

    def psb(name, shape, dtype=F32):
        P[name] = Buf(nc.alloc_sbuf_tensor("p_" + name, shape, dtype).ap(), name)
        return P[name]

    psb("onesc", [1, 128])
    psb("onesr", [1, TC])
    psb("ones1", [128, 1])
    psb("ident", [128, 128])
    psb("A0f", [128, DIT * N0])
    psb("A0b", [128, DIT * N0])
    psb("A1", [128, N1])
    psb("sw1", [128, F1T])
    psb("sw2", [128, DT])
    psb("b1", [128, F1T])
    psb("b2", [128, DT])
    psb("rcp1", [128, 1])
    psb("seff1", [128, F1T])
    psb("beff1", [128, F1T])
    psb("rcp2", [128, 1])
    psb("seff2", [128, DT])
    psb("beff2", [128, DT])
    psb("gmx", [128, 1])
    psb("ghx", [128, 1])

    with nc.Block() as block:
        ctx = Ctx(nc, block)
        plan = Plan(nc)
        plan.nop_sem = nc.semaphore("nop").__enter__()
        emit_all(plan, nc, io, DB, P, upto=upto)
        plan.emit(ctx)
    return nc


_UNIQ = [0]


def sb(nc, st, name, shape, dtype=F32):
    _UNIQ[0] += 1
    return Buf(st.enter_context(
        nc.sbuf_tensor(f"{name}_{_UNIQ[0]}", shape, dtype)).ap(), name)


def ps(nc, st, name, shape, dtype=F32):
    _UNIQ[0] += 1
    return Buf(st.enter_context(
        nc.psum_tensor(f"{name}_{_UNIQ[0]}", shape, dtype)).ap(), name)



def emit_all(plan, nc, io, DB, P, upto=10 ** 9):
    stage_i = [0]

    def gate():
        stage_i[0] += 1
        return stage_i[0] <= upto

    if not gate():
        return
    emit_prep(plan, nc, io, DB, P)
    plan.barrier()
    if not gate():
        return
    emit_prep_wq0(plan, nc, io, DB, P)
    plan.barrier()
    if not gate():
        return
    emit_prep_quantw(plan, nc, io, DB, P, "w1f", "w1T", F1, D, P["sw1"],
                     DB["wq1T_s"], DB["s1_s"], F1T)
    plan.barrier()
    if not gate():
        return
    emit_prep_quantw(plan, nc, io, DB, P, "w2f", "w2T", D, F1, P["sw2"],
                     DB["wq2T_s"], DB["s2_s"], DT)
    plan.barrier()
    for seq in range(BSEQ):
        if not gate():
            return
        emit_preln(plan, nc, io, DB, P, seq)
        plan.barrier()
    for seq in range(BSEQ):
        for d in ("f", "b"):
            if not gate():
                return
            emit_inproj0(plan, nc, io, DB, P, seq, d)
            plan.barrier()
            if not gate():
                return
            emit_conv(plan, nc, io, DB, seq, DB[f"xi_{d}_s"], DB[f"xc_{d}_s"],
                      io[f"{d}_convw"], io[f"{d}_convb"], DB[f"{d}_convw"],
                      DB[f"{d}_convb"], fwd=(d == "f"))
            plan.barrier()
            if not gate():
                return
            emit_xw0(plan, nc, io, DB, P, seq, d)
            plan.barrier()
            if not gate():
                return
            emit_dtproj(plan, nc, io, DB, P, seq,
                        io[f"{d}_dtwT"], DB[f"{d}_dtwT"],
                        io[f"{d}_dtb"], DB[f"{d}_dtb"],
                        DB[f"dtr_{d}_s"], DB[f"dl_{d}_s"])
            plan.barrier()
            if not gate():
                return
            emit_scan0(plan, nc, io, DB, P, seq, d)
            plan.barrier()
            if not gate():
                return
            emit_gating(plan, nc, io, DB, seq, DB[f"y_{d}_s"], DB[f"z_{d}_s"])
            plan.barrier()
            if not gate():
                return
            emit_proj_simple(plan, nc, io, DB, P, seq, io[f"{d}_outT"],
                             DB[f"{d}_outT"], DB[f"y_{d}_s"], DIT, DT,
                             DB[f"o_{d}_s"], name=f"op0{seq}{d}")
            plan.barrier()
    for seq in range(BSEQ):
        if not gate():
            return
        emit_combine(plan, nc, io, DB, P, seq)
        plan.barrier()
    for seq in range(BSEQ):
        if not gate():
            return
        emit_ffn0(plan, nc, io, DB, P, seq)
        plan.barrier()
    for seq in range(BSEQ):
        if not gate():
            return
        emit_inproj1(plan, nc, io, DB, P, seq)
        plan.barrier()
        if not gate():
            return
        emit_conv(plan, nc, io, DB, seq, DB["xi1_s"], DB["xc1_s"],
                  io["conv1w"], io["conv1b"], DB["conv1w"], DB["conv1b"],
                  fwd=True)
        plan.barrier()
        if not gate():
            return
        emit_dtproj(plan, nc, io, DB, P, seq, io["dt1T"], DB["dt1T"],
                    io["dt1b"], DB["dt1b"], DB["dtr1_s"], DB["dl1_s"])
        plan.barrier()
        if not gate():
            return
        emit_scan1(plan, nc, io, DB, P, seq)
        plan.barrier()
        if not gate():
            return
        emit_gating(plan, nc, io, DB, seq, DB["y1_s"], DB["z1_s"])
        plan.barrier()
        if not gate():
            return
        emit_outproj1_resln(plan, nc, io, DB, P, seq)
        plan.barrier()
    if not gate():
        return
    emit_quantprep(plan, nc, io, DB, P, P["gmx"], P["rcp1"], P["sw1"], P["b1"],
                   P["seff1"], P["beff1"], F1T,
                   (DB["qd1_sin"], DB["qd1_max"], DB["qd1_se"], DB["qd1_be"]))
    plan.barrier()
    for seq in range(BSEQ):
        if not gate():
            return
        emit_ffn1a(plan, nc, io, DB, P, seq)
        plan.barrier()
    if not gate():
        return
    emit_quantprep(plan, nc, io, DB, P, P["ghx"], P["rcp2"], P["sw2"], P["b2"],
                   P["seff2"], P["beff2"], DT,
                   (DB["qd2_sin"], DB["qd2_max"], DB["qd2_se"], DB["qd2_be"]))
    plan.barrier()
    for seq in range(BSEQ):
        if not gate():
            return
        emit_ffn1b(plan, nc, io, DB, P, seq)
        plan.barrier()



def emit_prep(plan, nc, io, DB, P):
    plan.op("dve", lambda h: h.memset(P["onesc"].ap, 1.0), writes=[P["onesc"]])
    plan.op("dve", lambda h: h.memset(P["onesr"].ap, 1.0), writes=[P["onesr"]])
    plan.op("dve", lambda h: h.memset(P["ones1"].ap, 1.0), writes=[P["ones1"]])
    plan.op("dve", lambda h: h.memset(P["gmx"].ap, -1e30), writes=[P["gmx"]])
    plan.op("dve", lambda h: h.memset(P["ghx"].ap, -1e30), writes=[P["ghx"]])
    plan.dma("sp", P["ident"], io["ident"], reads=[DB["ident"]])
    plan.dma("sp", P["b1"], io["b1"], reads=[DB["b1"]])
    plan.dma("sp", P["b2"], io["b2"], reads=[DB["b2"]])
    with contextlib.ExitStack() as st:
        at = sb(nc, st, "prep_at", [128, N0])
        for d in ("f", "b"):
            dst = P["A0f"] if d == "f" else P["A0b"]
            for dt in range(DIT):
                plan.dma("sp", at, io[f"{d}_alog"][dt * 128:(dt + 1) * 128, :],
                         reads=[DB[f"{d}_alog"]])
                plan.op("act",
                        lambda h, dt=dt, dst=dst: h.activation(
                            dst.ap[:, dt * N0:(dt + 1) * N0], at.ap, AF.Exp),
                        reads=[at], rw=[dst])
    plan.dma("sp", P["A1"], io["alog1b"], reads=[DB["alog1b"]])
    plan.op("act", lambda h: h.activation(P["A1"].ap, P["A1"].ap, AF.Exp),
            rw=[P["A1"]])


def round_nearest(plan, buf, ibuf, t1, t2):
    plan.op("dve", lambda h: h.tensor_copy(ibuf.ap, buf.ap),
            reads=[buf], writes=[ibuf])
    plan.op("dve", lambda h: h.tensor_copy(t1.ap, ibuf.ap),
            reads=[ibuf], writes=[t1])
    plan.op("dve", lambda h: h.tensor_tensor(buf.ap, buf.ap, t1.ap,
                                             ALU.subtract),
            reads=[t1], rw=[buf])
    plan.op("dve", lambda h: h.tensor_scalar(t2.ap, buf.ap, 0.5, None,
                                             ALU.is_gt),
            reads=[buf], writes=[t2])
    plan.op("dve", lambda h: h.tensor_scalar(buf.ap, buf.ap, -0.5, None,
                                             ALU.is_lt), rw=[buf])
    plan.op("dve", lambda h: h.tensor_tensor(t1.ap, t1.ap, t2.ap, ALU.add),
            reads=[t2], rw=[t1])
    plan.op("dve", lambda h: h.tensor_tensor(buf.ap, t1.ap, buf.ap,
                                             ALU.subtract),
            reads=[t1], rw=[buf])


def emit_prep_wq0(plan, nc, io, DB, P):
    with contextlib.ExitStack() as st:
        t = [sb(nc, st, f"wq_t{j}", [128, D]) for j in range(2)]
        sg = sb(nc, st, "wq_sg", [128, D])
        ti = sb(nc, st, "wq_ti", [128, D], I32)
        mx = sb(nc, st, "wq_mx", [128, 1])
        mx2 = sb(nc, st, "wq_mx2", [128, 1])
        sc = sb(nc, st, "wq_sc", [128, 1])
        rsc = sb(nc, st, "wq_rsc", [128, 1])
        srow = sb(nc, st, "wq_srow", [1, 1])
        pt = ps(nc, st, "wq_pt", [1, 128])
        pb = ps(nc, st, "wq_pb", [128, 1])
        k = io["ffn_k"]
        for dt in range(DT):
            b = t[dt % 2]
            plan.dma("sp", b, k[dt * 128:(dt + 1) * 128, :], reads=[DB["ffn_k"]])
            if dt == 0:
                plan.op("dve", lambda h: h.reduce_max(
                    mx.ap, t[0].ap, axis=AXX, apply_absolute_value=True),
                    reads=[t[0]], writes=[mx])
            else:
                plan.op("dve", lambda h, b=b: h.reduce_max(
                    mx2.ap, b.ap, axis=AXX, apply_absolute_value=True),
                    reads=[b], writes=[mx2])
                plan.op("dve", lambda h: h.tensor_max(mx.ap, mx.ap, mx2.ap),
                        reads=[mx2], rw=[mx])
        plan.op("pe", lambda h: h.transpose(pt.ap, mx.ap, P["ident"].ap),
                reads=[mx, P["ident"]], writes=[pt])
        plan.op("dve", lambda h: h.reduce_max(srow.ap, pt.ap, axis=AXX),
                reads=[pt], writes=[srow])
        plan.op("dve", lambda h: h.tensor_scalar(srow.ap, srow.ap, 1.0 / 7.0,
                                                 None, ALU.mult), rw=[srow])
        plan.op("pe", lambda h: h.matmul(pb.ap, P["onesc"].ap, srow.ap,
                                         start=True, stop=True),
                reads=[srow, P["onesc"]], writes=[pb])
        plan.op("dve", lambda h: h.tensor_copy(sc.ap, pb.ap),
                reads=[pb], writes=[sc])
        plan.op("dve", lambda h: h.reciprocal(rsc.ap, sc.ap),
                reads=[sc], writes=[rsc])
        sg2 = sb(nc, st, "wq_sg2", [128, D])
        for dt in range(DT):
            b = t[dt % 2]
            plan.dma("sp", b, k[dt * 128:(dt + 1) * 128, :], reads=[DB["ffn_k"]])
            plan.op("dve", lambda h, b=b: h.tensor_scalar(
                b.ap, b.ap, rsc.ap[:, 0:1], None, ALU.mult),
                reads=[rsc], rw=[b])
            round_nearest(plan, b, ti, sg, sg2)
            plan.op("dve", lambda h, b=b: h.tensor_scalar(
                b.ap, b.ap, -8.0, 7.0, ALU.max, ALU.min), rw=[b])
            plan.op("dve", lambda h, b=b: h.tensor_scalar(
                b.ap, b.ap, sc.ap[:, 0:1], None, ALU.mult),
                reads=[sc], rw=[b])
            plan.dma("pool", io["wq0_s"][dt * 128:(dt + 1) * 128, :], b,
                     writes=[DB["wq0_s"]])


def emit_prep_quantw(plan, nc, io, DB, P, wf_name, wT_name, NF, ND, s_sb,
                     outdb, srowdb, NFT):
    with contextlib.ExitStack() as st:
        t = [sb(nc, st, f"q_t{j}{wf_name}", [128, ND]) for j in range(2)]
        mx = sb(nc, st, f"q_mx{wf_name}", [128, 1])
        wf = io[wf_name]
        for m in range(NFT):
            b = t[m % 2]
            plan.dma("sp", b, wf[m * 128:(m + 1) * 128, :], reads=[DB[wf_name]])
            plan.op("dve", lambda h, b=b: h.reduce_max(
                mx.ap, b.ap, axis=AXX, apply_absolute_value=True),
                reads=[b], writes=[mx])
            plan.op("dve", lambda h, m=m: h.tensor_scalar(
                s_sb.ap[:, m:m + 1], mx.ap, 1e-8, 1.0 / 127.0, ALU.max,
                ALU.mult), reads=[mx], rw=[s_sb])
            plan.op("dve", lambda h, m=m: h.reciprocal(
                mx.ap, s_sb.ap[:, m:m + 1]), reads=[s_sb], writes=[mx])
            plan.dma("pool", srowdb.ap[m * 128:(m + 1) * 128, 0:1],
                     mx.ap, reads=[mx], writes=[srowdb])
        NDT = ND // 128
        CH = NF // TC
        w = [sb(nc, st, f"q_w{j}{wf_name}", [128, TC]) for j in range(2)]
        sr = [sb(nc, st, f"q_sr{j}{wf_name}", [128, TC]) for j in range(2)]
        sg = sb(nc, st, f"q_sg{wf_name}", [128, TC])
        sg2 = sb(nc, st, f"q_sg2{wf_name}", [128, TC])
        ti = sb(nc, st, f"q_ti{wf_name}", [128, TC], I32)
        ob = sb(nc, st, f"q_ob{wf_name}", [128, TC], BF16)
        wT = io[wT_name]
        i_s = 0
        for c in range(CH):
            for kd in range(NDT):
                bw, bs = w[i_s % 2], sr[i_s % 2]
                plan.dma("sp", bw,
                         wT[kd * 128:(kd + 1) * 128, c * TC:(c + 1) * TC],
                         reads=[DB[wT_name]])
                plan.dma("sp", bs,
                         srowdb.ap[c * TC:(c + 1) * TC, 0:1]
                         .transpose([1, 0]).broadcast_to([128, TC]),
                         reads=[srowdb], writes=[bs])
                plan.op("dve", lambda h, bw=bw, bs=bs: h.tensor_tensor(
                    bw.ap, bw.ap, bs.ap, ALU.mult), reads=[bs], rw=[bw])
                round_nearest(plan, bw, ti, sg, sg2)
                plan.op("dve", lambda h, bw=bw: h.tensor_scalar(
                    bw.ap, bw.ap, -128.0, 127.0, ALU.max, ALU.min), rw=[bw])
                plan.op("dve", lambda h, bw=bw: h.tensor_copy(ob.ap, bw.ap),
                        reads=[bw], writes=[ob])
                plan.dma("pool",
                         outdb.ap[kd * 128:(kd + 1) * 128, c * TC:(c + 1) * TC],
                         ob, reads=[ob], writes=[outdb])
                i_s += 1



class LNScratch:
    def __init__(self, plan, nc, st, tag):
        self.p_sx = ps(nc, st, f"ln_psx{tag}", [1, TC])
        self.p_sq = ps(nc, st, f"ln_psq{tag}", [1, TC])
        self.p_ar = ps(nc, st, f"ln_par{tag}", [128, TC])
        self.p_br = ps(nc, st, f"ln_pbr{tag}", [128, TC])
        self.sq = sb(nc, st, f"ln_sq{tag}", [128, TC])
        self.rows = [sb(nc, st, f"ln_r{j}{tag}", [1, TC]) for j in range(4)]
        self.mxr = sb(nc, st, f"ln_mx{tag}", [128, 1])


def emit_postln(plan, P, lns, tiles, dst_fn, dstdb, g_ap=None, b_ap=None,
                gdeps=(), maxacc=None):
    r_mean, r_var, r_b, r_rstd = lns.rows
    for dt in range(DT):
        tl = tiles[dt]
        plan.op("pe", lambda h, dt=dt, tl=tl: h.matmul(
            lns.p_sx.ap, P["ones1"].ap, tl.ap, start=(dt == 0),
            stop=(dt == DT - 1)),
            reads=[tl, P["ones1"]], writes=[lns.p_sx] if dt == 0 else (),
            rw=() if dt == 0 else ())
        plan.op("act", lambda h, tl=tl: h.activation(lns.sq.ap, tl.ap,
                                                     AF.Square),
                reads=[tl], writes=[lns.sq])
        plan.op("pe", lambda h, dt=dt: h.matmul(
            lns.p_sq.ap, P["ones1"].ap, lns.sq.ap, start=(dt == 0),
            stop=(dt == DT - 1)),
            reads=[lns.sq, P["ones1"]], writes=[lns.p_sq] if dt == 0 else ())
    last_sx = plan.ops["pe"][-1]
    lns.p_sx.writers = [plan.ops["pe"][-2]]
    lns.p_sq.writers = [last_sx]
    plan.op("dve", lambda h: h.tensor_scalar(r_mean.ap, lns.p_sx.ap, 1.0 / D,
                                             None, ALU.mult),
            reads=[lns.p_sx], writes=[r_mean])
    plan.op("dve", lambda h: h.tensor_scalar(r_var.ap, lns.p_sq.ap, 1.0 / D,
                                             1e-5, ALU.mult, ALU.add),
            reads=[lns.p_sq], writes=[r_var])
    plan.op("dve", lambda h: h.tensor_tensor(r_b.ap, r_mean.ap, r_mean.ap,
                                             ALU.mult),
            reads=[r_mean], writes=[r_b])
    plan.op("dve", lambda h: h.tensor_tensor(r_var.ap, r_var.ap, r_b.ap,
                                             ALU.subtract),
            reads=[r_b], rw=[r_var])
    plan.op("act", lambda h: h.activation(r_rstd.ap, r_var.ap, AF.Sqrt),
            reads=[r_var], writes=[r_rstd])
    plan.op("dve", lambda h: h.reciprocal(r_rstd.ap, r_rstd.ap), rw=[r_rstd])
    for _ in range(2):
        plan.op("dve", lambda h: h.tensor_tensor(r_b.ap, r_rstd.ap, r_rstd.ap,
                                                 ALU.mult),
                reads=[r_rstd], writes=[r_b])
        plan.op("dve", lambda h: h.tensor_tensor(r_b.ap, r_b.ap, r_var.ap,
                                                 ALU.mult),
                reads=[r_var], rw=[r_b])
        plan.op("dve", lambda h: h.tensor_scalar(r_b.ap, r_b.ap, -0.5, 1.5,
                                                 ALU.mult, ALU.add), rw=[r_b])
        plan.op("dve", lambda h: h.tensor_tensor(r_rstd.ap, r_rstd.ap, r_b.ap,
                                                 ALU.mult),
                reads=[r_b], rw=[r_rstd])
    plan.op("dve", lambda h: h.tensor_tensor(r_b.ap, r_mean.ap, r_rstd.ap,
                                             ALU.mult),
            reads=[r_mean, r_rstd], writes=[r_b])
    plan.op("dve", lambda h: h.tensor_scalar(r_b.ap, r_b.ap, -1.0, None,
                                             ALU.mult), rw=[r_b])
    if g_ap is None:
        plan.op("pe", lambda h: h.matmul(lns.p_ar.ap, P["onesc"].ap, r_rstd.ap,
                                         start=True, stop=True),
                reads=[r_rstd, P["onesc"]], writes=[lns.p_ar])
        plan.op("pe", lambda h: h.matmul(lns.p_br.ap, P["onesc"].ap, r_b.ap,
                                         start=True, stop=True),
                reads=[r_b, P["onesc"]], writes=[lns.p_br])
    for dt in range(DT):
        tl = tiles[dt]
        if g_ap is not None:
            plan.op("pe", lambda h, dt=dt: h.matmul(
                lns.p_ar.ap, g_ap[:, dt * 128:(dt + 1) * 128], r_rstd.ap,
                start=True, stop=True),
                deps=list(gdeps), reads=[r_rstd], writes=[lns.p_ar])
            plan.op("pe", lambda h, dt=dt: h.matmul(
                lns.p_br.ap, g_ap[:, dt * 128:(dt + 1) * 128], r_b.ap,
                start=True, stop=False),
                deps=list(gdeps), reads=[r_b], writes=[lns.p_br])
            o = plan.op("pe", lambda h, dt=dt: h.matmul(
                lns.p_br.ap, b_ap[:, dt * 128:(dt + 1) * 128], P["onesr"].ap,
                start=False, stop=True),
                deps=list(gdeps), reads=[P["onesr"]])
            lns.p_br.writers = [o]
            lns.p_br.readers = []
        plan.op("dve", lambda h, tl=tl: h.tensor_tensor(
            tl.ap, tl.ap, lns.p_ar.ap, ALU.mult), reads=[lns.p_ar], rw=[tl])
        plan.op("dve", lambda h, tl=tl: h.tensor_tensor(
            tl.ap, tl.ap, lns.p_br.ap, ALU.add), reads=[lns.p_br], rw=[tl])
        if maxacc is not None:
            plan.op("dve", lambda h, tl=tl: h.reduce_max(lns.mxr.ap, tl.ap,
                                                         axis=AXX),
                    reads=[tl], writes=[lns.mxr])
            plan.op("dve", lambda h: h.tensor_max(maxacc.ap, maxacc.ap,
                                                  lns.mxr.ap),
                    reads=[lns.mxr], rw=[maxacc])
        plan.dma("pool", dst_fn(dt), tl, reads=[tl], writes=[dstdb])


def emit_preln(plan, nc, io, DB, P, seq):
    with contextlib.ExitStack() as st:
        tiles = [sb(nc, st, f"pl_t{j}", [128, TC]) for j in range(DT)]
        gsb = sb(nc, st, "pl_g", [1, D])
        bsb = sb(nc, st, "pl_b", [1, D])
        lns = LNScratch(plan, nc, st, "pl")
        g1 = plan.dma("sp", gsb, io["ln_g"], reads=[DB["ln_g"]])
        g2 = plan.dma("sp", bsb, io["ln_b"], reads=[DB["ln_b"]])
        for c in range(NCH):
            col = seq * L + c * TC
            for dt in range(DT):
                plan.dma("sp", tiles[dt],
                         io["x0"][dt * 128:(dt + 1) * 128, col:col + TC],
                         reads=[DB["x0"]])
            emit_postln(plan, P, lns, tiles,
                        lambda dt, col=col: io["xn_s"][dt * 128:(dt + 1) * 128,
                                                       col:col + TC],
                        DB["xn_s"], g_ap=gsb.ap, b_ap=bsb.ap,
                        gdeps=[g1, g2, gsb.writers[0] if gsb.writers else None])



def mm_group(plan, psum, pairs, mw=None):
    n = len(pairs)
    first_deps_bufs = [psum]
    for k, (wb, wa, mb, ma) in enumerate(pairs):
        start = (k == 0)
        stop = (k == n - 1)
        if start:
            o = plan.op("pe", lambda h, wa=wa, ma=ma, start=start, stop=stop:
                        h.matmul(psum.ap if mw is None else psum.ap[0:mw, :],
                                 wa, ma, start=start, stop=stop),
                        reads=[wb, mb], writes=[psum])
        else:
            o = plan.op("pe", lambda h, wa=wa, ma=ma, start=start, stop=stop:
                        h.matmul(psum.ap if mw is None else psum.ap[0:mw, :],
                                 wa, ma, start=start, stop=stop),
                        reads=[wb, mb])
    psum.writers = [o]
    psum.readers = []
    return o


def emit_proj_simple(plan, nc, io, DB, P, seq, wT_ap, wT_db, mov_db, KT, MT,
                     dst_db, name=""):
    with contextlib.ExitStack() as st:
        wb = [sb(nc, st, f"pj_w{k}{name}", [128, MT * 128]) for k in range(KT)]
        mb = [[sb(nc, st, f"pj_m{s}_{k}{name}", [128, TC]) for k in range(KT)]
              for s in range(2)]
        pb = [ps(nc, st, f"pj_p{j}{name}", [128, TC]) for j in range(2)]
        eb = [sb(nc, st, f"pj_e{j}{name}", [128, TC]) for j in range(2)]
        for k in range(KT):
            plan.dma("sp", wb[k], wT_ap[k * 128:(k + 1) * 128, 0:MT * 128],
                     reads=[wT_db])
        for c in range(NCH):
            col = seq * L + c * TC
            ms = c % 2
            for k in range(KT):
                plan.dma("sp", mb[ms][k],
                         mov_db.ap[k * 128:(k + 1) * 128, col:col + TC],
                         reads=[mov_db])
            for m in range(MT):
                gi = c * MT + m
                bank = gi % 2
                mm_group(plan, pb[bank],
                         [(wb[k], wb[k].ap[:, m * 128:(m + 1) * 128],
                           mb[ms][k], mb[ms][k].ap) for k in range(KT)])
                e = eb[gi % 2]
                plan.op("act", lambda h, e=e, bank=bank: h.activation(
                    e.ap, pb[bank].ap, AF.Copy), reads=[pb[bank]], writes=[e])
                plan.dma("pool", dst_db.ap[m * 128:(m + 1) * 128, col:col + TC],
                         e, reads=[e], writes=[dst_db])


def emit_inproj0(plan, nc, io, DB, P, seq, d):
    with contextlib.ExitStack() as st:
        xn = [sb(nc, st, f"ip_xn{k}", [128, L]) for k in range(DT)]
        wb = [[sb(nc, st, f"ip_w{s}_{k}", [128, 128]) for k in range(DT)]
              for s in range(2)]
        pb = [ps(nc, st, f"ip_p{j}", [128, TC]) for j in range(2)]
        eb = [sb(nc, st, f"ip_e{j}", [128, TC]) for j in range(3)]
        for k in range(DT):
            plan.dma("sp", xn[k], io["xn_s"][k * 128:(k + 1) * 128,
                                             seq * L:(seq + 1) * L],
                     reads=[DB["xn_s"]])
        gi = 0
        for m in range(2 * DIT):
            ws = m % 2
            for k in range(DT):
                plan.dma("sp", wb[ws][k],
                         io[f"{d}_inT"][k * 128:(k + 1) * 128,
                                        m * 128:(m + 1) * 128],
                         reads=[DB[f"{d}_inT"]])
            for c in range(NCH):
                col = seq * L + c * TC
                bank = gi % 2
                mm_group(plan, pb[bank],
                         [(wb[ws][k], wb[ws][k].ap, xn[k],
                           xn[k].ap[:, c * TC:(c + 1) * TC]) for k in range(DT)])
                e = eb[gi % 3]
                plan.op("act", lambda h, e=e, bank=bank: h.activation(
                    e.ap, pb[bank].ap, AF.Copy), reads=[pb[bank]], writes=[e])
                dst = DB[f"xi_{d}_s"] if m < DIT else DB[f"z_{d}_s"]
                row = (m if m < DIT else m - DIT) * 128
                plan.dma("pool", dst.ap[row:row + 128, col:col + TC], e,
                         reads=[e], writes=[dst])
                gi += 1


def emit_conv(plan, nc, io, DB, seq, src_db, dst_db, convw_ap, convb_ap,
              convw_db, convb_db, fwd=True):
    with contextlib.ExitStack() as st:
        xb = [sb(nc, st, f"cv_x{j}", [128, L + 3]) for j in range(2)]
        ac = [sb(nc, st, f"cv_a{j}", [128, L]) for j in range(2)]
        wc = sb(nc, st, "cv_w", [128, DIT * 4])
        bc = sb(nc, st, "cv_b", [128, DIT])
        plan.dma("sp", wc, convw_ap, reads=[convw_db])
        plan.dma("sp", bc, convb_ap, reads=[convb_db])
        for kt in range(DIT):
            b = xb[kt % 2]
            a = ac[kt % 2]
            if fwd:
                plan.dma("sp", b.ap[:, 3:3 + L],
                         src_db.ap[kt * 128:(kt + 1) * 128,
                                   seq * L:(seq + 1) * L],
                         reads=[src_db], writes=[b])
                plan.op("dve", lambda h, b=b: h.memset(b.ap[:, 0:3], 0.0),
                        rw=[b])
            else:
                plan.dma("sp", b.ap[:, 0:L],
                         src_db.ap[kt * 128:(kt + 1) * 128,
                                   seq * L:(seq + 1) * L],
                         reads=[src_db], writes=[b])
                plan.op("dve", lambda h, b=b: h.memset(b.ap[:, L:L + 3], 0.0),
                        rw=[b])
            off0 = 0 if fwd else 3
            plan.op("dve", lambda h, b=b, a=a, kt=kt, off0=off0: h.tensor_scalar(
                a.ap, b.ap[:, off0:off0 + L], wc.ap[:, kt * 4:kt * 4 + 1],
                None, ALU.mult), reads=[b, wc], writes=[a])
            for j in range(1, 4):
                off = j if fwd else 3 - j
                plan.op("dve", lambda h, b=b, a=a, kt=kt, j=j, off=off:
                        h.scalar_tensor_tensor(
                            a.ap, b.ap[:, off:off + L],
                            wc.ap[:, kt * 4 + j:kt * 4 + j + 1], a.ap,
                            ALU.mult, ALU.add), reads=[b, wc], rw=[a])
            plan.op("act", lambda h, a=a, kt=kt: h.activation(
                a.ap, a.ap, AF.Silu, bias=bc.ap[:, kt:kt + 1]),
                reads=[bc], rw=[a])
            plan.dma("pool", dst_db.ap[kt * 128:(kt + 1) * 128,
                                       seq * L:(seq + 1) * L], a,
                     reads=[a], writes=[dst_db])


def emit_xw0(plan, nc, io, DB, P, seq, d):
    with contextlib.ExitStack() as st:
        wb = [sb(nc, st, f"xw_w{k}", [128, 96]) for k in range(DIT)]
        mb = [[sb(nc, st, f"xw_m{s}_{k}", [128, TC]) for k in range(DIT)]
              for s in range(2)]
        pb = [ps(nc, st, f"xw_p{j}", [96, TC]) for j in range(2)]
        eb = [sb(nc, st, f"xw_e{j}", [96, TC]) for j in range(2)]
        for k in range(DIT):
            plan.dma("sp", wb[k], io[f"{d}_xwT"][k * 128:(k + 1) * 128, :],
                     reads=[DB[f"{d}_xwT"]])
        for c in range(NCH):
            col = seq * L + c * TC
            s2 = c % 2
            for k in range(DIT):
                plan.dma("sp", mb[s2][k],
                         DB[f"xc_{d}_s"].ap[k * 128:(k + 1) * 128, col:col + TC],
                         reads=[DB[f"xc_{d}_s"]])
            mm_group(plan, pb[s2],
                     [(wb[k], wb[k].ap, mb[s2][k], mb[s2][k].ap)
                      for k in range(DIT)])
            e = eb[s2]
            plan.op("act", lambda h, e=e, s2=s2: h.activation(
                e.ap, pb[s2].ap, AF.Copy), reads=[pb[s2]], writes=[e])
            plan.dma("pool", DB[f"dtr_{d}_s"].ap[:, col:col + TC],
                     e.ap[0:64, :], reads=[e], writes=[DB[f"dtr_{d}_s"]])
            plan.dma("pool", DB[f"bc_{d}_s"].ap[:, col:col + TC],
                     e.ap[64:96, :], reads=[e], writes=[DB[f"bc_{d}_s"]])


def emit_dtproj(plan, nc, io, DB, P, seq, dtwT_ap, dtwT_db, dtb_ap, dtb_db,
                dtr_db, dl_db):
    with contextlib.ExitStack() as st:
        wb = [sb(nc, st, f"dp_w{j}", [64, 128]) for j in range(2)]
        mbb = sb(nc, st, "dp_m", [64, L])
        pb = [ps(nc, st, f"dp_p{j}", [128, TC]) for j in range(2)]
        sg = [sb(nc, st, f"dp_s{j}", [128, TC]) for j in range(2)]
        nb = sb(nc, st, "dp_nb", [128, DIT])
        bt = sb(nc, st, "dp_bt", [128, DIT])
        plan.dma("sp", bt, dtb_ap, reads=[dtb_db])
        plan.op("dve", lambda h: h.tensor_scalar(nb.ap, bt.ap, -1.0, None,
                                                 ALU.mult),
                reads=[bt], writes=[nb])
        plan.dma("sp", mbb, dtr_db.ap[:, seq * L:(seq + 1) * L],
                 reads=[dtr_db])
        gi = 0
        for m in range(DIT):
            ws = m % 2
            plan.dma("sp", wb[ws], dtwT_ap[:, m * 128:(m + 1) * 128],
                     reads=[dtwT_db])
            for c in range(NCH):
                col = seq * L + c * TC
                bank = gi % 2
                mm_group(plan, pb[bank],
                         [(wb[ws], wb[ws].ap, mbb,
                           mbb.ap[:, c * TC:(c + 1) * TC])])
                s = sg[gi % 2]
                plan.op("act", lambda h, s=s, bank=bank, m=m: h.activation(
                    s.ap, pb[bank].ap, AF.Sigmoid, scale=-1.0,
                    bias=nb.ap[:, m:m + 1]),
                    reads=[pb[bank], nb], writes=[s])
                plan.op("act", lambda h, s=s: h.activation(s.ap, s.ap, AF.Ln),
                        rw=[s])
                plan.dma("pool", dl_db.ap[m * 128:(m + 1) * 128, col:col + TC],
                         s, reads=[s], writes=[dl_db])
                gi += 1



def emit_scan(plan, nc, io, DB, P, seq, *, N, NG, TCS, dl_db, xc_db, bc_db,
              y_db, dp_ap, dp_db, a_col_fn, reverse, name):
    nh = L // TCS
    with contextlib.ExitStack() as st:
        brep = [sb(nc, st, f"sc_B{j}{name}", [128, TCS]) for j in range(NG)]
        crep = [sb(nc, st, f"sc_C{j}{name}", [128, TCS]) for j in range(NG)]
        dl = [sb(nc, st, f"sc_dl{j}{name}", [128, TCS]) for j in range(2)]
        xc = [sb(nc, st, f"sc_xc{j}{name}", [128, TCS]) for j in range(2)]
        du = sb(nc, st, f"sc_du{name}", [128, TCS])
        ya = sb(nc, st, f"sc_ya{name}", [128, TCS])
        da = [sb(nc, st, f"sc_da{j}{name}", [128, TCS]) for j in range(2)]
        bb = [sb(nc, st, f"sc_b{j}{name}", [128, TCS]) for j in range(2)]
        hh = [sb(nc, st, f"sc_h{j}{name}", [128, TCS]) for j in range(2)]
        hc = [sb(nc, st, f"sc_hc{j}{name}", [128, TCS]) for j in range(4)]
        dpb = sb(nc, st, f"sc_dp{name}", [128, DIT])
        assert nh == 1, "chunked scans unused now"
        plan.dma("sp", dpb, dp_ap, reads=[dp_db])
        chunks = list(range(nh)) if not reverse else list(range(nh - 1, -1, -1))
        for ci, ch in enumerate(chunks):
            col = seq * L + ch * TCS
            for g in range(N // NG):
                ns = [g * NG + j for j in range(NG)]
                for j, n in enumerate(ns):
                    plan.dma("sp", brep[j],
                             bc_db.ap[n:n + 1, col:col + TCS]
                             .broadcast_to([128, TCS]),
                             reads=[bc_db], writes=[brep[j]])
                    plan.dma("sp", crep[j],
                             bc_db.ap[N + n:N + n + 1, col:col + TCS]
                             .broadcast_to([128, TCS]),
                             reads=[bc_db], writes=[crep[j]])
                for dt in range(DIT):
                    dlb, xcb = dl[dt % 2], xc[dt % 2]
                    plan.dma("sp", dlb, dl_db.ap[dt * 128:(dt + 1) * 128,
                                                 col:col + TCS], reads=[dl_db])
                    plan.dma("sp", xcb, xc_db.ap[dt * 128:(dt + 1) * 128,
                                                 col:col + TCS], reads=[xc_db])
                    plan.op("dve", lambda h, dlb=dlb, xcb=xcb:
                            h.scalar_tensor_tensor(du.ap, xcb.ap, -1.0, dlb.ap,
                                                   ALU.mult, ALU.mult),
                            reads=[dlb, xcb], writes=[du])
                    if g == 0:
                        plan.op("dve", lambda h, xcb=xcb, dt=dt: h.tensor_scalar(
                            ya.ap, xcb.ap, dpb.ap[:, dt:dt + 1], None, ALU.mult),
                            reads=[xcb, dpb], writes=[ya])
                    def emit_pre(j):
                        aap = a_col_fn(dt, ns[j])
                        p = j % 2
                        dap, bbp = da[p], bb[p]
                        plan.op("act", lambda h, dlb=dlb, aap=aap, dap=dap:
                                h.activation(dap.ap, dlb.ap, AF.Exp, scale=aap),
                                reads=[dlb], writes=[dap])
                        plan.op("dve", lambda h, j=j, bbp=bbp: h.tensor_tensor(
                            bbp.ap, du.ap, brep[j].ap, ALU.mult),
                            reads=[du, brep[j]], writes=[bbp])

                    if g == 0:
                        plan.dma("pool", y_db.ap[dt * 128:(dt + 1) * 128,
                                                 col:col + TCS], ya,
                                 reads=[ya], rw=[y_db])
                    emit_pre(0)
                    for j, n in enumerate(ns):
                        if j + 1 < len(ns):
                            emit_pre(j + 1)
                        p = j % 2
                        dap, bbp, hhp, hcp = da[p], bb[p], hh[p], hc[(g * NG + j) % 4]
                        if not reverse:
                            plan.op("dve", lambda h, dap=dap, bbp=bbp, hhp=hhp:
                                    h.tensor_tensor_scan(hhp.ap, dap.ap,
                                                         bbp.ap, 0.0,
                                                         ALU.mult, ALU.add),
                                    reads=[dap, bbp], writes=[hhp])
                        else:
                            plan.op("dve", lambda h, dap=dap, bbp=bbp, hhp=hhp:
                                    h.tensor_tensor_scan(hhp.ap[:, ::-1],
                                                         dap.ap[:, ::-1],
                                                         bbp.ap[:, ::-1],
                                                         0.0, ALU.mult,
                                                         ALU.add),
                                    reads=[dap, bbp], writes=[hhp])
                        plan.op("dve", lambda h, j=j, hhp=hhp, hcp=hcp:
                                h.tensor_tensor(hcp.ap, hhp.ap, crep[j].ap,
                                                ALU.mult),
                                reads=[hhp, crep[j]], writes=[hcp])
                        plan.dma("pool", y_db.ap[dt * 128:(dt + 1) * 128,
                                                 col:col + TCS], hcp,
                                 reads=[hcp], rw=[y_db], accum_op=ALU.add)


def emit_scan0(plan, nc, io, DB, P, seq, d):
    A = P["A0f"] if d == "f" else P["A0b"]

    def acol(dt, n):
        return A.ap[:, dt * N0 + n:dt * N0 + n + 1]

    emit_scan(plan, nc, io, DB, P, seq, N=N0, NG=4, TCS=L,
              dl_db=DB[f"dl_{d}_s"], xc_db=DB[f"xc_{d}_s"],
              bc_db=DB[f"bc_{d}_s"], y_db=DB[f"y_{d}_s"],
              dp_ap=io[f"{d}_dp"], dp_db=DB[f"{d}_dp"], a_col_fn=acol,
              reverse=(d == "b"), name=f"s0{d}")


def emit_scan1(plan, nc, io, DB, P, seq):
    def acol(dt, n):
        return P["A1"].ap[:, n:n + 1]

    emit_scan(plan, nc, io, DB, P, seq, N=N1, NG=4, TCS=L,
              dl_db=DB["dl1_s"], xc_db=DB["xc1_s"], bc_db=DB["bc1_s"],
              y_db=DB["y1_s"], dp_ap=io["dp1"], dp_db=DB["dp1"],
              a_col_fn=acol, reverse=False, name="s1")


def emit_gating(plan, nc, io, DB, seq, y_db, z_db):
    with contextlib.ExitStack() as st:
        yb = [sb(nc, st, f"gt_y{j}", [128, L]) for j in range(2)]
        zb = [sb(nc, st, f"gt_z{j}", [128, L]) for j in range(2)]
        for kt in range(DIT):
            y, z = yb[kt % 2], zb[kt % 2]
            plan.dma("sp", y, y_db.ap[kt * 128:(kt + 1) * 128,
                                      seq * L:(seq + 1) * L], reads=[y_db])
            plan.dma("sp", z, z_db.ap[kt * 128:(kt + 1) * 128,
                                      seq * L:(seq + 1) * L], reads=[z_db])
            plan.op("act", lambda h, z=z: h.activation(z.ap, z.ap, AF.Silu),
                    rw=[z])
            plan.op("dve", lambda h, y=y, z=z: h.tensor_tensor(
                y.ap, y.ap, z.ap, ALU.mult), reads=[z], rw=[y])
            plan.dma("pool", y_db.ap[kt * 128:(kt + 1) * 128,
                                     seq * L:(seq + 1) * L], y,
                     reads=[y], rw=[y_db])



def emit_combine(plan, nc, io, DB, P, seq):
    with contextlib.ExitStack() as st:
        wb = [sb(nc, st, f"cb_w{k}", [128, D]) for k in range(2 * DT)]
        mb = [[sb(nc, st, f"cb_m{s}_{k}", [128, TC]) for k in range(2 * DT)]
              for s in range(2)]
        pb = [ps(nc, st, f"cb_p{j}", [128, TC]) for j in range(2)]
        res = [sb(nc, st, f"cb_r{j}", [128, TC]) for j in range(DT)]
        out = [sb(nc, st, f"cb_o{j}", [128, TC]) for j in range(DT)]
        cbb = sb(nc, st, "cb_bias", [128, DT])
        lns = LNScratch(plan, nc, st, "cb")
        plan.dma("sp", cbb, io["combb"], reads=[DB["combb"]])
        for k in range(2 * DT):
            plan.dma("sp", wb[k], io["combT"][k * 128:(k + 1) * 128, 0:D],
                     reads=[DB["combT"]])
        for c in range(NCH):
            col = seq * L + c * TC
            ms = c % 2
            for k in range(2 * DT):
                src = DB["o_f_s"] if k < DT else DB["o_b_s"]
                row = (k if k < DT else k - DT) * 128
                plan.dma("sp", mb[ms][k], src.ap[row:row + 128, col:col + TC],
                         reads=[src])
            for dt in range(DT):
                plan.dma("sp", res[dt],
                         io["x0"][dt * 128:(dt + 1) * 128, col:col + TC],
                         reads=[DB["x0"]])
            for m in range(DT):
                gi = c * DT + m
                bank = gi % 2
                mm_group(plan, pb[bank],
                         [(wb[k], wb[k].ap[:, m * 128:(m + 1) * 128],
                           mb[ms][k], mb[ms][k].ap) for k in range(2 * DT)])
                plan.op("dve", lambda h, m=m, bank=bank: h.scalar_tensor_tensor(
                    out[m].ap, res[m].ap, 2.0, pb[bank].ap, ALU.mult, ALU.add),
                    reads=[pb[bank], res[m]], writes=[out[m]])
                plan.op("dve", lambda h, m=m: h.tensor_scalar(
                    out[m].ap, out[m].ap, cbb.ap[:, m:m + 1], None, ALU.add),
                    reads=[cbb], rw=[out[m]])
            emit_postln(plan, P, lns, out,
                        lambda dt, col=col: io["x1_s"][dt * 128:(dt + 1) * 128,
                                                       col:col + TC],
                        DB["x1_s"])


def emit_ffn0(plan, nc, io, DB, P, seq):
    with contextlib.ExitStack() as st:
        wq = [sb(nc, st, f"f0_wq{k}", [128, D]) for k in range(DT)]
        ab = [sb(nc, st, f"f0_a{k}", [128, 16]) for k in range(DT)]
        bb = sb(nc, st, "f0_b", [16, D])
        bias = sb(nc, st, "f0_bias", [128, DT])
        x1 = [sb(nc, st, f"f0_x{k}", [128, TC]) for k in range(DT)]
        pu = ps(nc, st, "f0_pu", [16, TC])
        ub = sb(nc, st, "f0_u", [16, TC])
        pb = [ps(nc, st, f"f0_p{j}", [128, TC]) for j in range(2)]
        out = [sb(nc, st, f"f0_o{j}", [128, TC]) for j in range(DT)]
        lns = LNScratch(plan, nc, st, "f0")
        for k in range(DT):
            plan.dma("sp", wq[k], DB["wq0_s"].ap[k * 128:(k + 1) * 128, :],
                     reads=[DB["wq0_s"]])
            plan.dma("sp", ab[k], io["ffn_a"][k * 128:(k + 1) * 128, :],
                     reads=[DB["ffn_a"]])
        plan.dma("sp", bb, io["ffn_b"], reads=[DB["ffn_b"]])
        plan.dma("sp", bias, io["ffn_bias"], reads=[DB["ffn_bias"]])
        for c in range(NCH):
            col = seq * L + c * TC
            for k in range(DT):
                plan.dma("sp", x1[k],
                         DB["x1_s"].ap[k * 128:(k + 1) * 128, col:col + TC],
                         reads=[DB["x1_s"]])
            mm_group(plan, pu, [(ab[k], ab[k].ap, x1[k], x1[k].ap)
                                for k in range(DT)])
            plan.op("act", lambda h: h.activation(ub.ap, pu.ap, AF.Copy),
                    reads=[pu], writes=[ub])
            for m in range(DT):
                bank = m % 2
                pairs = [(wq[k], wq[k].ap[:, m * 128:(m + 1) * 128], x1[k],
                          x1[k].ap) for k in range(DT)]
                pairs.append((bb, bb.ap[:, m * 128:(m + 1) * 128], ub, ub.ap))
                mm_group(plan, pb[bank], pairs)
                plan.op("dve", lambda h, m=m, bank=bank: h.scalar_tensor_tensor(
                    out[m].ap, pb[bank].ap, bias.ap[:, m:m + 1], x1[m].ap,
                    ALU.add, ALU.add),
                    reads=[pb[bank], bias, x1[m]], writes=[out[m]])
            emit_postln(plan, P, lns, out,
                        lambda dt, col=col: io["x2_s"][dt * 128:(dt + 1) * 128,
                                                       col:col + TC],
                        DB["x2_s"])


def emit_inproj1(plan, nc, io, DB, P, seq):
    with contextlib.ExitStack() as st:
        x2 = [sb(nc, st, f"i1_x{k}", [128, L]) for k in range(DT)]
        wb = [[sb(nc, st, f"i1_w{s}_{k}", [128, 128]) for k in range(DT)]
              for s in range(2)]
        pb = [ps(nc, st, f"i1_p{j}", [128, TC]) for j in range(2)]
        eb = [sb(nc, st, f"i1_e{j}", [128, TC]) for j in range(3)]
        for k in range(DT):
            plan.dma("sp", x2[k], DB["x2_s"].ap[k * 128:(k + 1) * 128,
                                                seq * L:(seq + 1) * L],
                     reads=[DB["x2_s"]])
        mdefs = [(m * 128, 128) for m in range(33)] + [(4224, 64)]
        gi = 0
        for mi, (moff, mw) in enumerate(mdefs):
            ws = mi % 2
            for k in range(DT):
                plan.dma("sp", wb[ws][k].ap[:, 0:mw],
                         io["in1T"][k * 128:(k + 1) * 128, moff:moff + mw],
                         reads=[DB["in1T"]], writes=[wb[ws][k]])
            for c in range(NCH):
                col = seq * L + c * TC
                bank = gi % 2
                mm_group(plan, pb[bank],
                         [(wb[ws][k], wb[ws][k].ap[:, 0:mw], x2[k],
                           x2[k].ap[:, c * TC:(c + 1) * TC])
                          for k in range(DT)], mw=mw)
                e = eb[gi % 3]
                plan.op("act", lambda h, e=e, bank=bank, mw=mw: h.activation(
                    e.ap[0:mw, :], pb[bank].ap[0:mw, :], AF.Copy),
                    reads=[pb[bank]], writes=[e])
                if moff < DI:
                    dst, row = DB["xi1_s"], moff
                elif moff < 2 * DI:
                    dst, row = DB["z1_s"], moff - DI
                elif moff == 4096:
                    dst, row = DB["bc1_s"], 0
                else:
                    dst, row = DB["dtr1_s"], 0
                plan.dma("pool", dst.ap[row:row + mw, col:col + TC],
                         e.ap[0:mw, :], reads=[e], writes=[dst])
                gi += 1


def emit_outproj1_resln(plan, nc, io, DB, P, seq):
    with contextlib.ExitStack() as st:
        wb = [sb(nc, st, f"o1_w{k}", [128, D]) for k in range(DIT)]
        mb = [[sb(nc, st, f"o1_m{s}_{k}", [128, TC]) for k in range(DIT)]
              for s in range(2)]
        pb = [ps(nc, st, f"o1_p{j}", [128, TC]) for j in range(2)]
        out = [sb(nc, st, f"o1_o{j}", [128, TC]) for j in range(DT)]
        res = [sb(nc, st, f"o1_r{j}", [128, TC]) for j in range(DT)]
        lns = LNScratch(plan, nc, st, "o1")
        for k in range(DIT):
            plan.dma("sp", wb[k], io["out1T"][k * 128:(k + 1) * 128, 0:D],
                     reads=[DB["out1T"]])
        for c in range(NCH):
            col = seq * L + c * TC
            ms = c % 2
            for k in range(DIT):
                plan.dma("sp", mb[ms][k],
                         DB["y1_s"].ap[k * 128:(k + 1) * 128, col:col + TC],
                         reads=[DB["y1_s"]])
            for m in range(DT):
                gi = c * DT + m
                bank = gi % 2
                mm_group(plan, pb[bank],
                         [(wb[k], wb[k].ap[:, m * 128:(m + 1) * 128],
                           mb[ms][k], mb[ms][k].ap) for k in range(DIT)])
                plan.op("act", lambda h, m=m, bank=bank: h.activation(
                    out[m].ap, pb[bank].ap, AF.Copy),
                    reads=[pb[bank]], writes=[out[m]])
            for dt in range(DT):
                plan.dma("sp", res[dt],
                         DB["x2_s"].ap[dt * 128:(dt + 1) * 128, col:col + TC],
                         reads=[DB["x2_s"]])
                plan.op("dve", lambda h, dt=dt: h.tensor_tensor(
                    out[dt].ap, out[dt].ap, res[dt].ap, ALU.add),
                    reads=[res[dt]], rw=[out[dt]])
            emit_postln(plan, P, lns, out,
                        lambda dt, col=col: io["x3_s"][dt * 128:(dt + 1) * 128,
                                                       col:col + TC],
                        DB["x3_s"], maxacc=P["gmx"])


def emit_quantprep(plan, nc, io, DB, P, macc, rcp, sw, bvec, seff, beff, NFT,
                   dbgdb=None):
    with contextlib.ExitStack() as st:
        pt = ps(nc, st, "qp_pt", [1, 128])
        pbb = ps(nc, st, "qp_pb", [128, 1])
        srow = sb(nc, st, "qp_srow", [1, 1])
        sbc = sb(nc, st, "qp_sbc", [128, 1])
        sgn = sb(nc, st, "qp_sg", [128, NFT])
        tii = sb(nc, st, "qp_ti", [128, NFT], I32)
        plan.op("pe", lambda h: h.transpose(pt.ap, macc.ap, P["ident"].ap),
                reads=[macc, P["ident"]], writes=[pt])
        plan.op("dve", lambda h: h.reduce_max(srow.ap, pt.ap, axis=AXX),
                reads=[pt], writes=[srow])
        plan.dma("pool", io["cc_in"], srow, reads=[srow], writes=[DB["cc_in"]])
        plan.op("pool", lambda h: h.collective_compute(
            "AllReduce", ALU.max,
            replica_groups=[[0, 1, 2, 3, 4, 5, 6, 7]],
            ins=[io["cc_in"]], outs=[io["cc_out"]]),
            reads=[DB["cc_in"]], writes=[DB["cc_out"]])
        plan.dma("pool", srow, io["cc_out"], reads=[DB["cc_out"]],
                 writes=[srow])
        plan.op("dve", lambda h: h.tensor_scalar(srow.ap, srow.ap, 1e-8,
                                                 1.0 / 255.0, ALU.max,
                                                 ALU.mult), rw=[srow])
        plan.op("pe", lambda h: h.matmul(pbb.ap, P["onesc"].ap, srow.ap,
                                         start=True, stop=True),
                reads=[srow, P["onesc"]], writes=[pbb])
        plan.op("dve", lambda h: h.tensor_copy(sbc.ap, pbb.ap),
                reads=[pbb], writes=[sbc])
        plan.op("dve", lambda h: h.reciprocal(rcp.ap, sbc.ap),
                reads=[sbc], writes=[rcp])
        plan.op("dve", lambda h: h.tensor_scalar(seff.ap, sw.ap, sbc.ap[:, 0:1],
                                                 None, ALU.mult),
                reads=[sw, sbc], writes=[seff])
        rcq = sb(nc, st, "qp_rcq", [128, NFT])
        plan.op("dve", lambda h: h.reciprocal(rcq.ap, seff.ap),
                reads=[seff], writes=[rcq])
        plan.op("dve", lambda h: h.tensor_tensor(beff.ap, bvec.ap, rcq.ap,
                                                 ALU.mult),
                reads=[bvec, rcq], writes=[beff])
        sgn2 = sb(nc, st, "qp_sg2", [128, NFT])
        round_nearest(plan, beff, tii, sgn, sgn2)
        plan.op("dve", lambda h: h.tensor_tensor(beff.ap, beff.ap, seff.ap,
                                                 ALU.mult),
                reads=[seff], rw=[beff])
        if dbgdb is not None:
            dsin, dmax, dse, dbe = dbgdb
            plan.dma("pool", dsin.ap, sbc, reads=[sbc], writes=[dsin])
            plan.dma("pool", dmax.ap, macc, reads=[macc], writes=[dmax])
            plan.dma("pool", dse.ap, seff, reads=[seff], writes=[dse])
            plan.dma("pool", dbe.ap, beff, reads=[beff], writes=[dbe])


def quantize_tiles(plan, src_bufs, dst_bufs, rcp, tmp, tii, t1, t2):
    for sbuf, dbuf in zip(src_bufs, dst_bufs):
        plan.op("dve", lambda h, sbuf=sbuf: h.tensor_scalar(
            tmp.ap, sbuf.ap, rcp.ap[:, 0:1], None, ALU.mult),
            reads=[sbuf, rcp], writes=[tmp])
        plan.op("dve", lambda h: h.tensor_scalar(tmp.ap, tmp.ap, 0.0, 255.0,
                                                 ALU.max, ALU.min), rw=[tmp])
        round_nearest(plan, tmp, tii, t1, t2)
        plan.op("dve", lambda h, dbuf=dbuf: h.tensor_copy(dbuf.ap, tmp.ap),
                reads=[tmp], writes=[dbuf])


def emit_ffn1a(plan, nc, io, DB, P, seq):
    with contextlib.ExitStack() as st:
        wq = [sb(nc, st, f"fa_w{k}", [128, F1], BF16) for k in range(DT)]
        x3 = [sb(nc, st, f"fa_x{k}", [128, TC]) for k in range(DT)]
        xq = [sb(nc, st, f"fa_q{k}", [128, TC], BF16) for k in range(DT)]
        tmp = sb(nc, st, "fa_tmp", [128, TC])
        tq1 = sb(nc, st, "fa_tq1", [128, TC])
        tq2 = sb(nc, st, "fa_tq2", [128, TC])
        tii = sb(nc, st, "fa_ti", [128, TC], I32)
        pb = [ps(nc, st, f"fa_p{j}", [128, TC]) for j in range(2)]
        eb = [sb(nc, st, f"fa_e{j}", [128, TC]) for j in range(3)]
        mxr = sb(nc, st, "fa_mx", [128, 1])
        for k in range(DT):
            plan.dma("sp", wq[k], DB["wq1T_s"].ap[k * 128:(k + 1) * 128, :],
                     reads=[DB["wq1T_s"]])
        for c in range(NCH):
            col = seq * L + c * TC
            for k in range(DT):
                plan.dma("sp", x3[k],
                         DB["x3_s"].ap[k * 128:(k + 1) * 128, col:col + TC],
                         reads=[DB["x3_s"]])
            quantize_tiles(plan, x3, xq, P["rcp1"], tmp, tii, tq1, tq2)
            for m in range(F1T):
                bank = m % 2
                mm_group(plan, pb[bank],
                         [(wq[k], wq[k].ap[:, m * 128:(m + 1) * 128], xq[k],
                           xq[k].ap) for k in range(DT)])
                e = eb[m % 3]
                plan.op("act", lambda h, e=e, bank=bank, m=m: h.activation(
                    e.ap, pb[bank].ap, AF.Relu,
                    scale=P["seff1"].ap[:, m:m + 1],
                    bias=P["beff1"].ap[:, m:m + 1]),
                    reads=[pb[bank], P["seff1"], P["beff1"]], writes=[e])
                plan.op("dve", lambda h, e=e: h.reduce_max(mxr.ap, e.ap,
                                                           axis=AXX),
                        reads=[e], writes=[mxr])
                plan.op("dve", lambda h: h.tensor_max(P["ghx"].ap, P["ghx"].ap,
                                                      mxr.ap),
                        reads=[mxr], rw=[P["ghx"]])
                plan.dma("pool", DB["h_s"].ap[m * 128:(m + 1) * 128,
                                              col:col + TC], e,
                         reads=[e], writes=[DB["h_s"]])


def emit_ffn1b(plan, nc, io, DB, P, seq):
    with contextlib.ExitStack() as st:
        wq = [sb(nc, st, f"fb_w{k}", [128, D], BF16) for k in range(F1T)]
        hb = [sb(nc, st, f"fb_h{j}", [128, TC]) for j in range(2)]
        hq = [sb(nc, st, f"fb_q{k}", [128, TC], BF16) for k in range(F1T)]
        tmp = sb(nc, st, "fb_tmp", [128, TC])
        tq1 = sb(nc, st, "fb_tq1", [128, TC])
        tq2 = sb(nc, st, "fb_tq2", [128, TC])
        tii = sb(nc, st, "fb_ti", [128, TC], I32)
        pb = [ps(nc, st, f"fb_p{j}", [128, TC]) for j in range(2)]
        out = [sb(nc, st, f"fb_o{j}", [128, TC]) for j in range(DT)]
        res = [sb(nc, st, f"fb_r{j}", [128, TC]) for j in range(DT)]
        lns = LNScratch(plan, nc, st, "fb")
        for k in range(F1T):
            plan.dma("sp", wq[k], DB["wq2T_s"].ap[k * 128:(k + 1) * 128, :],
                     reads=[DB["wq2T_s"]])
        for c in range(NCH):
            col = seq * L + c * TC
            for k in range(F1T):
                b = hb[k % 2]
                plan.dma("sp", b,
                         DB["h_s"].ap[k * 128:(k + 1) * 128, col:col + TC],
                         reads=[DB["h_s"]])
                quantize_tiles(plan, [b], [hq[k]], P["rcp2"], tmp, tii, tq1, tq2)
            for m in range(DT):
                bank = m % 2
                mm_group(plan, pb[bank],
                         [(wq[k], wq[k].ap[:, m * 128:(m + 1) * 128], hq[k],
                           hq[k].ap) for k in range(F1T)])
                plan.op("dve", lambda h, m=m, bank=bank: h.tensor_scalar(
                    out[m].ap, pb[bank].ap, P["seff2"].ap[:, m:m + 1],
                    P["beff2"].ap[:, m:m + 1], ALU.mult, ALU.add),
                    reads=[pb[bank], P["seff2"], P["beff2"]], writes=[out[m]])
                plan.dma("sp", res[m],
                         DB["x3_s"].ap[m * 128:(m + 1) * 128, col:col + TC],
                         reads=[DB["x3_s"]])
                plan.op("dve", lambda h, m=m: h.tensor_tensor(
                    out[m].ap, out[m].ap, res[m].ap, ALU.add),
                    reads=[res[m]], rw=[out[m]])
            emit_postln(plan, P, lns, out,
                        lambda dt, col=col: io["xout"][dt * 128:(dt + 1) * 128,
                                                       col:col + TC],
                        DB["xout"])



def make_inputs(core, A):
    t = lambda x: np.ascontiguousarray(np.asarray(x, np.float32).T)
    col = lambda x: np.ascontiguousarray(np.asarray(x, np.float32).reshape(-1, 1))
    colk = lambda x, k: np.ascontiguousarray(
        np.asarray(x, np.float32).reshape(k, 128).T)
    m = {}
    x = np.asarray(A["L0_attn_x"][2 * core:2 * core + 2], np.float32)
    m["x0"] = np.ascontiguousarray(np.transpose(x, (2, 0, 1)).reshape(D, NT))
    m["ln_g"] = np.ascontiguousarray(np.asarray(A["L0_attn_ln_g"], np.float32).reshape(1, D))
    m["ln_b"] = np.ascontiguousarray(np.asarray(A["L0_attn_ln_b"], np.float32).reshape(1, D))
    for d in ("f", "b"):
        m[f"{d}_inT"] = t(A[f"L0_attn_{d}_in_w"])
        m[f"{d}_convw"] = np.ascontiguousarray(np.asarray(
            A[f"L0_attn_{d}_conv_w"], np.float32).reshape(DIT, 128, 4)
            .transpose(1, 0, 2).reshape(128, DIT * 4))
        m[f"{d}_convb"] = colk(A[f"L0_attn_{d}_conv_b"], DIT)
        m[f"{d}_xwT"] = t(A[f"L0_attn_{d}_x_w"])
        m[f"{d}_dtwT"] = t(A[f"L0_attn_{d}_dt_w"])
        m[f"{d}_dtb"] = colk(A[f"L0_attn_{d}_dt_b"], DIT)
        m[f"{d}_alog"] = np.ascontiguousarray(np.asarray(
            A[f"L0_attn_{d}_A_log"], np.float32))
        m[f"{d}_dp"] = colk(A[f"L0_attn_{d}_D"], DIT)
        m[f"{d}_outT"] = t(A[f"L0_attn_{d}_out_w"])
    m["combT"] = t(A["L0_attn_combine_w"])
    m["combb"] = colk(A["L0_attn_combine_b"], DT)
    m["ffn_k"] = np.ascontiguousarray(np.asarray(A["L0_ffn_kernel"], np.float32))
    m["ffn_bias"] = colk(A["L0_ffn_bias"], DT)
    m["ffn_a"] = np.ascontiguousarray(np.asarray(A["L0_ffn_a"], np.float32))
    m["ffn_b"] = np.ascontiguousarray(np.asarray(A["L0_ffn_b"], np.float32))
    m["in1T"] = t(A["L1_attn_in_proj_w"])
    m["conv1w"] = np.ascontiguousarray(np.asarray(
        A["L1_attn_conv_w"], np.float32).reshape(DIT, 128, 4)
        .transpose(1, 0, 2).reshape(128, DIT * 4))
    m["conv1b"] = colk(A["L1_attn_conv_b"], DIT)
    m["alog1b"] = np.ascontiguousarray(np.broadcast_to(
        np.asarray(A["L1_attn_A_log"], np.float32)[None, :], (128, N1)).copy())
    m["dp1"] = colk(A["L1_attn_D_param"], DIT)
    m["dt1T"] = t(A["L1_attn_dt_proj_w"])
    m["dt1b"] = colk(A["L1_attn_dt_proj_b"], DIT)
    m["out1T"] = t(A["L1_attn_out_proj_w"])
    m["w1f"] = np.ascontiguousarray(np.asarray(A["L1_ffn_w1"], np.float32))
    m["w1T"] = t(A["L1_ffn_w1"])
    m["b1"] = np.ascontiguousarray(
        np.asarray(A["L1_ffn_b1"], np.float32).reshape(F1T, 128).T)
    m["w2f"] = np.ascontiguousarray(np.asarray(A["L1_ffn_w2"], np.float32))
    m["w2T"] = t(A["L1_ffn_w2"])
    m["b2"] = np.ascontiguousarray(
        np.asarray(A["L1_ffn_b2"], np.float32).reshape(DT, 128).T)
    m["ident"] = np.eye(128, dtype=np.float32)
    return m


def kernel(**inputs):
    nc = build_program()
    in_maps = [make_inputs(c, inputs) for c in range(NCORE)]
    res = run_bass_kernel_spmd(nc, in_maps, core_ids=list(range(NCORE)))
    outs = []
    for c in range(NCORE):
        o = np.asarray(res.results[c]["xout"])
        outs.append(np.transpose(o.reshape(D, BSEQ, L), (1, 2, 0)))
    return np.concatenate(outs, axis=0).astype(np.float32)
